# version 03
# baseline (speedup 1.0000x reference)
import math
import os
from contextlib import ExitStack

import numpy as np
import ml_dtypes

import concourse.bass as bass
import concourse.mybir as mybir
from concourse.bass_utils import run_bass_kernel_spmd
from concourse.bass_core_rust import IndirectOffsetOnAxis

F32 = mybir.dt.float32
BF16 = mybir.dt.bfloat16
I32 = mybir.dt.int32
AF = mybir.ActivationFunctionType
ALU = mybir.AluOpType
AX = mybir.AxisListType
BFNP = ml_dtypes.bfloat16

T = 4096
D = 1024
TC = 512
NT = T // TC
NKD = D // 128
H = 16
HD = 64

MODE = "multi"

ENGS = ["sync", "tensor", "scalar", "vector", "gpsimd"]
_QUEUE_OF = {"sync": "qS", "gpsimd": "qG", "scalar": "qA"}

SCRATCH = {}


def _reg(name, shape, dtype):
    SCRATCH[name] = (tuple(shape), dtype)
    return name


_reg("x_fm", (D, T), np.float32)
_reg("xn0", (D, T), np.float32)
_reg("xn0b", (D, T), BFNP)
_reg("q0b", (D, T), BFNP)
_reg("k0b", (D, T), BFNP)
_reg("v0t", (T, D), BFNP)
_reg("c0b", (H, T), BFNP)
_reg("ap0", (D, T), BFNP)
_reg("attn0", (D, T), np.float32)
for i in range(13):
    _reg(f"xs{i}", (D, T), np.float32)
    _reg(f"xs{i}b", (D, T), BFNP)
_reg("sqr", (D, T), BFNP)
_reg("skr", (D, T), BFNP)
_reg("svf", (D, T), BFNP)
_reg("gact", (2816, T), BFNP)
_reg("augrd", (2, T), BFNP)
_reg("x1", (D, T), np.float32)
_reg("sq", (D, T), BFNP)
_reg("sk", (D, T), BFNP)
_reg("svt", (T, D), BFNP)
_reg("scb", (H, T), BFNP)
_reg("sap", (D, T), BFNP)
_reg("r1row", (1, T), np.float32)
_reg("xn1", (D, T), np.float32)
_reg("xn1b", (D, T), BFNP)
_reg("q1b", (D, T), BFNP)
_reg("k1b", (D, T), BFNP)
_reg("v1t", (T, D), BFNP)
_reg("c1m", (H, T), BFNP)
_reg("c1n", (H, T), BFNP)
_reg("c1p", (H, T), BFNP)
_reg("grev", (H, 1280), np.float32)
_reg("biasT", (H, 6, 128, TC), np.float32)
_reg("ap1", (D, T), BFNP)
_reg("attn1", (D, T), np.float32)
_reg("wqB", (1024, 1024), BFNP)
_reg("wqT", (1024, 1024), BFNP)
_reg("wkB", (256, 1024), BFNP)
_reg("wkT", (1024, 256), BFNP)
_reg("wvB", (256, 1024), BFNP)
_reg("wvT", (1024, 256), BFNP)
_reg("woB", (1024, 1024), BFNP)
_reg("woT", (1024, 1024), BFNP)
_reg("q2r", (D, T), BFNP)
_reg("k2r", (256, T), BFNP)
_reg("q2b", (D, T), BFNP)
_reg("k2b", (256, T), BFNP)
_reg("v2t", (T, 256), BFNP)
_reg("c2b", (H, T), BFNP)
_reg("ap2", (D, T), BFNP)
_reg("u", (D, T), np.float32)
_reg("gatev", (2, 2), np.float32)
_reg("gidx", (2, 2), np.float32)
_reg("fmixB", (4096, 1024), BFNP)
_reg("fmixT0", (1024, 4096), BFNP)
_reg("fmixT1", (1024, 4096), BFNP)
_reg("pmixB", (1024, 4096), BFNP)
_reg("pmixT0", (4096, 1024), BFNP)
_reg("pmixT1", (4096, 1024), BFNP)
_reg("hb", (D, T), BFNP)
_reg("hmid", (4096, T), np.float32)
_reg("h2b", (4096, T), BFNP)
_reg("ffn1a", (D, T), np.float32)
_reg("ffn1", (D, T), np.float32)
_reg("fsum", (D, T), np.float32)
_reg("out", (T, D), np.float32)



def _pair_perm(nrows):
    perm = np.zeros(nrows, dtype=np.int64)
    for c in range(nrows // 128):
        h0, h1 = 2 * c, 2 * c + 1
        b = c * 128
        perm[b + 0:b + 32] = h0 * 64 + np.arange(32)
        perm[b + 32:b + 64] = h1 * 64 + np.arange(32)
        perm[b + 64:b + 96] = h0 * 64 + 32 + np.arange(32)
        perm[b + 96:b + 128] = h1 * 64 + 32 + np.arange(32)
    return perm


ROPE_PERM = _pair_perm(D)
ROW_D = (ROPE_PERM % 64).astype(np.int64)
KPERM4 = _pair_perm(256)


def make_consts():
    c = {}
    c["identity_f32"] = np.eye(128, dtype=np.float32)
    c["zeros128"] = np.zeros((128, 1), dtype=np.float32)
    c["zeros128b"] = np.zeros((128, 1), dtype=np.float32).astype(BFNP)

    half = np.arange(0, HD, 2, dtype=np.float64)
    inv_freq = 1.0 / 10000.0 ** (half / HD)
    t = np.arange(T, dtype=np.float64)
    freqs = np.concatenate([t[:, None] * inv_freq[None, :]] * 2, -1)
    base_scale = (half + 0.4 * HD) / (1.4 * HD)
    power = (t - T // 2) / 128.0
    scale = np.concatenate([base_scale[None, :] ** power[:, None]] * 2, -1)
    cos, sin = np.cos(freqs), np.sin(freqs)
    dr = ROW_D[:128]
    sgn_x = np.where(np.arange(128) % 128 < 64, -1.0, 1.0)[:, None]
    c["xpos_cq"] = (8.0 * cos * scale)[:, dr].T.astype(np.float32)
    c["xpos_sq"] = (sgn_x * (8.0 * sin * scale)[:, dr].T).astype(np.float32)
    c["xpos_ck"] = (cos / scale)[:, dr].T.astype(np.float32)
    c["xpos_sk"] = (sgn_x * (sin / scale)[:, dr].T).astype(np.float32)

    fr = t[:, None] * (1.0 / 10000.0 ** (np.arange(0, HD, 2, dtype=np.float64) / HD))[None, :]
    d32 = dr % 32
    sgn_r = -sgn_x
    c["rope1_cq"] = (0.125 * np.cos(fr))[:, d32].T.astype(np.float32)
    c["rope1_sq"] = (sgn_r * (0.125 * np.sin(fr))[:, d32].T).astype(np.float32)
    c["rope1_ck"] = np.cos(fr)[:, d32].T.astype(np.float32)
    c["rope1_sk"] = (sgn_r * np.sin(fr)[:, d32].T).astype(np.float32)

    sw = np.zeros((128, 128), dtype=np.float32)
    for r in range(128):
        sw[r ^ 64, r] = 1.0
    c["swap64"] = sw.astype(BFNP)

    indH = np.zeros((NKD, 128, H), dtype=np.float32)
    for ch in range(NKD):
        for r in range(128):
            indH[ch, r, (ch * 128 + r) // 64] = 1.0
    c["indH"] = indH.astype(BFNP)
    indP = np.zeros((NKD, 128, H), dtype=np.float32)
    for ch in range(NKD):
        for r in range(128):
            indP[ch, r, 2 * ch + (1 if (32 <= r < 64 or 96 <= r < 128) else 0)] = 1.0
    c["indP"] = indP.astype(BFNP)
    indK = np.zeros((2, 128, 4), dtype=np.float32)
    for ch in range(2):
        for r in range(128):
            indK[ch, r, 2 * ch + (1 if (32 <= r < 64 or 96 <= r < 128) else 0)] = 1.0
    c["indK"] = indK.astype(BFNP)
    c["indPT"] = np.ascontiguousarray(indP.transpose(0, 2, 1)).astype(BFNP)
    c["indKT"] = np.ascontiguousarray(indK.transpose(0, 2, 1)).astype(BFNP)

    kk = np.arange(128)[:, None]
    qq = np.arange(TC)[None, :]
    cm = np.zeros((4, 128, TC), dtype=np.float32)
    for j, d0 in enumerate((0, 128, 256, 384)):
        cm[j] = ((kk + d0) <= qq)
    c["cmask"] = cm.astype(BFNP)
    qq2 = np.arange(256)[None, :]
    lm = np.zeros((4, 128, 256), dtype=np.float32)
    lm[0] = (qq2 <= kk)
    lm[1] = (qq2 <= kk + 128)
    lm[2] = (kk <= qq2)
    lm[3] = (kk + 128 <= qq2)
    c["lmask"] = lm.astype(BFNP)

    def bucket(d):
        nb = 16
        rb = nb if d > 0 else 0
        ad = abs(d)
        mx = nb // 2
        if ad < mx:
            return rb + ad
        rl = mx + int(math.log(ad / mx) / math.log(128 / mx) * (nb - mx))
        return rb + min(rl, nb - 1)
    oh = np.zeros((32, 255), dtype=np.float32)
    for i in range(255):
        oh[bucket(127 - i), i] = 1.0
    c["OHrev"] = oh

    f = np.arange(1024)
    c["iotaA_fc"] = (16 * (f % 64)).astype(np.float32)
    c["iotaB_fc"] = (f // 64).astype(np.float32)
    f4 = np.arange(4096)
    c["iotaA_pj"] = (64 * (f4 % 64)).astype(np.float32)
    c["iotaB_pj"] = (f4 // 64).astype(np.float32)
    return c


def make_weights(inp):
    w = {}

    def A(x):
        return np.ascontiguousarray(np.asarray(x, dtype=np.float32))

    w["L0qkw"] = np.ascontiguousarray(np.concatenate([A(inp["L0_attn_Wq"]), A(inp["L0_attn_Wk"])], axis=1))
    w["L0qkb"] = np.concatenate([A(inp["L0_attn_bq"]), A(inp["L0_attn_bk"])])
    w["L0wv"] = A(inp["L0_attn_Wv"])
    w["L0bv"] = A(inp["L0_attn_bv"])
    w["L0wo"] = A(inp["L0_attn_Wo"])
    w["L0bo"] = A(inp["L0_attn_bo"])

    Wqkv = A(inp["L0_ffn_Wqkv"])
    perm_cols = np.concatenate([ROPE_PERM, 1024 + ROPE_PERM, 2048 + np.arange(1024)])
    w["S_wqkv"] = np.ascontiguousarray(Wqkv[:, :, perm_cols])
    w["S_qs"] = np.ascontiguousarray(A(inp["L0_ffn_q_scale"])[:, ROW_D[:128]])
    w["S_ks"] = np.ascontiguousarray(A(inp["L0_ffn_k_scale"])[:, ROW_D[:128]])
    w["S_wo"] = A(inp["L0_ffn_Wo"])
    w["S_g"] = A(inp["L0_ffn_ln_g"])
    w["S_b"] = A(inp["L0_ffn_ln_b"])
    W1 = A(inp["L0_ffn_W1"])
    W1p = np.zeros((6, 1024, 5632), dtype=np.float32)
    W1p[:, :, 0:2730] = W1[:, :, 0:2730]
    W1p[:, :, 2816:2816 + 2730] = W1[:, :, 2730:5460]
    w["S_w1"] = W1p
    b1 = A(inp["L0_ffn_b1"])
    b1p = np.zeros((6, 5632), dtype=np.float32)
    b1p[:, 0:2730] = b1[:, 0:2730]
    b1p[:, 2816:2816 + 2730] = b1[:, 2730:5460]
    w["S_b1"] = b1p
    W2 = A(inp["L0_ffn_W2"])
    W2p = np.zeros((6, 2816, 1024), dtype=np.float32)
    W2p[:, 0:2730, :] = W2
    w["S_w2"] = W2p
    w["S_b2"] = A(inp["L0_ffn_b2"])

    w["L1qkv"] = A(inp["L1_attn_W_qkv"])
    w["L1wo"] = A(inp["L1_attn_W_out"])
    w["relT"] = np.ascontiguousarray(A(inp["L1_attn_rel_emb"]).T)
    w["rel32"] = A(inp["L1_attn_rel_emb"])

    w["F_wq"] = np.ascontiguousarray(A(inp["L1_ffn_wq"])[ROPE_PERM, :])
    w["F_wk"] = np.ascontiguousarray(A(inp["L1_ffn_wk"])[KPERM4, :])
    w["F_wv"] = A(inp["L1_ffn_wv"])
    w["F_wo"] = A(inp["L1_ffn_wo"])
    w["F_qg"] = A(inp["L1_ffn_q_gain"])
    w["F_kg"] = A(inp["L1_ffn_k_gain"])
    w["F_fs1"] = A(inp["L1_ffn_fc_s1"])
    w["F_fs2"] = A(inp["L1_ffn_fc_s2"])
    w["F_fmix"] = A(inp["L1_ffn_fc_mix"])
    w["F_flog"] = A(inp["L1_ffn_fc_logits"])
    w["F_ps1"] = A(inp["L1_ffn_pj_s1"])
    w["F_ps2"] = A(inp["L1_ffn_pj_s2"])
    w["F_pmix"] = A(inp["L1_ffn_pj_mix"])
    w["F_plog"] = A(inp["L1_ffn_pj_logits"])
    return w


class Ctx:
    def __init__(self, nc, block, stack):
        self.nc = nc
        self.block = block
        self.stack = stack
        self.sstack = None
        self.sem = {}
        self.cnt = {}
        for s in ("qS", "qG", "qA", "eT", "eS", "eV", "eG", "bar"):
            self.sem[s] = stack.enter_context(nc.semaphore(s))
            self.cnt[s] = 0
        self.dram = {}
        self.stage_idx = 0
        self._uid = 0

    def uid(self):
        self._uid += 1
        return self._uid

    def body(self, eng, fn):
        getattr(self.block, eng)(fn)

    def sb(self, shape, dtype, name="t"):
        return self.sstack.enter_context(self.nc.sbuf_tensor(f"{name}{self.uid()}", list(shape), dtype))

    def psum(self, shape, dtype=F32, name="p"):
        return self.sstack.enter_context(self.nc.psum_tensor(f"{name}{self.uid()}", list(shape), dtype))


class WT:

    def __init__(self, ctx, e):
        self.ctx = ctx
        self.e = e
        self.last = {}

    def __call__(self, sem, val):
        if val > self.last.get(sem, 0):
            self.e.wait_ge(self.ctx.sem[sem], val)
            self.last[sem] = val


def emit_barrier(ctx):
    ctx.stage_idx += 1
    target = 5 * ctx.stage_idx
    snap = dict(ctx.cnt)
    for eng in ENGS:
        q = _QUEUE_OF.get(eng)

        def fn(e, q=q, t=target, snap=snap):
            if q is not None:
                e.wait_ge(ctx.sem[q], snap[q])
            e.nop().then_inc(ctx.sem["bar"], 1)
            e.wait_ge(ctx.sem["bar"], t)
        ctx.body(eng, fn)



def load_weights_resident(ctx, w_dram, K, M, name="w", nbuf=2):
    NKC = K // 128
    wsb = ctx.sb((128, NKC * M), BF16, name)
    if w_dram.dtype == BF16:
        b_qS = ctx.cnt["qS"]
        ctx.cnt["qS"] = b_qS + 16 * NKC
        tgt = ctx.cnt["qS"]
        b_eV = ctx.cnt["eV"]
        ctx.cnt["eV"] = b_eV + 1

        def f_syncb(e):
            e.wait_ge(ctx.sem["qS"], b_qS)
            for kc in range(NKC):
                e.dma_start(out=wsb[:, kc * M:(kc + 1) * M],
                            in_=w_dram[kc * 128:(kc + 1) * 128, :]).then_inc(ctx.sem["qS"], 16)
        ctx.body("sync", f_syncb)

        def f_vecb(e):
            e.wait_ge(ctx.sem["qS"], tgt)
            e.nop().then_inc(ctx.sem["eV"], 1)
        ctx.body("vector", f_vecb)
        return wsb, ctx.cnt["eV"]
    wtmp = [ctx.sb((128, M), F32, name + "f") for _ in range(nbuf)]
    b_qS = ctx.cnt["qS"]
    ld = [b_qS + 16 * (kc + 1) for kc in range(NKC)]
    ctx.cnt["qS"] = b_qS + 16 * NKC
    b_eV = ctx.cnt["eV"]
    ctx.cnt["eV"] = b_eV + NKC
    done = ctx.cnt["eV"]

    def f_sync(e):
        for kc in range(NKC):
            if kc >= 1:
                e.wait_ge(ctx.sem["qS"], ld[kc - 1])
            if kc >= nbuf:
                e.wait_ge(ctx.sem["eV"], b_eV + kc - nbuf + 1)
            e.dma_start(out=wtmp[kc % nbuf][:], in_=w_dram[kc * 128:(kc + 1) * 128, :]).then_inc(ctx.sem["qS"], 16)
    ctx.body("sync", f_sync)

    def f_vec(e):
        for kc in range(NKC):
            e.wait_ge(ctx.sem["qS"], ld[kc])
            e.tensor_copy(wsb[:, kc * M:(kc + 1) * M], wtmp[kc % nbuf][:]).then_inc(ctx.sem["eV"], 1)
    ctx.body("vector", f_vec)
    load_weights_resident.last_tmp = wtmp
    return wsb, done


def load_bias_cols(ctx, b_dram, n_cols, scale_first=None):
    tile = ctx.sb((128, n_cols), F32, "bias")
    b_qS = ctx.cnt["qS"]
    ctx.cnt["qS"] = b_qS + 16 * n_cols
    b_eV = ctx.cnt["eV"]
    ctx.cnt["eV"] = b_eV + 1

    def f_sync(e):
        e.wait_ge(ctx.sem["qS"], b_qS)
        for j in range(n_cols):
            e.dma_start(out=tile[:, j], in_=b_dram[j * 128:(j + 1) * 128]).then_inc(ctx.sem["qS"], 16)
    ctx.body("sync", f_sync)

    def f_vec(e):
        e.wait_ge(ctx.sem["qS"], b_qS + 16 * n_cols)
        if scale_first is not None:
            ncs, s = scale_first
            e.tensor_scalar_mul(tile[:, 0:ncs], tile[:, 0:ncs], s).then_inc(ctx.sem["eV"], 1)
        else:
            e.nop().then_inc(ctx.sem["eV"], 1)
    ctx.body("vector", f_vec)
    return tile, ctx.cnt["eV"]


def gemm_fm(ctx, *, wsb, w_ready, NKC, MC, rhs_dram, epi_emit, rhs_shift=False,
            aug=None):
    M = MC * 128
    NB = 2
    psums = [ctx.psum((128, TC), F32, "gp") for _ in range(NB)]
    rbuf = [[ctx.sb((128, TC), BF16, "rb") for _ in range(NKC)] for _ in range(2)]
    zcol = ctx.dram["zeros128b"]

    b = dict(ctx.cnt)
    per_tc = [NKC + ((NKC - NKC // 2) if (rhs_shift and i == 0) else 0) for i in range(NT)]
    ld_t = []
    cur = b["qS"]
    for i in range(NT):
        cur += 16 * per_tc[i]
        ld_t.append(cur)
    ctx.cnt["qS"] = cur
    mm0 = b["eT"]
    n_iter = NT * MC
    ctx.cnt["eT"] = mm0 + n_iter
    epi0 = b["eS"]
    ctx.cnt["eS"] = epi0 + n_iter

    info = dict(psums=psums, mm0=mm0, epi0=epi0, MC=MC, NB=NB, b=b)

    def f_sync(e):
        e.wait_ge(ctx.sem["qS"], b["qS"])
        for i in range(NT):
            if i >= 1:
                e.wait_ge(ctx.sem["qS"], ld_t[i - 1])
            if i >= 2:
                e.wait_ge(ctx.sem["eT"], mm0 + (i - 1) * MC)
            for kc in range(NKC):
                dst = rbuf[i % 2][kc]
                if rhs_shift and kc >= NKC // 2:
                    if i == 0:
                        e.dma_start(out=dst[:, 0:1], in_=zcol[:, :]).then_inc(ctx.sem["qS"], 16)
                        e.dma_start(out=dst[:, 1:TC],
                                    in_=rhs_dram[kc * 128:(kc + 1) * 128, 0:TC - 1]).then_inc(ctx.sem["qS"], 16)
                    else:
                        e.dma_start(out=dst[:],
                                    in_=rhs_dram[kc * 128:(kc + 1) * 128, i * TC - 1:(i + 1) * TC - 1]).then_inc(ctx.sem["qS"], 16)
                else:
                    e.dma_start(out=dst[:],
                                in_=rhs_dram[kc * 128:(kc + 1) * 128, i * TC:(i + 1) * TC]).then_inc(ctx.sem["qS"], 16)
    ctx.body("sync", f_sync)

    def f_tensor(e):
        e.wait_ge(ctx.sem["eV"], w_ready)
        if aug is not None:
            e.wait_ge(aug[2][0], aug[2][1])
        for i in range(NT):
            e.wait_ge(ctx.sem["qS"], ld_t[i])
            for mc in range(MC):
                it = i * MC + mc
                bank = psums[it % NB]
                if it >= NB:
                    e.wait_ge(ctx.sem["eS"], epi0 + it - NB + 1)
                ins = None
                for kc in range(NKC):
                    ins = e.matmul(bank[:], wsb[:, kc * M + mc * 128: kc * M + mc * 128 + 128],
                                   rbuf[i % 2][kc][:], start=(kc == 0),
                                   stop=(kc == NKC - 1 and aug is None),
                                   skip_group_check=True)
                if aug is not None:
                    ins = e.matmul(bank[:], aug[0][:, mc * 128:mc * 128 + 128], aug[1](i),
                                   start=False, stop=True, skip_group_check=True)
                ins.then_inc(ctx.sem["eT"], 1)
    ctx.body("tensor", f_tensor)
    epi_emit(info)
    return info


def epi_act_store(ctx, info, stores, bias=None, scale=None, func=None, out_dtype=BF16):
    MC, NB, psums = info["MC"], info["NB"], info["psums"]
    obuf = [ctx.sb((128, TC), out_dtype, "ob") for _ in range(4)]
    st0 = ctx.cnt["qG"]
    plan = [stores(i, mc) for i in range(NT) for mc in range(MC)]
    ctx.cnt["qG"] = st0 + 16 * sum(len(p) for p in plan)
    epi0, mm0 = info["epi0"], info["mm0"]
    cum = [0]
    for p in plan:
        cum.append(cum[-1] + len(p))

    def f_scalar(e):
        for it in range(len(plan)):
            mc = it % MC
            e.wait_ge(ctx.sem["eT"], mm0 + it + 1)
            if it >= 4:
                e.wait_ge(ctx.sem["qG"], st0 + 16 * cum[it - 3])
            f = func(mc) if func else AF.Copy
            bb = bias(mc) if bias else 0.0
            if f == AF.Copy and not isinstance(bb, float):
                f = AF.Identity
            e.activation(out=obuf[it % 4][:], in_=psums[it % NB][:, :],
                         func=f, bias=bb,
                         scale=(scale(mc) if scale else 1.0)).then_inc(ctx.sem["eS"], 1)
    ctx.body("scalar", f_scalar)

    def f_gp(e):
        for it in range(len(plan)):
            e.wait_ge(ctx.sem["eS"], epi0 + it + 1)
            if it >= 1:
                e.wait_ge(ctx.sem["qG"], st0 + 16 * cum[it])
            for d in plan[it]:
                e.dma_start(out=d, in_=obuf[it % 4][:]).then_inc(ctx.sem["qG"], 16)
    ctx.body("gpsimd", f_gp)



def stage_ingest(ctx):
    transpose_td_to_dt(ctx, src=ctx.dram["x_in"], dst=ctx.dram["x_fm"])


def transpose_td_to_dt(ctx, src, dst):
    ident = ctx.dram["identity_f32"]
    idt = ctx.sb((128, 128), F32, "id")
    NBUF = 3
    tin = [ctx.sb((128, D), F32, "ti") for _ in range(NBUF)]
    tout = [ctx.sb((128, 128), F32, "to") for _ in range(4)]
    pss = [ctx.psum((128, 128), F32, "pt") for _ in range(2)]
    NTT = T // 128

    b = dict(ctx.cnt)
    ld = [b["qS"] + 16 * (1 + i + 1) for i in range(NTT)]
    ctx.cnt["qS"] = b["qS"] + 16 * (1 + NTT)
    mm0 = b["eT"]
    ctx.cnt["eT"] = mm0 + NTT * 8
    ev0 = b["eS"]
    ctx.cnt["eS"] = ev0 + NTT * 8
    st0 = b["qG"]
    ctx.cnt["qG"] = st0 + 16 * NTT * 8

    def f_sync(e):
        e.dma_start(out=idt[:], in_=ident[:, :]).then_inc(ctx.sem["qS"], 16)
        for tt in range(NTT):
            if tt >= 1:
                e.wait_ge(ctx.sem["qS"], ld[tt - 1])
            if tt >= NBUF:
                e.wait_ge(ctx.sem["eT"], mm0 + (tt - NBUF + 1) * 8)
            e.dma_start(out=tin[tt % NBUF][:], in_=src[tt * 128:(tt + 1) * 128, :]).then_inc(ctx.sem["qS"], 16)
    ctx.body("sync", f_sync)

    def f_tensor(e):
        for tt in range(NTT):
            e.wait_ge(ctx.sem["qS"], ld[tt])
            for fc in range(8):
                k = tt * 8 + fc
                if k >= 2:
                    e.wait_ge(ctx.sem["eS"], ev0 + k - 1)
                e.transpose(pss[k % 2][:], tin[tt % NBUF][:, fc * 128:(fc + 1) * 128],
                            idt[:]).then_inc(ctx.sem["eT"], 1)
    ctx.body("tensor", f_tensor)

    def f_scalar(e):
        for k in range(NTT * 8):
            e.wait_ge(ctx.sem["eT"], mm0 + k + 1)
            if k >= 4:
                e.wait_ge(ctx.sem["qG"], st0 + 16 * (k - 3))
            e.copy(tout[k % 4][:], pss[k % 2][:]).then_inc(ctx.sem["eS"], 1)
    ctx.body("scalar", f_scalar)

    def f_gp(e):
        for tt in range(NTT):
            for fc in range(8):
                k = tt * 8 + fc
                e.wait_ge(ctx.sem["eS"], ev0 + k + 1)
                if k >= 1:
                    e.wait_ge(ctx.sem["qG"], st0 + 16 * k)
                e.dma_start(out=dst[fc * 128:(fc + 1) * 128, tt * 128:(tt + 1) * 128],
                            in_=tout[k % 4][:]).then_inc(ctx.sem["qG"], 16)
    ctx.body("gpsimd", f_gp)


def stage_ln(ctx, src, dsts_f32, dsts_bf16, eps=1e-5, shift=False):
    xfm = ctx.dram[src]
    d_f = [ctx.dram[n] for n in dsts_f32]
    d_b = [ctx.dram[n] for n in dsts_bf16]
    zcol = ctx.dram["zeros128"]

    ones1 = ctx.sb((128, 1), F32, "on1")
    onesr = ctx.sb((1, 128), F32, "onr")
    xt = [[ctx.sb((128, TC), F32, "x") for _ in range(NKD)] for _ in range(2)]
    sq = [ctx.sb((128, TC), F32, "sq") for _ in range(NKD)]
    rows = ctx.sb((1, 4 * TC), F32, "rw")
    rstdA = ctx.sb((1, T), F32, "rstdA")
    nmrA = ctx.sb((1, T), F32, "nmrA")
    xo = [ctx.sb((128, TC), F32, "xo") for _ in range(3)]
    xob = [ctx.sb((128, TC), BF16, "xob") for _ in range(3)]
    p_s = ctx.psum((1, TC), F32, "ps")
    p_q = ctx.psum((1, TC), F32, "pq")
    p_r = ctx.psum((128, TC), F32, "pr")
    p_m = ctx.psum((128, TC), F32, "pm")

    def n_loads(i):
        return NKD + ((NKD - NKD // 2) if (shift and i == 0) else 0)

    def load_tiles(e, i, buf):
        for kc in range(NKD):
            dst = xt[buf][kc]
            if shift and kc >= NKD // 2:
                if i == 0:
                    e.dma_start(out=dst[:, 0:1], in_=zcol[:, :]).then_inc(ctx.sem["qS"], 16)
                    e.dma_start(out=dst[:, 1:TC],
                                in_=xfm[kc * 128:(kc + 1) * 128, 0:TC - 1]).then_inc(ctx.sem["qS"], 16)
                else:
                    e.dma_start(out=dst[:],
                                in_=xfm[kc * 128:(kc + 1) * 128, i * TC - 1:(i + 1) * TC - 1]).then_inc(ctx.sem["qS"], 16)
            else:
                e.dma_start(out=dst[:],
                            in_=xfm[kc * 128:(kc + 1) * 128, i * TC:(i + 1) * TC]).then_inc(ctx.sem["qS"], 16)

    b = dict(ctx.cnt)
    eT0, eS0, eV0, eG0 = b["eT"], b["eS"], b["eV"], b["eG"]
    NV1 = 7
    NS1 = 9
    ld1 = []
    cur = b["qS"]
    for i in range(NT):
        cur += 16 * n_loads(i)
        ld1.append(cur)
    ctx.cnt["qS"] = cur

    def f1_sync(e):
        for i in range(NT):
            if i >= 1:
                e.wait_ge(ctx.sem["qS"], ld1[i - 1])
            if i >= 2:
                e.wait_ge(ctx.sem["eT"], eT0 + (i - 1))
            load_tiles(e, i, i % 2)
    ctx.body("sync", f1_sync)

    def f1_gp(e):
        e.memset(ones1[:], 1.0).then_inc(ctx.sem["eG"], 1)
        e.memset(onesr[:], 1.0).then_inc(ctx.sem["eG"], 1)
    ctx.body("gpsimd", f1_gp)
    ctx.cnt["eG"] = eG0 + 2

    def f1_scalar(e):
        for i in range(NT):
            e.wait_ge(ctx.sem["qS"], ld1[i])
            if i >= 1:
                e.wait_ge(ctx.sem["eT"], eT0 + i)
            for kc in range(NKD):
                e.square(sq[kc][:], xt[i % 2][kc][:]).then_inc(ctx.sem["eS"], 1)
            e.wait_ge(ctx.sem["eV"], eV0 + NV1 * i + 5)
            e.activation(out=rows[:, 3 * TC:4 * TC], in_=rows[:, 2 * TC:3 * TC],
                         func=AF.Sqrt).then_inc(ctx.sem["eS"], 1)
    ctx.body("scalar", f1_scalar)
    ctx.cnt["eS"] = eS0 + NS1 * NT

    def f1_tensor(e):
        e.wait_ge(ctx.sem["eG"], eG0 + 2)
        for i in range(NT):
            e.wait_ge(ctx.sem["qS"], ld1[i])
            if i >= 1:
                e.wait_ge(ctx.sem["eV"], eV0 + NV1 * (i - 1) + 2)
            for kc in range(NKD):
                e.matmul(p_s[:], ones1[:], xt[i % 2][kc][:], start=(kc == 0),
                         stop=(kc == NKD - 1), skip_group_check=True)
            e.wait_ge(ctx.sem["eS"], eS0 + NS1 * i + 8)
            ins = None
            for kc in range(NKD):
                ins = e.matmul(p_q[:], ones1[:], sq[kc][:], start=(kc == 0),
                               stop=(kc == NKD - 1), skip_group_check=True)
            ins.then_inc(ctx.sem["eT"], 1)
    ctx.body("tensor", f1_tensor)
    ctx.cnt["eT"] = eT0 + NT

    def f1_vec(e):
        for i in range(NT):
            bv = eV0 + NV1 * i
            e.wait_ge(ctx.sem["eT"], eT0 + i + 1)
            if i >= 1:
                e.wait_ge(ctx.sem["eV"], bv)
            e.tensor_scalar_mul(rows[:, 0:TC], p_s[:], 1.0 / D).then_inc(ctx.sem["eV"], 1)
            e.tensor_scalar_mul(rows[:, TC:2 * TC], p_q[:], 1.0 / D).then_inc(ctx.sem["eV"], 1)
            e.wait_ge(ctx.sem["eV"], bv + 2)
            e.tensor_tensor(rows[:, 2 * TC:3 * TC], rows[:, 0:TC], rows[:, 0:TC],
                            ALU.mult).then_inc(ctx.sem["eV"], 1)
            e.wait_ge(ctx.sem["eV"], bv + 3)
            e.tensor_sub(rows[:, 2 * TC:3 * TC], rows[:, TC:2 * TC],
                         rows[:, 2 * TC:3 * TC]).then_inc(ctx.sem["eV"], 1)
            e.wait_ge(ctx.sem["eV"], bv + 4)
            e.tensor_scalar_add(rows[:, 2 * TC:3 * TC], rows[:, 2 * TC:3 * TC],
                                eps).then_inc(ctx.sem["eV"], 1)
            e.wait_ge(ctx.sem["eS"], eS0 + NS1 * i + 9)
            e.reciprocal(rstdA[:, i * TC:(i + 1) * TC],
                         rows[:, 3 * TC:4 * TC]).then_inc(ctx.sem["eV"], 1)
            e.wait_ge(ctx.sem["eV"], bv + 6)
            e.scalar_tensor_tensor(nmrA[:, i * TC:(i + 1) * TC], rows[:, 0:TC], -1.0,
                                   rstdA[:, i * TC:(i + 1) * TC], ALU.mult,
                                   ALU.mult).then_inc(ctx.sem["eV"], 1)
    ctx.body("vector", f1_vec)
    ctx.cnt["eV"] = eV0 + NV1 * NT
    ph1_done = ctx.cnt["eV"]

    b2 = dict(ctx.cnt)
    eT20, eV20 = b2["eT"], b2["eV"]
    st0 = b2["qG"]
    n_dst = len(d_f) + len(d_b)
    ld2 = []
    cur = ctx.cnt["qS"]
    for i in range(NT):
        cur += 16 * n_loads(i)
        ld2.append(cur)
    ctx.cnt["qS"] = cur

    def f2_sync(e):
        e.wait_ge(ctx.sem["eV"], ph1_done)
        e.wait_ge(ctx.sem["qS"], b2["qS"])
        for i in range(NT):
            if i >= 1:
                e.wait_ge(ctx.sem["qS"], ld2[i - 1])
            if i >= 2:
                e.wait_ge(ctx.sem["eV"], eV20 + 3 * NKD * (i - 1))
            load_tiles(e, i, i % 2)
    ctx.body("sync", f2_sync)

    def f2_tensor(e):
        e.wait_ge(ctx.sem["eV"], ph1_done)
        for i in range(NT):
            if i >= 1:
                e.wait_ge(ctx.sem["eV"], eV20 + 3 * NKD * i)
            e.matmul(p_r[:], onesr[:], rstdA[:, i * TC:(i + 1) * TC], start=True,
                     stop=True, skip_group_check=True)
            e.matmul(p_m[:], onesr[:], nmrA[:, i * TC:(i + 1) * TC], start=True,
                     stop=True, skip_group_check=True).then_inc(ctx.sem["eT"], 1)
    ctx.body("tensor", f2_tensor)
    ctx.cnt["eT"] = eT20 + NT

    def f2_vec(e):
        for i in range(NT):
            e.wait_ge(ctx.sem["qS"], ld2[i])
            e.wait_ge(ctx.sem["eT"], eT20 + i + 1)
            for kc in range(NKD):
                kidx = i * NKD + kc
                bv = eV20 + 3 * kidx
                if kidx >= 3:
                    e.wait_ge(ctx.sem["qG"], st0 + 16 * n_dst * (kidx - 2))
                e.tensor_mul(xo[kidx % 3][:], xt[i % 2][kc][:], p_r[:]).then_inc(ctx.sem["eV"], 1)
                e.wait_ge(ctx.sem["eV"], bv + 1)
                e.tensor_add(xo[kidx % 3][:], xo[kidx % 3][:], p_m[:]).then_inc(ctx.sem["eV"], 1)
                e.wait_ge(ctx.sem["eV"], bv + 2)
                e.tensor_copy(xob[kidx % 3][:], xo[kidx % 3][:]).then_inc(ctx.sem["eV"], 1)
    ctx.body("vector", f2_vec)
    ctx.cnt["eV"] = eV20 + 3 * NKD * NT

    def f2_gp(e):
        for i in range(NT):
            for kc in range(NKD):
                kidx = i * NKD + kc
                e.wait_ge(ctx.sem["eV"], eV20 + 3 * (kidx + 1))
                if kidx >= 1:
                    e.wait_ge(ctx.sem["qG"], st0 + 16 * n_dst * kidx)
                r0, c0 = kc * 128, i * TC
                for d in d_f:
                    e.dma_start(out=d[r0:r0 + 128, c0:c0 + TC], in_=xo[kidx % 3][:]).then_inc(ctx.sem["qG"], 16)
                for d in d_b:
                    e.dma_start(out=d[r0:r0 + 128, c0:c0 + TC], in_=xob[kidx % 3][:]).then_inc(ctx.sem["qG"], 16)
    ctx.body("gpsimd", f2_gp)
    ctx.cnt["qG"] = st0 + 16 * n_dst * NKD * NT


def stage_qk(ctx, w_ap, M2, q_out, k_out, bias_vec=None, q_scale=1.0,
             rhs_name="xn0b"):
    MC = M2 // 128
    MCh = MC // 2
    wsb, wrdy = load_weights_resident(ctx, w_ap, D, M2, "wqk")
    if bias_vec is not None:
        btile, brdy = load_bias_cols(ctx, ctx.dram[bias_vec], MC,
                                     scale_first=(MCh, q_scale) if q_scale != 1.0 else None)
        wrdy = max(wrdy, brdy)
    qd, kd = ctx.dram[q_out], ctx.dram[k_out]

    def stores(i, mc):
        if mc < MCh:
            return [qd[mc * 128:(mc + 1) * 128, i * TC:(i + 1) * TC]]
        return [kd[(mc - MCh) * 128:(mc - MCh + 1) * 128, i * TC:(i + 1) * TC]]

    def epi(info):
        epi_act_store(ctx, info, stores,
                      bias=(lambda mc: btile[:, mc:mc + 1]) if bias_vec else None,
                      scale=lambda mc: q_scale if mc < MCh else 1.0,
                      out_dtype=BF16)
    gemm_fm(ctx, wsb=wsb, w_ready=wrdy, NKC=NKD, MC=MC,
            rhs_dram=ctx.dram[rhs_name], epi_emit=epi)


def stage_qkv0(ctx):
    stage_qk(ctx, ctx.dram["L0qkw"], 2048, "q0b", "k0b", bias_vec="L0qkb",
             q_scale=0.125, rhs_name="xn0b")


def stage_vproj(ctx, w_ap, b_name, rhs_name, out_name, scale_row=None):
    Dout = SCRATCH[out_name][0][1]
    WN = min(TC, Dout)
    MCn = Dout // WN
    wsb, wrdy = load_weights_resident(ctx, w_ap, D, Dout, "wv")
    vout = ctx.dram[out_name]
    srow = ctx.dram[scale_row] if scale_row else None

    xres = ctx.sb((128, NKD * T), BF16, "xres")
    b_qS = ctx.cnt["qS"]
    ctx.cnt["qS"] = b_qS + 16 * NKD
    xl = ctx.cnt["qS"]

    def f_sync(e):
        e.wait_ge(ctx.sem["qS"], b_qS)
        for kc in range(NKD):
            e.dma_start(out=xres[:, kc * T:(kc + 1) * T],
                        in_=ctx.dram[rhs_name][kc * 128:(kc + 1) * 128, :]).then_inc(ctx.sem["qS"], 16)
    ctx.body("sync", f_sync)

    NTT = T // 128
    use_bias = b_name is not None
    onesr = ctx.sb((1, 128), F32, "onr")
    brow = ctx.sb((1, Dout), F32, "brow") if use_bias else None
    bvb = ctx.sb((128, Dout), F32, "bvb") if use_bias else None
    scol = [ctx.sb((128, 1), F32, "scl") for _ in range(2)] if srow is not None else None
    p_b = ctx.psum((128, TC), F32, "pbb") if use_bias else None
    b_qS2 = ctx.cnt["qS"]
    eG0 = ctx.cnt["eG"]
    eT0 = ctx.cnt["eT"]
    eS0 = ctx.cnt["eS"]
    bias_rdy = eS0

    if use_bias:
        ctx.cnt["qS"] = b_qS2 + 16

        def f_sync2(e):
            e.wait_ge(ctx.sem["qS"], xl)
            e.dma_start(out=brow[:], in_=ctx.dram[b_name][:]).then_inc(ctx.sem["qS"], 16)
        ctx.body("sync", f_sync2)

        def f_gp0(e):
            e.memset(onesr[:], 1.0).then_inc(ctx.sem["eG"], 1)
        ctx.body("gpsimd", f_gp0)
        ctx.cnt["eG"] = eG0 + 1

        def f_t0(e):
            e.wait_ge(ctx.sem["eG"], eG0 + 1)
            e.wait_ge(ctx.sem["qS"], b_qS2 + 16)
            for j in range(MCn):
                if j >= 1:
                    e.wait_ge(ctx.sem["eS"], eS0 + j)
                e.matmul(p_b[:], onesr[:], brow[:, j * TC:(j + 1) * TC], start=True,
                         stop=True, skip_group_check=True).then_inc(ctx.sem["eT"], 1)
        ctx.body("tensor", f_t0)

        def f_s0(e):
            for j in range(MCn):
                e.wait_ge(ctx.sem["eT"], eT0 + j + 1)
                e.copy(bvb[:, j * TC:(j + 1) * TC], p_b[:]).then_inc(ctx.sem["eS"], 1)
        ctx.body("scalar", f_s0)
        ctx.cnt["eT"] = eT0 + MCn
        ctx.cnt["eS"] = eS0 + MCn
        bias_rdy = ctx.cnt["eS"]

    NB = 2
    psums = [ctx.psum((128, TC), F32, "vp") for _ in range(NB)]
    obuf = [ctx.sb((128, TC), BF16, "vo") for _ in range(4)]
    b3 = dict(ctx.cnt)
    mm0 = b3["eT"]
    ev0 = b3["eV"]
    st0 = b3["qG"]
    qA0 = b3["qA"]
    n_iter = NTT * MCn
    ctx.cnt["eT"] = mm0 + n_iter
    nv_per = 1 if (use_bias or srow is not None) else 1
    ctx.cnt["eV"] = ev0 + n_iter
    ctx.cnt["qG"] = st0 + 16 * n_iter
    if srow is not None:
        srld = [qA0 + 16 * (tt + 1) for tt in range(NTT)]
        ctx.cnt["qA"] = qA0 + 16 * NTT

        def f_sc_s(e):
            w = WT(ctx, e)
            w("qA", qA0)
            for tt in range(NTT):
                if tt >= 1:
                    w("qA", srld[tt - 1])
                if tt >= 2:
                    w("eV", ev0 + MCn * (tt - 1))
                e.dma_start(out=scol[tt % 2][:, 0:1],
                            in_=srow[:, tt * 128:(tt + 1) * 128].rearrange("o t -> t o")).then_inc(ctx.sem["qA"], 16)
        ctx.body("scalar", f_sc_s)

    def f_tensor(e):
        e.wait_ge(ctx.sem["eV"], wrdy)
        e.wait_ge(ctx.sem["qS"], xl)
        for tt in range(NTT):
            for j in range(MCn):
                it = tt * MCn + j
                bank = psums[it % NB]
                if it >= NB:
                    e.wait_ge(ctx.sem["eV"], ev0 + it - NB + 1)
                ins = None
                for kc in range(NKD):
                    ins = e.matmul(bank[:, 0:WN], xres[:, kc * T + tt * 128: kc * T + tt * 128 + 128],
                                   wsb[:, kc * Dout + j * WN: kc * Dout + (j + 1) * WN],
                                   start=(kc == 0), stop=(kc == NKD - 1),
                                   skip_group_check=True)
                ins.then_inc(ctx.sem["eT"], 1)
    ctx.body("tensor", f_tensor)

    def f_vec(e):
        w = WT(ctx, e)
        if use_bias:
            w("eS", bias_rdy)
        for it in range(n_iter):
            tt, j = it // MCn, it % MCn
            w("eT", mm0 + it + 1)
            if srow is not None:
                w("qA", srld[tt])
            if it >= 4:
                w("qG", st0 + 16 * (it - 3))
            if use_bias:
                e.tensor_add(obuf[it % 4][:, 0:WN], psums[it % NB][:, 0:WN],
                             bvb[:, j * TC:(j + 1) * TC]).then_inc(ctx.sem["eV"], 1)
            elif srow is not None:
                e.tensor_scalar_mul(obuf[it % 4][:, 0:WN], psums[it % NB][:, 0:WN],
                                    scol[tt % 2][:, 0:1]).then_inc(ctx.sem["eV"], 1)
            else:
                e.tensor_copy(obuf[it % 4][:, 0:WN], psums[it % NB][:, 0:WN]).then_inc(ctx.sem["eV"], 1)
    ctx.body("vector", f_vec)

    def f_gp(e):
        for tt in range(NTT):
            for j in range(MCn):
                it = tt * MCn + j
                e.wait_ge(ctx.sem["eV"], ev0 + it + 1)
                if it >= 1:
                    e.wait_ge(ctx.sem["qG"], st0 + 16 * it)
                e.dma_start(out=vout[tt * 128:(tt + 1) * 128, j * WN:(j + 1) * WN],
                            in_=obuf[it % 4][:, 0:WN]).then_inc(ctx.sem["qG"], 16)
    ctx.body("gpsimd", f_gp)


def stage_attnprep(ctx, qname, kname, cb_out, *, ind_q, ind_k, nh_k,
                   window=None, a_coef=0.35, a_const=None, variants=None):
    qb = ctx.dram[qname]
    kb = ctx.dram[kname]
    NKQ = SCRATCH[qname][0][0] // 128
    NKK = SCRATCH[kname][0][0] // 128
    indq = ctx.dram[ind_q]
    indk = ctx.dram[ind_k]
    const_mode = a_const is not None
    gqa = const_mode and a_const[0] == "gqa"

    iq = ctx.sb((128, NKQ * H), BF16, "iq")
    ik = ctx.sb((128, NKK * nh_k), BF16, "ik")
    qt = [ctx.sb((128, TC), BF16, "qt") for _ in range(NKQ)]
    kt = [ctx.sb((128, TC), BF16, "kt") for _ in range(NKK)]
    prod = [ctx.sb((128, TC), BF16, "pr") for _ in range(max(NKQ, NKK))]
    krepb = [ctx.sb((128, TC), BF16, "krp") for _ in range(NKQ)] if gqa else None
    qnR = ctx.sb((H, T), F32, "qnR")
    dgR = ctx.sb((H, T), F32, "dgR")
    knR = ctx.sb((nh_k, T), F32, "knR")
    kmax = ctx.sb((nh_k, 16), F32, "kmax")
    arow = ctx.sb((1, 260), F32, "arow")
    acol = ctx.sb((H, 1), F32, "acol")
    onesr = ctx.sb((1, H), F32, "onh")
    cw = ctx.sb((H, TC), F32, "cw")
    n_var = 1 + (len(variants) if variants else 0)
    cb = [ctx.sb((H, TC), BF16, "cb") for _ in range(2 * n_var)]
    relc = ctx.sb((H, 32), F32, "relc") if variants else None
    p_qn = ctx.psum((H, TC), F32, "pqn")
    p_kn = ctx.psum((nh_k, TC), F32, "pkn")
    p_dg = ctx.psum((H, TC), F32, "pdg")
    p_a = ctx.psum((H, 1), F32, "pac")

    b = dict(ctx.cnt)
    eT0, eS0, eV0, eG0 = b["eT"], b["eS"], b["eV"], b["eG"]
    ctx.cnt["qS"] = b["qS"] + 16 * (NKQ + NKK)
    ind_ld = ctx.cnt["qS"]

    def f_sync0(e):
        e.wait_ge(ctx.sem["qS"], b["qS"])
        for c in range(NKQ):
            e.dma_start(out=iq[:, c * H:(c + 1) * H], in_=indq[c, :, :]).then_inc(ctx.sem["qS"], 16)
        for c in range(NKK):
            e.dma_start(out=ik[:, c * nh_k:(c + 1) * nh_k], in_=indk[c, :, :]).then_inc(ctx.sem["qS"], 16)
    ctx.body("sync", f_sync0)

    ld1 = []
    cur = ctx.cnt["qS"]
    for i in range(NT):
        cur += 16 * (NKQ + NKK)
        ld1.append(cur)
    ctx.cnt["qS"] = cur

    NVG = NKQ if const_mode else (NKQ + NKK + NKQ)
    NTG = 1 if const_mode else 3
    NSG = 1 if const_mode else 3

    def f_sync1(e):
        e.wait_ge(ctx.sem["qS"], ind_ld)
        for i in range(NT):
            if i >= 1:
                e.wait_ge(ctx.sem["qS"], ld1[i - 1])
                e.wait_ge(ctx.sem["eV"], eV0 + NVG * i)
            for c in range(NKQ):
                e.dma_start(out=qt[c][:], in_=qb[c * 128:(c + 1) * 128, i * TC:(i + 1) * TC]).then_inc(ctx.sem["qS"], 16)
            for c in range(NKK):
                e.dma_start(out=kt[c][:], in_=kb[c * 128:(c + 1) * 128, i * TC:(i + 1) * TC]).then_inc(ctx.sem["qS"], 16)
    ctx.body("sync", f_sync1)

    if gqa:
        qG0g = ctx.cnt["qG"]
        krld = [qG0g + 16 * 4 * NKQ * (i + 1) for i in range(NT)]
        ctx.cnt["qG"] = qG0g + 16 * 4 * NKQ * NT

        def f_gp_kr(e):
            w = WT(ctx, e)
            w("qG", qG0g)
            for i in range(NT):
                if i >= 1:
                    w("qG", krld[i - 1])
                    w("eV", eV0 + NVG * i)
                for c in range(NKQ):
                    kv = c // 2
                    ch = kv // 2
                    base = ch * 128 + (0 if kv % 2 == 0 else 32)
                    for (dst0, src0) in ((0, base), (32, base), (64, base + 64), (96, base + 64)):
                        e.dma_start(out=krepb[c][dst0:dst0 + 32, :],
                                    in_=kb[src0:src0 + 32, i * TC:(i + 1) * TC]).then_inc(ctx.sem["qG"], 16)
        ctx.body("gpsimd", f_gp_kr)

    def f_vec1(e):
        for i in range(NT):
            e.wait_ge(ctx.sem["qS"], ld1[i])
            if gqa:
                e.wait_ge(ctx.sem["qG"], krld[i])
            if i >= 1:
                e.wait_ge(ctx.sem["eT"], eT0 + NTG * i)
            if not const_mode:
                for c in range(NKQ):
                    e.tensor_mul(prod[c][:], qt[c][:], qt[c][:]).then_inc(ctx.sem["eV"], 1)
                e.wait_ge(ctx.sem["eT"], eT0 + NTG * i + 1)
                for c in range(NKK):
                    e.tensor_mul(prod[c][:], kt[c][:], kt[c][:]).then_inc(ctx.sem["eV"], 1)
                e.wait_ge(ctx.sem["eT"], eT0 + NTG * i + 2)
            for c in range(NKQ):
                e.tensor_mul(prod[c][:], qt[c][:],
                             (krepb[c] if gqa else kt[c])[:]).then_inc(ctx.sem["eV"], 1)
    ctx.body("vector", f_vec1)
    ctx.cnt["eV"] = eV0 + NVG * NT

    def f_tensor1(e):
        e.wait_ge(ctx.sem["qS"], ind_ld)
        for i in range(NT):
            bv = eV0 + NVG * i
            if i >= 1:
                e.wait_ge(ctx.sem["eS"], eS0 + NSG * i)
            ins = None
            if not const_mode:
                e.wait_ge(ctx.sem["eV"], bv + NKQ)
                for c in range(NKQ):
                    ins = e.matmul(p_qn[:], iq[:, c * H:(c + 1) * H], prod[c][:], start=(c == 0),
                                   stop=(c == NKQ - 1), skip_group_check=True)
                ins.then_inc(ctx.sem["eT"], 1)
                e.wait_ge(ctx.sem["eV"], bv + NKQ + NKK)
                for c in range(NKK):
                    ins = e.matmul(p_kn[:], ik[:, c * nh_k:(c + 1) * nh_k], prod[c][:], start=(c == 0),
                                   stop=(c == NKK - 1), skip_group_check=True)
                ins.then_inc(ctx.sem["eT"], 1)
            e.wait_ge(ctx.sem["eV"], bv + NVG)
            for c in range(NKQ):
                ins = e.matmul(p_dg[:], iq[:, c * H:(c + 1) * H], prod[c][:], start=(c == 0),
                               stop=(c == NKQ - 1), skip_group_check=True)
            ins.then_inc(ctx.sem["eT"], 1)
    ctx.body("tensor", f_tensor1)
    ctx.cnt["eT"] = eT0 + NTG * NT

    def f_scalar1(e):
        for i in range(NT):
            if not const_mode:
                e.wait_ge(ctx.sem["eT"], eT0 + NTG * i + 2)
                e.activation(out=qnR[:, i * TC:(i + 1) * TC], in_=p_qn[:], func=AF.Sqrt).then_inc(ctx.sem["eS"], 1)
                e.activation(out=knR[:, i * TC:(i + 1) * TC], in_=p_kn[:], func=AF.Sqrt).then_inc(ctx.sem["eS"], 1)
            e.wait_ge(ctx.sem["eT"], eT0 + NTG * (i + 1))
            e.copy(dgR[:, i * TC:(i + 1) * TC], p_dg[:]).then_inc(ctx.sem["eS"], 1)
    ctx.body("scalar", f_scalar1)
    ctx.cnt["eS"] = eS0 + NSG * NT

    b2 = dict(ctx.cnt)
    eV20, eT20, eS20 = b2["eV"], b2["eT"], b2["eS"]
    st0 = b2["qG"]
    qA0 = b2["qA"]

    if const_mode:
        if gqa:
            gcolq = ctx.sb((H, 2), F32, "gcq")
            ld_a = b2["qA"] + 16 * 17
            ctx.cnt["qA"] = ld_a

            def f_sc2(e):
                e.wait_ge(ctx.sem["qA"], b2["qA"])
                e.dma_start(out=gcolq[:, 0], in_=ctx.dram["F_qg"][:]).then_inc(ctx.sem["qA"], 16)
                for h in range(H):
                    e.dma_start(out=gcolq[h:h + 1, 1],
                                in_=ctx.dram["F_kg"][h // 4:h // 4 + 1]).then_inc(ctx.sem["qA"], 16)
            ctx.body("scalar", f_sc2)

            def f_vec2a(e):
                e.wait_ge(ctx.sem["qA"], ld_a)
                e.tensor_tensor(acol[:, 0:1], gcolq[:, 0:1], gcolq[:, 1:2],
                                ALU.mult).then_inc(ctx.sem["eV"], 1)
            ctx.body("vector", f_vec2a)
            ctx.cnt["eV"] = eV20 + 1

            def f_sc2b(e):
                e.wait_ge(ctx.sem["eV"], eV20 + 1)
                e.activation(out=acol[:, 0:1], in_=acol[:, 0:1], func=AF.Abs,
                             scale=a_coef * 8.0).then_inc(ctx.sem["eS"], 1)
            ctx.body("scalar", f_sc2b)
            ctx.cnt["eS"] = eS20 + 1
            eVb2 = ctx.cnt["eV"]
        else:
            qs_name, li, a_scale = a_const
            qs_ap = ctx.dram["S_qs"][li, :]
            ks_ap = ctx.dram["S_ks"][li, :]

            def f_sc2(e):
                e.wait_ge(ctx.sem["qA"], b2["qA"])
                e.dma_start(out=arow[:, 0:128], in_=qs_ap).then_inc(ctx.sem["qA"], 16)
                e.dma_start(out=arow[:, 128:256], in_=ks_ap).then_inc(ctx.sem["qA"], 16)
            ctx.body("scalar", f_sc2)
            ld_a = b2["qA"] + 32
            ctx.cnt["qA"] = ld_a

            def f_gp2a(e):
                e.memset(onesr[:], 1.0).then_inc(ctx.sem["eG"], 1)
            ctx.body("gpsimd", f_gp2a)
            ctx.cnt["eG"] = ctx.cnt["eG"] + 1
            eG2a = ctx.cnt["eG"]

            def f_vec2a(e):
                e.wait_ge(ctx.sem["qA"], ld_a)
                e.reduce_max(arow[:, 256:257], arow[:, 0:128], axis=AX.X,
                             apply_absolute_value=True).then_inc(ctx.sem["eV"], 1)
                e.reduce_max(arow[:, 257:258], arow[:, 128:256], axis=AX.X,
                             apply_absolute_value=True).then_inc(ctx.sem["eV"], 1)
                e.wait_ge(ctx.sem["eV"], eV20 + 2)
                e.tensor_tensor(arow[:, 258:259], arow[:, 256:257], arow[:, 257:258],
                                ALU.mult).then_inc(ctx.sem["eV"], 1)
                e.wait_ge(ctx.sem["eV"], eV20 + 3)
                e.tensor_scalar_mul(arow[:, 258:259], arow[:, 258:259],
                                    a_coef * a_scale).then_inc(ctx.sem["eV"], 1)
            ctx.body("vector", f_vec2a)
            ctx.cnt["eV"] = eV20 + 4

            def f_t2(e):
                e.wait_ge(ctx.sem["eV"], eV20 + 4)
                e.wait_ge(ctx.sem["eG"], eG2a)
                e.matmul(p_a[:], onesr[:], arow[:, 258:259], start=True, stop=True,
                         skip_group_check=True).then_inc(ctx.sem["eT"], 1)
            ctx.body("tensor", f_t2)
            ctx.cnt["eT"] = eT20 + 1

            def f_sc3(e):
                e.wait_ge(ctx.sem["eT"], eT20 + 1)
                e.copy(acol[:], p_a[:]).then_inc(ctx.sem["eS"], 1)
            ctx.body("scalar", f_sc3)
            ctx.cnt["eS"] = eS20 + 1
        eVb2 = ctx.cnt["eV"]

        def f_vec2(e):
            w = WT(ctx, e)
            w("eS", eS20 + 1)
            for i in range(NT):
                bv = eVb2 + 2 * i
                w("eV", bv)
                if i >= 2:
                    w("qG", st0 + 16 * (i - 1))
                e.tensor_scalar_max(cw[:], dgR[:, i * TC:(i + 1) * TC],
                                    acol[:, 0:1]).then_inc(ctx.sem["eV"], 1)
                w("eV", bv + 1)
                e.tensor_scalar_mul(cb[i % 2][:], cw[:], -1.0).then_inc(ctx.sem["eV"], 1)
        ctx.body("vector", f_vec2)
        ctx.cnt["eV"] = eVb2 + 2 * NT
        ev_done = lambda i: eVb2 + 2 * (i + 1)
    else:
        nwin = (T // window) if window else 1
        npc = (TC // window) if window else 1
        NV2 = npc + 1 + n_var
        if variants:
            def f_sc_rel(e):
                e.wait_ge(ctx.sem["qA"], qA0)
                e.dma_start(out=relc[:], in_=ctx.dram["relT"][:, :]).then_inc(ctx.sem["qA"], 16)
            ctx.body("scalar", f_sc_rel)
            ctx.cnt["qA"] = qA0 + 16
            rel_tgt = ctx.cnt["qA"]

        def f_vec2(e):
            w = WT(ctx, e)
            w("eS", eS0 + NSG * NT)
            if variants:
                w("qA", rel_tgt)
            for wi in range(nwin):
                j0 = max(0, (wi - 1) * window) if window else 0
                j1 = (wi + 1) * window if window else T
                e.reduce_max(kmax[:, wi:wi + 1], knR[:, j0:j1], axis=AX.X).then_inc(ctx.sem["eV"], 1)
            for i in range(NT):
                bv = eV20 + nwin + NV2 * i
                w("eV", bv)
                if i >= 2:
                    w("qG", st0 + 16 * n_var * (i - 1))
                for wj in range(npc):
                    wi = i * npc + wj
                    e.tensor_scalar(cw[:, wj * (window or TC):(wj + 1) * (window or TC)],
                                    qnR[:, i * TC + wj * (window or TC): i * TC + (wj + 1) * (window or TC)],
                                    kmax[:, wi:wi + 1] if window else kmax[:, 0:1],
                                    a_coef, ALU.mult, ALU.mult).then_inc(ctx.sem["eV"], 1)
                w("eV", bv + npc)
                e.tensor_tensor(cw[:], cw[:], dgR[:, i * TC:(i + 1) * TC],
                                ALU.max).then_inc(ctx.sem["eV"], 1)
                w("eV", bv + npc + 1)
                e.tensor_scalar_mul(cb[n_var * (i % 2)][:], cw[:], -1.0).then_inc(ctx.sem["eV"], 1)
                if variants:
                    for vi, (vname, rcol) in enumerate(variants):
                        w("eV", bv + npc + 2 + vi)
                        e.tensor_scalar(cb[n_var * (i % 2) + 1 + vi][:], cw[:], -1.0,
                                        relc[:, rcol:rcol + 1], ALU.mult,
                                        ALU.add).then_inc(ctx.sem["eV"], 1)
        ctx.body("vector", f_vec2)
        ctx.cnt["eV"] = eV20 + nwin + NV2 * NT
        ev_done = lambda i: eV20 + nwin + NV2 * (i + 1)

    def f_gp2(e):
        for i in range(NT):
            e.wait_ge(ctx.sem["eV"], ev_done(i))
            if i >= 1:
                e.wait_ge(ctx.sem["qG"], st0 + 16 * n_var * i)
            e.dma_start(out=ctx.dram[cb_out][:, i * TC:(i + 1) * TC],
                        in_=cb[n_var * (i % 2)][:]).then_inc(ctx.sem["qG"], 16)
            if variants:
                for vi, (vname, rcol) in enumerate(variants):
                    e.dma_start(out=ctx.dram[vname][:, i * TC:(i + 1) * TC],
                                in_=cb[n_var * (i % 2) + 1 + vi][:]).then_inc(ctx.sem["qG"], 16)
    ctx.body("gpsimd", f_gp2)
    ctx.cnt["qG"] = st0 + 16 * n_var * NT


def stage_attention(ctx, qname, kname, vname, cbname, outname, *, mode,
                    perm_qk=False, nkv=H, heads_per_kv=1, vcols=None,
                    bias=None):
    qb, kb, vt = ctx.dram[qname], ctx.dram[kname], ctx.dram[vname]
    outd = ctx.dram[outname]
    NKT = T // 128
    if vcols is None:
        vcols = lambda kv: kv * 64
    QW = 256 if mode == "local" else TC
    NQC = T // QW
    n_var = 3 if bias else 1
    cbrs = [ctx.dram[cbname]] + ([ctx.dram[bias[1]], ctx.dram[bias[2]]] if bias else [])
    biasTd = ctx.dram[bias[0]] if bias else None

    def ktl_of(qc):
        if mode == "causal":
            return list(range(0, (qc + 1) * QW // 128))
        if mode == "full":
            return list(range(NKT))
        return list(range(max(0, 2 * qc - 2), 2 * qc + 2))

    def mask_of(qc, kt):
        if mode == "causal":
            d0 = kt * 128 - qc * QW
            return d0 // 128 if d0 >= 0 else None
        if mode == "local":
            return kt - (2 * qc - 2)
        return None

    def var_of(qc, kt):
        if not bias:
            return 0, None
        d0 = kt * 128 - qc * TC
        if d0 <= -256:
            return 1, None
        if d0 >= 640:
            return 2, None
        return 0, (d0 + 128) // 128

    mask_name = "lmask" if mode == "local" else "cmask"
    has_mask = mode in ("causal", "local")

    Kaug = ctx.sb((65, T), BF16, "Kaug")
    Vaug = [ctx.sb((128, 65), BF16, "Va") for _ in range(NKT)]
    Qaug = [ctx.sb((65, QW), BF16, "Qa") for _ in range(2 * n_var)]
    pbuf = [ctx.sb((128, QW), BF16, "pb") for _ in range(3)]
    masks = ctx.sb((128, 4 * QW), BF16, "msk") if has_mask else None
    bbuf = [ctx.sb((128, QW), F32, "bb") for _ in range(2)] if bias else None
    rrow = ctx.sb((65, QW), F32, "rr")
    aux1 = ctx.sb((65, 64), F32, "ax")
    of32 = ctx.sb((64, QW), F32, "of")
    obuf = [ctx.sb((64, QW), BF16, "oo") for _ in range(2)]
    ps_s = [ctx.psum((128, QW), F32, "ss") for _ in range(2)]
    ps_o = ctx.psum((65, QW), F32, "so")
    ps_b = ctx.psum((64, QW), F32, "sbb")

    def qrows(h):
        if not perm_qk:
            return [(h * 64, 64)]
        c = h // 2
        if h % 2 == 0:
            return [(c * 128, 32), (c * 128 + 64, 32)]
        return [(c * 128 + 32, 32), (c * 128 + 96, 32)]

    def krows(kv):
        if not perm_qk:
            return [(kv * 64, 64)]
        c = kv // 2
        if kv % 2 == 0:
            return [(c * 128, 32), (c * 128 + 64, 32)]
        return [(c * 128 + 32, 32), (c * 128 + 96, 32)]

    heads = [(kv, kv * heads_per_kv + j) for kv in range(nkv) for j in range(heads_per_kv)]
    hq_order = [(kv, h, qc) for kv, h in heads for qc in range(NQC)]

    b = dict(ctx.cnt)
    eT0, eS0, eV0, eG0 = b["eT"], b["eS"], b["eV"], b["eG"]
    qS0, qA0, qG0 = b["qS"], b["qA"], b["qG"]

    qload_tgt = {}
    cq = qS0
    for kv, h in heads:
        cq += 16 * (len(krows(kv)) + NKT)
        for qc in range(NQC):
            cq += 16 * n_var * (len(qrows(h)) + 1)
            qload_tgt[(h, qc)] = cq
    ctx.cnt["qS"] = cq

    t_scores, t_av, t_bcast = {}, {}, {}
    s_exp, s_copy = {}, {}
    v_recip, v_mul, v_badd = {}, {}, {}
    g_mask = {}
    a_bload = {}
    n_memset = 2 + NKT
    cT, cS, cV, cG = eT0, eS0, eV0, eG0 + n_memset
    cA = qA0 + (16 * 4 if has_mask else 0)
    mask_tgt = cA
    n_band = 0
    for gi, (kv, h, qc) in enumerate(hq_order):
        for kt in ktl_of(qc):
            vi, bi = var_of(qc, kt)
            cT += 1
            t_scores[(h, qc, kt)] = cT
            if bi is not None:
                cA += 16
                a_bload[(h, qc, kt)] = cA
                n_band += 1
                cV += 1
                v_badd[(h, qc, kt)] = cV
            cS += 1
            s_exp[(h, qc, kt)] = cS
            if mask_of(qc, kt) is not None:
                cG += 1
                g_mask[(h, qc, kt)] = cG
            cT += 1
            t_av[(h, qc, kt)] = cT
        cV += 1
        v_recip[(h, qc)] = cV
        cT += 1
        t_bcast[(h, qc)] = cT
        cS += 1
        s_copy[(h, qc)] = cS
        cV += 1
        v_mul[(h, qc)] = cV
    ctx.cnt["eT"] = cT
    ctx.cnt["eS"] = cS
    ctx.cnt["eV"] = cV
    ctx.cnt["eG"] = cG
    ctx.cnt["qA"] = cA
    n_hq = len(hq_order)
    ctx.cnt["qG"] = qG0 + 16 * n_hq
    band_list = sorted(a_bload.keys(), key=lambda k: a_bload[k])

    def f_sync(e):
        w = WT(ctx, e)
        gi = 0
        for kv, h in heads:
            if gi > 0:
                e.wait_ge(ctx.sem["qS"], qload_tgt[(hq_order[gi - 1][1], hq_order[gi - 1][2])])
                w("eV", v_mul[hq_order[gi - 1][1], hq_order[gi - 1][2]])
            r0 = 0
            for (rs, n) in krows(kv):
                e.dma_start(out=Kaug[r0:r0 + n, :], in_=kb[rs:rs + n, :]).then_inc(ctx.sem["qS"], 16)
                r0 += n
            for kt in range(NKT):
                e.dma_start(out=Vaug[kt][:, 0:64],
                            in_=vt[kt * 128:(kt + 1) * 128, vcols(kv):vcols(kv) + 64]).then_inc(ctx.sem["qS"], 16)
            for qc in range(NQC):
                if gi >= 1:
                    e.wait_ge(ctx.sem["qS"], qload_tgt[(hq_order[gi - 1][1], hq_order[gi - 1][2])])
                if gi >= 2:
                    w("eV", v_mul[hq_order[gi - 2][1], hq_order[gi - 2][2]])
                for vi in range(n_var):
                    qa = Qaug[n_var * (gi % 2) + vi]
                    r0 = 0
                    for (rs, n) in qrows(h):
                        e.dma_start(out=qa[r0:r0 + n, :],
                                    in_=qb[rs:rs + n, qc * QW:(qc + 1) * QW]).then_inc(ctx.sem["qS"], 16)
                        r0 += n
                    e.dma_start(out=qa[64:65, :],
                                in_=cbrs[vi][h:h + 1, qc * QW:(qc + 1) * QW]).then_inc(ctx.sem["qS"], 16)
                gi += 1
    ctx.body("sync", f_sync)

    def f_gp(e):
        e.memset(aux1[64:65, :], 1.0).then_inc(ctx.sem["eG"], 1)
        e.memset(Kaug[64:65, :], 1.0).then_inc(ctx.sem["eG"], 1)
        for kt in range(NKT):
            e.memset(Vaug[kt][:, 64:65], 1.0).then_inc(ctx.sem["eG"], 1)
        w = WT(ctx, e)
        if has_mask:
            w("qA", mask_tgt)
        for gi, (kv, h, qc) in enumerate(hq_order):
            ktl = ktl_of(qc)
            for idx, kt in enumerate(ktl):
                key = (h, qc, kt)
                if key in g_mask:
                    w("eS", s_exp[key])
                    mi = mask_of(qc, kt)
                    e.tensor_mul(pbuf[idx % 3][:], pbuf[idx % 3][:],
                                 masks[:, mi * QW:(mi + 1) * QW]).then_inc(ctx.sem["eG"], 1)
            w("eV", v_mul[(h, qc)])
            if gi >= 1:
                w("qG", qG0 + 16 * gi)
            e.dma_start(out=outd[h * 64:(h + 1) * 64, qc * QW:(qc + 1) * QW],
                        in_=obuf[gi % 2][:]).then_inc(ctx.sem["qG"], 16)
    ctx.body("gpsimd", f_gp)

    if has_mask:
        def f_sc_ld(e):
            w = WT(ctx, e)
            w("qA", qA0)
            for j in range(4):
                e.dma_start(out=masks[:, j * QW:(j + 1) * QW],
                            in_=ctx.dram[mask_name][j, :, :]).then_inc(ctx.sem["qA"], 16)
        ctx.body("scalar", f_sc_ld)

    def f_tensor(e):
        w = WT(ctx, e)
        w("eG", eG0 + n_memset)
        for gi, (kv, h, qc) in enumerate(hq_order):
            ktl = ktl_of(qc)
            w("qS", qload_tgt[(h, qc)])
            if gi >= 1:
                ph, pqc = hq_order[gi - 1][1], hq_order[gi - 1][2]
                w("eS", s_copy[(ph, pqc)])
                w("eV", v_recip[(ph, pqc)])
            for idx, kt in enumerate(ktl):
                vi, bi = var_of(qc, kt)
                qa = Qaug[n_var * (gi % 2) + vi]
                if idx >= 2:
                    w("eS", s_exp[(h, qc, ktl[idx - 2])])
                e.matmul(ps_s[idx % 2][:], Kaug[:, kt * 128:(kt + 1) * 128], qa[:],
                         start=True, stop=True, skip_group_check=True).then_inc(ctx.sem["eT"], 1)
                key = (h, qc, kt)
                if key in g_mask:
                    w("eG", g_mask[key])
                else:
                    w("eS", s_exp[key])
                e.matmul(ps_o[:], Vaug[kt][:], pbuf[idx % 3][:],
                         start=(idx == 0), stop=(idx == len(ktl) - 1),
                         skip_group_check=True).then_inc(ctx.sem["eT"], 1)
            w("eV", v_recip[(h, qc)])
            e.matmul(ps_b[:], aux1[64:65, 0:64], rrow[64:65, :], start=True,
                     stop=True, skip_group_check=True).then_inc(ctx.sem["eT"], 1)
    ctx.body("tensor", f_tensor)

    def f_scalar(e):
        w = WT(ctx, e)
        if not has_mask:
            w("qA", qA0)
        bj_c = [0]

        def maybe_bload(key):
            if key in a_bload:
                bj = bj_c[0]
                if bj >= 1:
                    w("qA", a_bload[band_list[bj - 1]])
                if bj >= 2:
                    w("eV", v_badd[band_list[bj - 2]])
                h2, qc2, kt2 = key
                _, bi2 = var_of(qc2, kt2)
                e.dma_start(out=bbuf[bj % 2][:],
                            in_=biasTd[h2, bi2, :, :]).then_inc(ctx.sem["qA"], 16)
                bj_c[0] += 1

        for gi, (kv, h, qc) in enumerate(hq_order):
            ktl = ktl_of(qc)
            for idx, kt in enumerate(ktl):
                key = (h, qc, kt)
                if bias:
                    if idx == 0:
                        maybe_bload(key)
                    if idx + 1 < len(ktl):
                        maybe_bload((h, qc, ktl[idx + 1]))
                if key in v_badd:
                    w("eV", v_badd[key])
                else:
                    w("eT", t_scores[key])
                if idx >= 3:
                    pk = (h, qc, ktl[idx - 3])
                    w("eT", t_av[pk])
                    if pk in g_mask:
                        w("eG", g_mask[pk])
                elif gi >= 1:
                    ph, pqc = hq_order[gi - 1][1], hq_order[gi - 1][2]
                    w("eT", t_av[(ph, pqc, ktl_of(pqc)[-1])])
                e.activation(out=pbuf[idx % 3][:], in_=ps_s[idx % 2][:],
                             func=AF.Exp).then_inc(ctx.sem["eS"], 1)
            w("eT", t_av[(h, qc, ktl[-1])])
            if gi >= 1:
                ph, pqc = hq_order[gi - 1][1], hq_order[gi - 1][2]
                w("eV", v_mul[(ph, pqc)])
            e.copy(of32[:], ps_o[0:64, :]).then_inc(ctx.sem["eS"], 1)
    ctx.body("scalar", f_scalar)

    def f_vec(e):
        w = WT(ctx, e)
        for gi, (kv, h, qc) in enumerate(hq_order):
            ktl = ktl_of(qc)
            if bias:
                for idx, kt in enumerate(ktl):
                    key = (h, qc, kt)
                    if key in v_badd:
                        w("eT", t_scores[key])
                        w("qA", a_bload[key])
                        bj = band_list.index(key)
                        e.tensor_add(ps_s[idx % 2][:], ps_s[idx % 2][:],
                                     bbuf[bj % 2][:]).then_inc(ctx.sem["eV"], 1)
            w("eT", t_av[(h, qc, ktl[-1])])
            e.reciprocal(rrow[64:65, :], ps_o[64:65, :]).then_inc(ctx.sem["eV"], 1)
            w("eT", t_bcast[(h, qc)])
            w("eS", s_copy[(h, qc)])
            if gi >= 2:
                w("qG", qG0 + 16 * (gi - 1))
            e.tensor_mul(obuf[gi % 2][:], of32[:], ps_b[:]).then_inc(ctx.sem["eV"], 1)
    ctx.body("vector", f_vec)


def stage_wo(ctx, w_name, b_name, rhs_name, out_name, out_dtype=F32):
    wsb, wrdy = load_weights_resident(ctx, ctx.dram[w_name], D, D, "wo")
    if b_name is not None:
        btile, brdy = load_bias_cols(ctx, ctx.dram[b_name], 8)
        wrdy = max(wrdy, brdy)
    outd = ctx.dram[out_name]

    def stores(i, mc):
        return [outd[mc * 128:(mc + 1) * 128, i * TC:(i + 1) * TC]]

    def epi(info):
        epi_act_store(ctx, info, stores,
                      bias=(lambda mc: btile[:, mc:mc + 1]) if b_name else None,
                      out_dtype=out_dtype)
    gemm_fm(ctx, wsb=wsb, w_ready=wrdy, NKC=NKD, MC=8,
            rhs_dram=ctx.dram[rhs_name], epi_emit=epi)


def stage_vtr(ctx, src_name, dst_name, rows=D):
    srcd = ctx.dram[src_name]
    dstd = ctx.dram[dst_name]
    NTT = T // 128
    NDC = rows // TC
    tb = [ctx.sb((128, TC), BF16, "vt") for _ in range(3)]
    b = dict(ctx.cnt)
    n_tiles = NTT * NDC
    ld = [b["qS"] + 16 * (j + 1) for j in range(n_tiles)]
    ctx.cnt["qS"] = b["qS"] + 16 * n_tiles
    st0 = b["qG"]
    ctx.cnt["qG"] = st0 + 16 * n_tiles

    def f_sync(e):
        j = 0
        for tt in range(NTT):
            for dc in range(NDC):
                if j >= 1:
                    e.wait_ge(ctx.sem["qS"], ld[j - 1])
                if j >= 3:
                    e.wait_ge(ctx.sem["qG"], st0 + 16 * (j - 2))
                e.dma_start(out=tb[j % 3][:],
                            in_=srcd[dc * TC:(dc + 1) * TC, tt * 128:(tt + 1) * 128],
                            transpose=True).then_inc(ctx.sem["qS"], 16)
                j += 1
    ctx.body("sync", f_sync)

    def f_gp(e):
        j = 0
        for tt in range(NTT):
            for dc in range(NDC):
                e.wait_ge(ctx.sem["qS"], ld[j])
                if j >= 1:
                    e.wait_ge(ctx.sem["qG"], st0 + 16 * j)
                e.dma_start(out=dstd[tt * 128:(tt + 1) * 128, dc * TC:(dc + 1) * TC],
                            in_=tb[j % 3][:]).then_inc(ctx.sem["qG"], 16)
                j += 1
    ctx.body("gpsimd", f_gp)


def stage_subqkv(ctx, i):
    wsb, wrdy = load_weights_resident(ctx, ctx.dram["S_wqkv"][i], D, 3072, "wsq")
    sqr, skr, svf = ctx.dram["sqr"], ctx.dram["skr"], ctx.dram["svf"]

    def stores(ti, mc):
        dst = sqr if mc < 8 else (skr if mc < 16 else svf)
        r = (mc % 8) * 128
        return [dst[r:r + 128, ti * TC:(ti + 1) * TC]]

    def epi(info):
        epi_act_store(ctx, info, stores, out_dtype=BF16)
    gemm_fm(ctx, wsb=wsb, w_ready=wrdy, NKC=NKD, MC=24,
            rhs_dram=ctx.dram[f"xs{2 * i}b"], epi_emit=epi, rhs_shift=True)


def stage_rope(ctx, i, kind):
    assert kind == "xpos"
    qin, kin = ctx.dram["sqr"], ctx.dram["skr"]
    qout, kout = ctx.dram["sq"], ctx.dram["sk"]
    indPT = ctx.dram["indPT"]
    indP = ctx.dram["indP"]

    tbl = {}
    ttmp = ctx.sb((128, T), F32, "ttmp")
    for nm in ("cq", "sq", "ck", "sk"):
        tbl[nm] = ctx.sb((128, T), BF16, "tb" + nm)
    scl = ctx.sb((128, 2), F32, "scl")
    scl_sw = ctx.sb((128, 2), F32, "sclsw")
    ipt = ctx.sb((16, NKD * 128), BF16, "ipt")
    ipq = ctx.sb((128, NKD * H), BF16, "ipq")
    swp = ctx.sb((128, 128), BF16, "swp")
    qt = [ctx.sb((128, TC), BF16, "qt") for _ in range(NKD)]
    kt = [ctx.sb((128, TC), BF16, "kt") for _ in range(NKD)]
    sqb = [ctx.sb((128, TC), BF16, "sqb") for _ in range(NKD)]
    rows = ctx.sb((H, 2 * TC), F32, "rws")
    rqb = ctx.sb((H, TC), BF16, "rqb")
    rkb = ctx.sb((H, TC), BF16, "rkb")
    t1b = [ctx.sb((128, TC), BF16, "t1") for _ in range(2)]
    u1b = [ctx.sb((128, TC), BF16, "u1") for _ in range(2)]
    u2b = [ctx.sb((128, TC), BF16, "u2") for _ in range(2)]
    ob = [ctx.sb((128, TC), BF16, "rob") for _ in range(3)]
    p_qn = ctx.psum((H, TC), F32, "pqn")
    p_kn = ctx.psum((H, TC), F32, "pkn")
    p_rb = [ctx.psum((128, TC), F32, "prb") for _ in range(2)]
    p_sw = [ctx.psum((128, TC), F32, "psw") for _ in range(2)]

    b = dict(ctx.cnt)
    eT0, eS0, eV0, eG0 = b["eT"], b["eS"], b["eV"], b["eG"]
    qS0, qG0, qA0 = b["qS"], b["qG"], b["qA"]

    n_pro = 4 + 2 + NKD + NKD + 1
    pro_tgt = qA0 + 16 * n_pro
    ctx.cnt["qA"] = pro_tgt

    def f_sc_pro(e):
        e.wait_ge(ctx.sem["qA"], qA0)
        e.dma_start(out=scl[:, 0], in_=ctx.dram["S_qs"][i, :]).then_inc(ctx.sem["qA"], 16)
        e.dma_start(out=scl[:, 1], in_=ctx.dram["S_ks"][i, :]).then_inc(ctx.sem["qA"], 16)
        e.dma_start(out=scl_sw[0:64, 0], in_=ctx.dram["S_qs"][i, 64:128]).then_inc(ctx.sem["qA"], 16)
        e.dma_start(out=scl_sw[64:128, 0], in_=ctx.dram["S_qs"][i, 0:64]).then_inc(ctx.sem["qA"], 16)
        e.dma_start(out=scl_sw[0:64, 1], in_=ctx.dram["S_ks"][i, 64:128]).then_inc(ctx.sem["qA"], 16)
        e.dma_start(out=scl_sw[64:128, 1], in_=ctx.dram["S_ks"][i, 0:64]).then_inc(ctx.sem["qA"], 16)
        for c in range(NKD):
            e.dma_start(out=ipt[:, c * 128:(c + 1) * 128], in_=indPT[c, :, :]).then_inc(ctx.sem["qA"], 16)
        for c in range(NKD):
            e.dma_start(out=ipq[:, c * H:(c + 1) * H], in_=indP[c, :, :]).then_inc(ctx.sem["qA"], 16)
        e.dma_start(out=swp[:], in_=ctx.dram["swap64"][:, :]).then_inc(ctx.sem["qA"], 16)
        names = {"cq": "xpos_cq", "sq": "xpos_sq", "ck": "xpos_ck", "sk": "xpos_sk"}
        for j, nm in enumerate(("cq", "sq", "ck", "sk")):
            if j >= 1:
                e.wait_ge(ctx.sem["eV"], eV0 + j)
                e.wait_ge(ctx.sem["qA"], qA0 + 16 * (23 + j))
            e.dma_start(out=ttmp[:], in_=ctx.dram[names[nm]][:, :]).then_inc(ctx.sem["qA"], 16)
    ctx.body("scalar", f_sc_pro)
    ctx.cnt["qA"] = qA0 + 16 * (23 + 4)
    pro_tgt = ctx.cnt["qA"]

    def f_v_pro(e):
        for j, nm in enumerate(("cq", "sq", "ck", "sk")):
            e.wait_ge(ctx.sem["qA"], qA0 + 16 * (23 + j + 1))
            col = 0 if nm in ("cq", "sq") else 1
            stile = scl if nm in ("cq", "ck") else scl_sw
            e.tensor_scalar_mul(tbl[nm][:], ttmp[:], stile[:, col:col + 1]).then_inc(ctx.sem["eV"], 1)
    ctx.body("vector", f_v_pro)
    ctx.cnt["eV"] = eV0 + 4
    eVp = ctx.cnt["eV"]

    ld1 = []
    cur = ctx.cnt["qS"]
    for ti in range(NT):
        cur += 16 * (2 * NKD)
        ld1.append(cur)
    ctx.cnt["qS"] = cur

    NVsq = 2 * NKD
    NVrow = 6
    NVtile = 4 * 2 * NKD
    NV = NVsq + NVrow + NVtile
    NSt = 2
    NTt = 2 + 2 * 2 * NKD
    eVb = lambda ti: eVp + NV * ti
    eTb = lambda ti: eT0 + NTt * ti
    eSb = lambda ti: eS0 + NSt * ti
    st0 = qG0
    ctx.cnt["eV"] = eVp + NV * NT
    ctx.cnt["eS"] = eS0 + NSt * NT
    ctx.cnt["eT"] = eT0 + NTt * NT
    ctx.cnt["qG"] = st0 + 16 * 2 * NKD * NT

    def f_sync(e):
        e.wait_ge(ctx.sem["qS"], qS0)
        for ti in range(NT):
            if ti >= 1:
                e.wait_ge(ctx.sem["qS"], ld1[ti - 1])
                e.wait_ge(ctx.sem["eV"], eVb(ti))
            for c in range(NKD):
                e.dma_start(out=qt[c][:], in_=qin[c * 128:(c + 1) * 128, ti * TC:(ti + 1) * TC]).then_inc(ctx.sem["qS"], 16)
            for c in range(NKD):
                e.dma_start(out=kt[c][:], in_=kin[c * 128:(c + 1) * 128, ti * TC:(ti + 1) * TC]).then_inc(ctx.sem["qS"], 16)
    ctx.body("sync", f_sync)

    def f_vec(e):
        w = WT(ctx, e)
        for ti in range(NT):
            bv = eVb(ti)
            w("qS", ld1[ti])
            if ti >= 1:
                w("eT", eTb(ti))
            for c in range(NKD):
                e.tensor_mul(sqb[c][:], qt[c][:], qt[c][:]).then_inc(ctx.sem["eV"], 1)
            w("eT", eTb(ti) + 1)
            for c in range(NKD):
                e.tensor_mul(sqb[c][:], kt[c][:], kt[c][:]).then_inc(ctx.sem["eV"], 1)
            w("eS", eSb(ti) + 2)
            e.tensor_scalar_max(rows[:, 0:TC], rows[:, 0:TC], 1e-12).then_inc(ctx.sem["eV"], 1)
            w("eV", bv + NVsq + 1)
            e.reciprocal(rows[:, 0:TC], rows[:, 0:TC]).then_inc(ctx.sem["eV"], 1)
            w("eV", bv + NVsq + 2)
            e.tensor_copy(rqb[:], rows[:, 0:TC]).then_inc(ctx.sem["eV"], 1)
            e.tensor_scalar_max(rows[:, TC:2 * TC], rows[:, TC:2 * TC], 1e-12).then_inc(ctx.sem["eV"], 1)
            w("eV", bv + NVsq + 4)
            e.reciprocal(rows[:, TC:2 * TC], rows[:, TC:2 * TC]).then_inc(ctx.sem["eV"], 1)
            w("eV", bv + NVsq + 5)
            e.tensor_copy(rkb[:], rows[:, TC:2 * TC]).then_inc(ctx.sem["eV"], 1)
            for j in range(2 * NKD):
                c = j % NKD
                is_q = j < NKD
                raw = qt[c] if is_q else kt[c]
                cs = tbl["cq"] if is_q else tbl["ck"]
                ss = tbl["sq"] if is_q else tbl["sk"]
                bt = bv + NVsq + NVrow + 4 * j
                tt = eTb(ti) + 2 + 2 * j
                w("eT", tt + 1)
                e.tensor_mul(t1b[j % 2][:], raw[:], p_rb[j % 2][:]).then_inc(ctx.sem["eV"], 1)
                w("eV", bt + 1)
                e.tensor_mul(u1b[j % 2][:], t1b[j % 2][:],
                             cs[:, ti * TC:(ti + 1) * TC]).then_inc(ctx.sem["eV"], 1)
                w("eT", tt + 2)
                e.tensor_mul(u2b[j % 2][:], p_sw[j % 2][:],
                             ss[:, ti * TC:(ti + 1) * TC]).then_inc(ctx.sem["eV"], 1)
                w("eV", bt + 3)
                jj = 2 * NKD * ti + j
                if jj >= 3:
                    w("qG", st0 + 16 * (jj - 2))
                e.tensor_add(ob[jj % 3][:], u1b[j % 2][:], u2b[j % 2][:]).then_inc(ctx.sem["eV"], 1)
    ctx.body("vector", f_vec)

    def f_tensor(e):
        w = WT(ctx, e)
        w("qA", pro_tgt)
        for ti in range(NT):
            bv = eVb(ti)
            w("qS", ld1[ti])
            w("eV", bv + NKD)
            if ti >= 1:
                w("eS", eSb(ti))
            ins = None
            for c in range(NKD):
                ins = e.matmul(p_qn[:], ipq[:, c * H:(c + 1) * H], sqb[c][:], start=(c == 0),
                               stop=(c == NKD - 1), skip_group_check=True)
            ins.then_inc(ctx.sem["eT"], 1)
            w("eV", bv + 2 * NKD)
            for c in range(NKD):
                ins = e.matmul(p_kn[:], ipq[:, c * H:(c + 1) * H], sqb[c][:], start=(c == 0),
                               stop=(c == NKD - 1), skip_group_check=True)
            ins.then_inc(ctx.sem["eT"], 1)
            for j in range(2 * NKD):
                c = j % NKD
                is_q = j < NKD
                raw = qt[c] if is_q else kt[c]
                rrow_b = rqb if is_q else rkb
                bt = bv + NVsq + NVrow + 4 * j
                w("eV", bv + NVsq + (3 if is_q else NVrow))
                if j >= 2:
                    w("eV", eVb(ti) + NVsq + NVrow + 4 * (j - 2) + 2)
                e.matmul(p_rb[j % 2][:], ipt[:, c * 128:(c + 1) * 128], rrow_b[:],
                         start=True, stop=True, skip_group_check=True).then_inc(ctx.sem["eT"], 1)
                w("eV", bt + 1)
                if j >= 2:
                    w("eV", eVb(ti) + NVsq + NVrow + 4 * (j - 2) + 4)
                e.matmul(p_sw[j % 2][:], swp[:], t1b[j % 2][:],
                         start=True, stop=True, skip_group_check=True).then_inc(ctx.sem["eT"], 1)
    ctx.body("tensor", f_tensor)

    def f_scalar(e):
        w = WT(ctx, e)
        for ti in range(NT):
            w("eT", eTb(ti) + 1)
            e.activation(out=rows[:, 0:TC], in_=p_qn[:], func=AF.Sqrt).then_inc(ctx.sem["eS"], 1)
            w("eT", eTb(ti) + 2)
            e.activation(out=rows[:, TC:2 * TC], in_=p_kn[:], func=AF.Sqrt).then_inc(ctx.sem["eS"], 1)
    ctx.body("scalar", f_scalar)

    def f_gp(e):
        for ti in range(NT):
            for j in range(2 * NKD):
                c = j % NKD
                dst = qout if j < NKD else kout
                e.wait_ge(ctx.sem["eV"], eVb(ti) + NVsq + NVrow + 4 * (j + 1))
                jj = 2 * NKD * ti + j
                if jj >= 1:
                    e.wait_ge(ctx.sem["qG"], st0 + 16 * jj)
                e.dma_start(out=dst[c * 128:(c + 1) * 128, ti * TC:(ti + 1) * TC],
                            in_=ob[jj % 3][:]).then_inc(ctx.sem["qG"], 16)
    ctx.body("gpsimd", f_gp)


def epi_resid_store(ctx, info, res_name, out_f32_name, out_bf16_name, bias_tile=None):
    MC, NB, psums = info["MC"], info["NB"], info["psums"]
    resd = ctx.dram[res_name]
    of = ctx.dram[out_f32_name]
    obf = ctx.dram[out_bf16_name] if out_bf16_name else None
    rbufs = [ctx.sb((128, TC), F32, "res") for _ in range(3)]
    xo = [ctx.sb((128, TC), F32, "eo") for _ in range(3)]
    xob = [ctx.sb((128, TC), BF16, "eob") for _ in range(3)]
    epi0, mm0 = info["epi0"], info["mm0"]
    b = dict(ctx.cnt)
    qA0, eV0_, st0 = b["qA"], b["eV"], b["qG"]
    n_iter = NT * MC
    n_dst = 2 if obf is not None else 1
    rld = [qA0 + 16 * (it + 1) for it in range(n_iter)]
    ctx.cnt["qA"] = qA0 + 16 * n_iter
    ctx.cnt["eV"] = eV0_ + (n_iter if obf is not None else 0)
    ctx.cnt["qG"] = st0 + 16 * n_dst * n_iter

    def f_sc(e):
        w = WT(ctx, e)
        w("qA", qA0)
        for it in range(n_iter):
            i, mc = it // MC, it % MC
            if it >= 1:
                w("qA", rld[it - 1])
            if it >= 3:
                if obf is not None:
                    w("eV", eV0_ + it - 2)
                else:
                    w("eS", epi0 + it - 2)
            e.dma_start(out=rbufs[it % 3][:],
                        in_=resd[mc * 128:(mc + 1) * 128, i * TC:(i + 1) * TC]).then_inc(ctx.sem["qA"], 16)
    ctx.body("scalar", f_sc)

    def f_vec(e):
        w = WT(ctx, e)
        for it in range(n_iter):
            i, mc = it // MC, it % MC
            w("eT", mm0 + it + 1)
            w("qA", rld[it])
            if it >= 3:
                w("qG", st0 + 16 * n_dst * (it - 2))
            if bias_tile is not None:
                ins = e.scalar_tensor_tensor(xo[it % 3][:], psums[it % NB][:],
                                             bias_tile[:, mc:mc + 1], rbufs[it % 3][:],
                                             ALU.add, ALU.add)
            else:
                ins = e.tensor_add(xo[it % 3][:], psums[it % NB][:], rbufs[it % 3][:])
            ins.then_inc(ctx.sem["eS"], 1)
            if obf is not None:
                w("eS", epi0 + it + 1)
                e.tensor_copy(xob[it % 3][:], xo[it % 3][:]).then_inc(ctx.sem["eV"], 1)
    ctx.body("vector", f_vec)

    def f_gp(e):
        w = WT(ctx, e)
        for it in range(n_iter):
            i, mc = it // MC, it % MC
            w("eS", epi0 + it + 1)
            if obf is not None:
                w("eV", eV0_ + it + 1)
            if it >= 1:
                w("qG", st0 + 16 * n_dst * it)
            r0, c0 = mc * 128, i * TC
            e.dma_start(out=of[r0:r0 + 128, c0:c0 + TC], in_=xo[it % 3][:]).then_inc(ctx.sem["qG"], 16)
            if obf is not None:
                e.dma_start(out=obf[r0:r0 + 128, c0:c0 + TC], in_=xob[it % 3][:]).then_inc(ctx.sem["qG"], 16)
    ctx.body("gpsimd", f_gp)


def stage_subwo(ctx, i):
    wsb, wrdy = load_weights_resident(ctx, ctx.dram["S_wo"][i], D, D, "wso")

    def epi(info):
        epi_resid_store(ctx, info, f"xs{2 * i}", f"xs{2 * i + 1}", f"xs{2 * i + 1}b")
    gemm_fm(ctx, wsb=wsb, w_ready=wrdy, NKC=NKD, MC=8,
            rhs_dram=ctx.dram["sap"], epi_emit=epi)


def stage_subf1(ctx, i):
    xsrc = ctx.dram[f"xs{2 * i + 1}"]
    gactd = ctx.dram["gact"]
    augrd = ctx.dram["augrd"]
    M1 = 5632
    NPAIR = 22
    wsb, wrdy = load_weights_resident(ctx, ctx.dram["S_w1"][i], D, M1, "w1r", nbuf=1)
    zcol = ctx.dram["zeros128"]

    gb = ctx.sb((128, 2 * NKD), BF16, "gb")
    gcol = ctx.sb((128, NKD), F32, "gcol")
    blncol = ctx.sb((128, NKD), F32, "blnc")
    augw = ctx.sb((2, M1), BF16, "augw")
    ones1 = ctx.sb((128, 1), F32, "on1")
    onesr = ctx.sb((1, 128), F32, "onr")
    xt = [ctx.sb((128, TC), F32, "x") for _ in range(NKD)]
    sq = [ctx.sb((128, TC), F32, "sq") for _ in range(NKD)]
    xg = [[ctx.sb((128, TC), BF16, "xg") for _ in range(NKD)] for _ in range(2)]
    rows = ctx.sb((1, 4 * TC), F32, "rw")
    rows2 = ctx.sb((1, 2 * TC), BF16, "rw2")
    augr = [ctx.sb((2, TC), BF16, "ar") for _ in range(2)]
    rstdb = ctx.sb((128, TC), F32, "rsb")
    tg = ctx.sb((128, TC), F32, "tg")
    gg = ctx.sb((128, TC), BF16, "gg")
    ta = ctx.sb((128, TC), F32, "ta")
    ob = [ctx.sb((128, TC), BF16, "go") for _ in range(3)]
    p_a = ctx.psum((128, TC), F32, "pa")
    p_g = ctx.psum((128, TC), F32, "pg")
    p_w = ctx.psum((2, TC), F32, "pw")
    p_s = ctx.psum((1, TC), F32, "ps")
    p_q = ctx.psum((1, TC), F32, "pq")
    p_r = ctx.psum((128, TC), F32, "pr")

    b = dict(ctx.cnt)
    eG0 = b["eG"]
    qS0, qA0 = b["qS"], b["qA"]
    eV0 = ctx.cnt["eV"]
    eS0 = ctx.cnt["eS"]

    def f_sc_pro(e):
        e.wait_ge(ctx.sem["qA"], qA0)
        for c in range(NKD):
            e.dma_start(out=gcol[:, c], in_=ctx.dram["S_g"][i, c * 128:(c + 1) * 128]).then_inc(ctx.sem["qA"], 16)
        for c in range(NKD):
            e.dma_start(out=blncol[:, c], in_=ctx.dram["S_b"][i, c * 128:(c + 1) * 128]).then_inc(ctx.sem["qA"], 16)
    ctx.body("scalar", f_sc_pro)
    ctx.cnt["qA"] = qA0 + 16 * 2 * NKD
    pro1 = ctx.cnt["qA"]

    def f_v_pro(e):
        e.wait_ge(ctx.sem["qA"], pro1)
        for c in range(NKD):
            e.tensor_copy(gb[:, 2 * c:2 * c + 1], gcol[:, c:c + 1]).then_inc(ctx.sem["eV"], 1)
            e.tensor_copy(gb[:, 2 * c + 1:2 * c + 2], blncol[:, c:c + 1]).then_inc(ctx.sem["eV"], 1)
    ctx.body("vector", f_v_pro)
    ctx.cnt["eV"] = eV0 + 2 * NKD
    eVp = ctx.cnt["eV"]

    def f_g_pro(e):
        e.memset(ones1[:], 1.0).then_inc(ctx.sem["eG"], 1)
        e.memset(onesr[:], 1.0).then_inc(ctx.sem["eG"], 1)
    ctx.body("gpsimd", f_g_pro)
    ctx.cnt["eG"] = eG0 + 2

    NCH = M1 // TC
    eTp0 = ctx.cnt["eT"]

    def f_t_pro(e):
        e.wait_ge(ctx.sem["eV"], eVp)
        for nch in range(NCH):
            if nch >= 1:
                e.wait_ge(ctx.sem["eS"], eS0 + nch)
            ins = None
            for kc in range(NKD):
                ins = e.matmul(p_w[:], gb[:, 2 * kc:2 * kc + 2],
                               wsb[:, kc * M1 + nch * TC: kc * M1 + (nch + 1) * TC],
                               start=(kc == 0), stop=(kc == NKD - 1), skip_group_check=True)
            ins.then_inc(ctx.sem["eT"], 1)
    ctx.body("tensor", f_t_pro)
    ctx.cnt["eT"] = eTp0 + NCH

    def f_s_pro(e):
        for nch in range(NCH):
            e.wait_ge(ctx.sem["eT"], eTp0 + nch + 1)
            e.activation(out=augw[:, nch * TC:(nch + 1) * TC], in_=p_w[:],
                         func=AF.Copy).then_inc(ctx.sem["eS"], 1)
    ctx.body("scalar", f_s_pro)
    ctx.cnt["eS"] = eS0 + NCH

    def f_g_pro2(e):
        e.wait_ge(ctx.sem["eS"], eS0 + NCH)
        e.dma_start(out=augw[1:2, :], in_=ctx.dram["S_b1"][i, :],
                    accum_op=ALU.add).then_inc(ctx.sem["qG"], 16)
    ctx.body("gpsimd", f_g_pro2)
    ctx.cnt["qG"] += 16
    aug_qg = ctx.cnt["qG"]

    b2 = dict(ctx.cnt)
    eT20, eS20, eV20 = b2["eT"], b2["eS"], b2["eV"]
    st0 = b2["qG"]
    ld1 = []
    cur = b2["qS"]
    for ti in range(NT):
        cur += 16 * (NKD + (NKD // 2 if ti == 0 else 0))
        ld1.append(cur)
    ctx.cnt["qS"] = cur
    NS2 = 10 + NPAIR
    NT2 = 2 + 2 * NPAIR
    NV2 = 8 + NKD + 3 * NPAIR
    ctx.cnt["qA"] = pro1 + 16 * 3 * NT
    ctx.cnt["eS"] = eS20 + NS2 * NT
    ctx.cnt["eT"] = eT20 + NT2 * NT
    ctx.cnt["eV"] = eV20 + NV2 * NT
    ctx.cnt["qG"] = st0 + 16 * NPAIR * NT

    def load_shift(e, ti, kc, dst):
        if kc >= NKD // 2:
            if ti == 0:
                e.dma_start(out=dst[:, 0:1], in_=zcol[:, :]).then_inc(ctx.sem["qS"], 16)
                e.dma_start(out=dst[:, 1:TC], in_=xsrc[kc * 128:(kc + 1) * 128, 0:TC - 1]).then_inc(ctx.sem["qS"], 16)
            else:
                e.dma_start(out=dst[:], in_=xsrc[kc * 128:(kc + 1) * 128, ti * TC - 1:(ti + 1) * TC - 1]).then_inc(ctx.sem["qS"], 16)
        else:
            e.dma_start(out=dst[:], in_=xsrc[kc * 128:(kc + 1) * 128, ti * TC:(ti + 1) * TC]).then_inc(ctx.sem["qS"], 16)

    def f_sync(e):
        e.wait_ge(ctx.sem["qS"], qS0)
        for ti in range(NT):
            if ti >= 1:
                e.wait_ge(ctx.sem["qS"], ld1[ti - 1])
                e.wait_ge(ctx.sem["eV"], eV20 + NV2 * (ti - 1) + 8 + NKD)
            for kc in range(NKD):
                load_shift(e, ti, kc, xt[kc])
    ctx.body("sync", f_sync)

    def f_scalar(e):
        w = WT(ctx, e)
        for ti in range(NT):
            bs = eS20 + NS2 * ti
            bv = eV20 + NV2 * ti
            bt = eT20 + NT2 * ti
            w("qS", ld1[ti])
            if ti >= 1:
                w("eT", eT20 + NT2 * ti)
            for kc in range(NKD):
                e.square(sq[kc][:], xt[kc][:]).then_inc(ctx.sem["eS"], 1)
            w("eV", bv + 5)
            e.activation(out=rows[:, 3 * TC:4 * TC], in_=rows[:, 2 * TC:3 * TC],
                         func=AF.Sqrt).then_inc(ctx.sem["eS"], 1)
            w("eT", bt + 2)
            if ti >= 1:
                w("eV", eV20 + NV2 * (ti - 1) + NV2)
            e.copy(rstdb[:], p_r[:]).then_inc(ctx.sem["eS"], 1)
            qa_b = pro1 + 16 * 3 * ti
            w("eV", bv + 8)
            if ti >= 2:
                w("eT", eT20 + NT2 * (ti - 1))
            e.dma_start(out=augrd[0:1, ti * TC:(ti + 1) * TC], in_=rows2[:, 0:TC]).then_inc(ctx.sem["qA"], 16)
            e.dma_start(out=augrd[1:2, ti * TC:(ti + 1) * TC], in_=rows2[:, TC:2 * TC]).then_inc(ctx.sem["qA"], 16)
            w("qA", qa_b + 32)
            e.dma_start(out=augr[ti % 2][:], in_=augrd[:, ti * TC:(ti + 1) * TC]).then_inc(ctx.sem["qA"], 16)
            for pj in range(NPAIR):
                pv = bv + 8 + NKD + 3 * pj
                w("eV", pv + 2)
                e.activation(out=gg[:], in_=tg[:], func=AF.Gelu).then_inc(ctx.sem["eS"], 1)
    ctx.body("scalar", f_scalar)

    def f_tensor(e):
        w = WT(ctx, e)
        w("eV", wrdy)
        w("qG", aug_qg)
        w("eG", eG0 + 2)
        for ti in range(NT):
            bt = eT20 + NT2 * ti
            bv = eV20 + NV2 * ti
            bs = eS20 + NS2 * ti
            w("qS", ld1[ti])
            w("eS", bs + 8)
            if ti >= 1:
                w("eV", eV20 + NV2 * (ti - 1) + 2)
            for kc in range(NKD):
                e.matmul(p_s[:], ones1[:], xt[kc][:], start=(kc == 0),
                         stop=(kc == NKD - 1), skip_group_check=True)
            ins = None
            for kc in range(NKD):
                ins = e.matmul(p_q[:], ones1[:], sq[kc][:], start=(kc == 0),
                               stop=(kc == NKD - 1), skip_group_check=True)
            ins.then_inc(ctx.sem["eT"], 1)
            w("eV", bv + 6)
            if ti >= 1:
                w("eS", eS20 + NS2 * (ti - 1) + 10)
            e.matmul(p_r[:], onesr[:], rows[:, TC:2 * TC], start=True,
                     stop=True, skip_group_check=True).then_inc(ctx.sem["eT"], 1)
            w("eV", bv + 8 + NKD)
            w("qA", pro1 + 16 * 3 * (ti + 1))
            for pj in range(NPAIR):
                if pj >= 1:
                    w("eV", bv + 8 + NKD + 3 * (pj - 1) + 2)
                for (mc, bank) in ((pj, p_a), (NPAIR + pj, p_g)):
                    ins = None
                    for kc in range(NKD):
                        ins = e.matmul(bank[:], wsb[:, kc * M1 + mc * 128: kc * M1 + mc * 128 + 128],
                                       xg[ti % 2][kc][:], start=(kc == 0), stop=False,
                                       skip_group_check=True)
                    ins = e.matmul(bank[:], augw[:, mc * 128:mc * 128 + 128],
                                   augr[ti % 2][:], start=False, stop=True,
                                   skip_group_check=True)
                    ins.then_inc(ctx.sem["eT"], 1)
    ctx.body("tensor", f_tensor)

    def f_vec(e):
        w = WT(ctx, e)
        w("qA", pro1)
        for ti in range(NT):
            bv = eV20 + NV2 * ti
            bt = eT20 + NT2 * ti
            bs = eS20 + NS2 * ti
            w("eT", bt + 1)
            e.tensor_scalar_mul(rows[:, 0:TC], p_s[:], 1.0 / D).then_inc(ctx.sem["eV"], 1)
            e.tensor_scalar_mul(rows[:, TC:2 * TC], p_q[:], 1.0 / D).then_inc(ctx.sem["eV"], 1)
            w("eV", bv + 2)
            e.tensor_tensor(rows[:, 2 * TC:3 * TC], rows[:, 0:TC], rows[:, 0:TC],
                            ALU.mult).then_inc(ctx.sem["eV"], 1)
            w("eV", bv + 3)
            e.tensor_sub(rows[:, 2 * TC:3 * TC], rows[:, TC:2 * TC],
                         rows[:, 2 * TC:3 * TC]).then_inc(ctx.sem["eV"], 1)
            w("eV", bv + 4)
            e.tensor_scalar_add(rows[:, 2 * TC:3 * TC], rows[:, 2 * TC:3 * TC],
                                1e-5).then_inc(ctx.sem["eV"], 1)
            w("eS", bs + 9)
            e.reciprocal(rows[:, TC:2 * TC], rows[:, 3 * TC:4 * TC]).then_inc(ctx.sem["eV"], 1)
            e.tensor_scalar_mul(rows2[:, 0:TC], rows[:, 0:TC], -1.0).then_inc(ctx.sem["eV"], 1)
            e.tensor_copy(rows2[:, TC:2 * TC], rows[:, 3 * TC:4 * TC]).then_inc(ctx.sem["eV"], 1)
            for kc in range(NKD):
                e.tensor_scalar_mul(xg[ti % 2][kc][:], xt[kc][:],
                                    gcol[:, kc:kc + 1]).then_inc(ctx.sem["eV"], 1)
            for pj in range(NPAIR):
                pv = bv + 8 + NKD + 3 * pj
                if pj >= 1:
                    w("eV", pv)
                w("eS", bs + 10)
                w("eT", bt + 2 + 2 * pj + 1)
                e.tensor_mul(ta[:], p_a[:], rstdb[:]).then_inc(ctx.sem["eV"], 1)
                w("eT", bt + 2 + 2 * pj + 2)
                e.tensor_mul(tg[:], p_g[:], rstdb[:]).then_inc(ctx.sem["eV"], 1)
                w("eS", bs + 10 + pj + 1)
                jj = NPAIR * ti + pj
                if jj >= 3:
                    w("qG", st0 + 16 * (jj - 2))
                e.tensor_mul(ob[jj % 3][:], ta[:], gg[:]).then_inc(ctx.sem["eV"], 1)
    ctx.body("vector", f_vec)

    def f_gp(e):
        for ti in range(NT):
            for pj in range(NPAIR):
                jj = NPAIR * ti + pj
                e.wait_ge(ctx.sem["eV"], eV20 + NV2 * ti + 8 + NKD + 3 * pj + 3)
                if jj >= 1:
                    e.wait_ge(ctx.sem["qG"], st0 + 16 * jj)
                e.dma_start(out=gactd[pj * 128:(pj + 1) * 128, ti * TC:(ti + 1) * TC],
                            in_=ob[jj % 3][:]).then_inc(ctx.sem["qG"], 16)
    ctx.body("gpsimd", f_gp)


def stage_subf2(ctx, i):
    wsb, wrdy = load_weights_resident(ctx, ctx.dram["S_w2"][i], 2816, D, "w2r")
    btile, brdy = load_bias_cols(ctx, ctx.dram["S_b2"][i], 8)

    def epi(info):
        epi_resid_store(ctx, info, f"xs{2 * i + 1}", f"xs{2 * i + 2}",
                        f"xs{2 * i + 2}b", bias_tile=btile)
    gemm_fm(ctx, wsb=wsb, w_ready=max(wrdy, brdy), NKC=22, MC=8,
            rhs_dram=ctx.dram["gact"], epi_emit=epi)


def stage_add3(ctx, a_name, b_name, c_name, out_f32, out_bf16=None):
    A, B, C = ctx.dram[a_name], ctx.dram[b_name], ctx.dram[c_name]
    OF = ctx.dram[out_f32]
    OB = ctx.dram[out_bf16] if out_bf16 else None
    at = [ctx.sb((128, TC), F32, "aa") for _ in range(2)]
    bt = [ctx.sb((128, TC), F32, "bb") for _ in range(2)]
    ct = [ctx.sb((128, TC), F32, "cc") for _ in range(2)]
    xo = [ctx.sb((128, TC), F32, "xo") for _ in range(2)]
    xob = [ctx.sb((128, TC), BF16, "xb") for _ in range(2)] if OB else None
    b = dict(ctx.cnt)
    qS0, st0, eV0 = b["qS"], b["qG"], b["eV"]
    n_it = NT * NKD
    n_dst = 2 if OB else 1
    nv = 3 if OB else 2
    ld = [qS0 + 16 * 3 * (j + 1) for j in range(n_it)]
    ctx.cnt["qS"] = qS0 + 16 * 3 * n_it
    ctx.cnt["eV"] = eV0 + nv * n_it
    ctx.cnt["qG"] = st0 + 16 * n_dst * n_it

    def f_sync(e):
        e.wait_ge(ctx.sem["qS"], qS0)
        for j in range(n_it):
            i, kc = j // NKD, j % NKD
            if j >= 1:
                e.wait_ge(ctx.sem["qS"], ld[j - 1])
            if j >= 2:
                e.wait_ge(ctx.sem["eV"], eV0 + nv * (j - 1))
            sl = (slice(kc * 128, (kc + 1) * 128), slice(i * TC, (i + 1) * TC))
            e.dma_start(out=at[j % 2][:], in_=A[sl[0], sl[1]]).then_inc(ctx.sem["qS"], 16)
            e.dma_start(out=bt[j % 2][:], in_=B[sl[0], sl[1]]).then_inc(ctx.sem["qS"], 16)
            e.dma_start(out=ct[j % 2][:], in_=C[sl[0], sl[1]]).then_inc(ctx.sem["qS"], 16)
    ctx.body("sync", f_sync)

    def f_vec(e):
        w = WT(ctx, e)
        for j in range(n_it):
            bv = eV0 + nv * j
            w("qS", ld[j])
            if j >= 2:
                w("qG", st0 + 16 * n_dst * (j - 1))
            e.tensor_add(xo[j % 2][:], at[j % 2][:], bt[j % 2][:]).then_inc(ctx.sem["eV"], 1)
            w("eV", bv + 1)
            e.tensor_add(xo[j % 2][:], xo[j % 2][:], ct[j % 2][:]).then_inc(ctx.sem["eV"], 1)
            if OB:
                w("eV", bv + 2)
                e.tensor_copy(xob[j % 2][:], xo[j % 2][:]).then_inc(ctx.sem["eV"], 1)
    ctx.body("vector", f_vec)

    def f_gp(e):
        for j in range(n_it):
            i, kc = j // NKD, j % NKD
            e.wait_ge(ctx.sem["eV"], eV0 + nv * (j + 1))
            if j >= 1:
                e.wait_ge(ctx.sem["qG"], st0 + 16 * n_dst * j)
            sl = (slice(kc * 128, (kc + 1) * 128), slice(i * TC, (i + 1) * TC))
            e.dma_start(out=OF[sl[0], sl[1]], in_=xo[j % 2][:]).then_inc(ctx.sem["qG"], 16)
            if OB:
                e.dma_start(out=OB[sl[0], sl[1]], in_=xob[j % 2][:]).then_inc(ctx.sem["qG"], 16)
    ctx.body("gpsimd", f_gp)


def stage_qk_generic(ctx, w_ap, K, M2, q_out, k_out, bias_name=None, q_scale=1.0):
    raise NotImplementedError


def stage_tbias(ctx):
    relT = ctx.dram["relT"]
    ohr = ctx.dram["OHrev"]
    grevd = ctx.dram["grev"]
    biasTd = ctx.dram["biasT"]

    rel = ctx.sb((32, 16), F32, "rel")
    relTt = ctx.sb((16, 32), F32, "relt")
    ohrt = ctx.sb((32, 255), F32, "ohr")
    gr = ctx.sb((16, 1280), F32, "gr")
    ones = ctx.sb((16, 512), F32, "onid")
    idxv = ctx.sb((128, 1), I32, "idx")
    btile = [ctx.sb((128, TC), F32, "bt") for _ in range(2)]
    p_g = ctx.psum((16, 255), F32, "pgm")

    b = dict(ctx.cnt)
    qS0, qG0, eT0, eS0, eV0, eG0 = b["qS"], b["qG"], b["eT"], b["eS"], b["eV"], b["eG"]

    def f_sync(e):
        e.wait_ge(ctx.sem["qS"], qS0)
        e.dma_start(out=relTt[:], in_=relT[:, :]).then_inc(ctx.sem["qS"], 16)
        e.dma_start(out=ohrt[:], in_=ohr[:, :]).then_inc(ctx.sem["qS"], 16)
        e.dma_start(out=rel[:], in_=ctx.dram["rel32"][:, :]).then_inc(ctx.sem["qS"], 16)
    ctx.body("sync", f_sync)
    ctx.cnt["qS"] = qS0 + 48

    def f_gp0(e):
        e.memset(ones[:], 1.0).then_inc(ctx.sem["eG"], 1)
    ctx.body("gpsimd", f_gp0)
    ctx.cnt["eG"] = eG0 + 1

    def f_t(e):
        e.wait_ge(ctx.sem["qS"], qS0 + 48)
        e.matmul(p_g[:], rel[:], ohrt[:], start=True, stop=True,
                 skip_group_check=True).then_inc(ctx.sem["eT"], 1)
    ctx.body("tensor", f_t)
    ctx.cnt["eT"] = eT0 + 1

    def f_v(e):
        e.wait_ge(ctx.sem["qS"], qS0 + 48)
        e.wait_ge(ctx.sem["eG"], eG0 + 1)
        e.tensor_scalar_mul(gr[:, 0:512], ones[:, 0:512], relTt[:, 31:32]).then_inc(ctx.sem["eV"], 1)
        e.tensor_scalar_mul(gr[:, 512:513], ones[:, 0:1], relTt[:, 31:32]).then_inc(ctx.sem["eV"], 1)
        e.tensor_scalar_mul(gr[:, 768:1280], ones[:, 0:512], relTt[:, 15:16]).then_inc(ctx.sem["eV"], 1)
        e.wait_ge(ctx.sem["eT"], eT0 + 1)
        e.tensor_copy(gr[:, 513:768], p_g[:]).then_inc(ctx.sem["eV"], 1)
    ctx.body("vector", f_v)
    ctx.cnt["eV"] = eV0 + 4

    def f_gp1(e):
        e.wait_ge(ctx.sem["eV"], eV0 + 4)
        e.dma_start(out=grevd[:, :], in_=gr[:]).then_inc(ctx.sem["qG"], 16)
        e.wait_ge(ctx.sem["qG"], qG0 + 16)
        grf = grevd.rearrange("h n -> (h n) ()")
        j = 0
        for h in range(H):
            for bi, d0 in enumerate((-128, 0, 128, 256, 384, 512)):
                base = h * 1280 + 640 - d0 - 127
                e.iota(idxv[:, 0:1], pattern=[[1, 1]], base=h * 1280 + 640 - d0,
                       channel_multiplier=-1).then_inc(ctx.sem["eG"], 1)
                e.wait_ge(ctx.sem["eG"], eG0 + 1 + j + 1)
                if j >= 2:
                    e.wait_ge(ctx.sem["qG"], qG0 + 16 + 16 * 2 * (j - 1))
                e.indirect_dma_start(
                    btile[j % 2][:, :], None,
                    grf,
                    IndirectOffsetOnAxis(ap=idxv[:, 0:1], axis=0),
                ).then_inc(ctx.sem["qG"], 16)
                e.wait_ge(ctx.sem["qG"], qG0 + 16 + 16 * (2 * j + 1))
                e.dma_start(out=biasTd[h, bi, :, :], in_=btile[j % 2][:]).then_inc(ctx.sem["qG"], 16)
                j += 1
    ctx.body("gpsimd", f_gp1)
    ctx.cnt["eG"] = eG0 + 1 + 96
    ctx.cnt["qG"] = qG0 + 16 + 16 * 2 * 96


def stage_rms(ctx, src, dst_bf16=None, out_row=None, eps=1e-6):
    xfm = ctx.dram[src]
    dstb = ctx.dram[dst_bf16] if dst_bf16 else None
    rowd = ctx.dram[out_row] if out_row else None

    ones1 = ctx.sb((128, 1), F32, "on1")
    onesr = ctx.sb((1, 128), F32, "onr")
    xt = [[ctx.sb((128, TC), F32, "x") for _ in range(NKD)] for _ in range(2)]
    sq = [ctx.sb((128, TC), F32, "sq") for _ in range(NKD)]
    rows = ctx.sb((1, 2 * TC), F32, "rw")
    rstdA = ctx.sb((1, T), F32, "rsA")
    xob = [ctx.sb((128, TC), BF16, "xob") for _ in range(3)]
    p_q = ctx.psum((1, TC), F32, "pq")
    p_r = ctx.psum((128, TC), F32, "pr")

    b = dict(ctx.cnt)
    eT0, eS0, eV0, eG0 = b["eT"], b["eS"], b["eV"], b["eG"]
    qS0, qG0 = b["qS"], b["qG"]
    ld1 = [qS0 + 16 * NKD * (i + 1) for i in range(NT)]
    ctx.cnt["qS"] = qS0 + 16 * NKD * NT
    NV1 = 3

    def f1_sync(e):
        e.wait_ge(ctx.sem["qS"], qS0)
        for i in range(NT):
            if i >= 1:
                e.wait_ge(ctx.sem["qS"], ld1[i - 1])
            if i >= 2:
                e.wait_ge(ctx.sem["eT"], eT0 + (i - 1))
            for kc in range(NKD):
                e.dma_start(out=xt[i % 2][kc][:],
                            in_=xfm[kc * 128:(kc + 1) * 128, i * TC:(i + 1) * TC]).then_inc(ctx.sem["qS"], 16)
    ctx.body("sync", f1_sync)

    def f1_gp(e):
        e.memset(ones1[:], 1.0).then_inc(ctx.sem["eG"], 1)
        e.memset(onesr[:], 1.0).then_inc(ctx.sem["eG"], 1)
    ctx.body("gpsimd", f1_gp)
    ctx.cnt["eG"] = eG0 + 2

    def f1_scalar(e):
        for i in range(NT):
            e.wait_ge(ctx.sem["qS"], ld1[i])
            if i >= 1:
                e.wait_ge(ctx.sem["eT"], eT0 + i)
            for kc in range(NKD):
                e.square(sq[kc][:], xt[i % 2][kc][:]).then_inc(ctx.sem["eS"], 1)
            e.wait_ge(ctx.sem["eV"], eV0 + NV1 * i + 2)
            e.activation(out=rows[:, TC:2 * TC], in_=rows[:, 0:TC],
                         func=AF.Sqrt).then_inc(ctx.sem["eS"], 1)
    ctx.body("scalar", f1_scalar)
    ctx.cnt["eS"] = eS0 + (NKD + 1) * NT

    def f1_tensor(e):
        e.wait_ge(ctx.sem["eG"], eG0 + 2)
        for i in range(NT):
            e.wait_ge(ctx.sem["eS"], eS0 + (NKD + 1) * i + NKD)
            if i >= 1:
                e.wait_ge(ctx.sem["eV"], eV0 + NV1 * (i - 1) + 1)
            ins = None
            for kc in range(NKD):
                ins = e.matmul(p_q[:], ones1[:], sq[kc][:], start=(kc == 0),
                               stop=(kc == NKD - 1), skip_group_check=True)
            ins.then_inc(ctx.sem["eT"], 1)
    ctx.body("tensor", f1_tensor)
    ctx.cnt["eT"] = eT0 + NT

    def f1_vec(e):
        for i in range(NT):
            bv = eV0 + NV1 * i
            e.wait_ge(ctx.sem["eT"], eT0 + i + 1)
            e.tensor_scalar_mul(rows[:, 0:TC], p_q[:], 1.0 / D).then_inc(ctx.sem["eV"], 1)
            e.wait_ge(ctx.sem["eV"], bv + 1)
            e.tensor_scalar_add(rows[:, 0:TC], rows[:, 0:TC], eps).then_inc(ctx.sem["eV"], 1)
            e.wait_ge(ctx.sem["eS"], eS0 + (NKD + 1) * (i + 1))
            e.reciprocal(rstdA[:, i * TC:(i + 1) * TC],
                         rows[:, TC:2 * TC]).then_inc(ctx.sem["eV"], 1)
    ctx.body("vector", f1_vec)
    ctx.cnt["eV"] = eV0 + NV1 * NT
    ph1 = ctx.cnt["eV"]

    if rowd is not None:
        def f_gpr(e):
            e.wait_ge(ctx.sem["eV"], ph1)
            e.dma_start(out=rowd[:, :], in_=rstdA[:]).then_inc(ctx.sem["qG"], 16)
        ctx.body("gpsimd", f_gpr)
        ctx.cnt["qG"] += 16

    if dstb is None:
        return

    b2 = dict(ctx.cnt)
    eT20, eV20 = b2["eT"], b2["eV"]
    st0 = b2["qG"]
    ld2 = [b2["qS"] + 16 * NKD * (i + 1) for i in range(NT)]
    ctx.cnt["qS"] = b2["qS"] + 16 * NKD * NT

    def f2_sync(e):
        e.wait_ge(ctx.sem["eV"], ph1)
        e.wait_ge(ctx.sem["qS"], b2["qS"])
        for i in range(NT):
            if i >= 1:
                e.wait_ge(ctx.sem["qS"], ld2[i - 1])
            if i >= 2:
                e.wait_ge(ctx.sem["eV"], eV20 + NKD * (i - 1))
            for kc in range(NKD):
                e.dma_start(out=xt[i % 2][kc][:],
                            in_=xfm[kc * 128:(kc + 1) * 128, i * TC:(i + 1) * TC]).then_inc(ctx.sem["qS"], 16)
    ctx.body("sync", f2_sync)

    def f2_tensor(e):
        e.wait_ge(ctx.sem["eV"], ph1)
        for i in range(NT):
            if i >= 1:
                e.wait_ge(ctx.sem["eV"], eV20 + NKD * i)
            e.matmul(p_r[:], onesr[:], rstdA[:, i * TC:(i + 1) * TC], start=True,
                     stop=True, skip_group_check=True).then_inc(ctx.sem["eT"], 1)
    ctx.body("tensor", f2_tensor)
    ctx.cnt["eT"] = eT20 + NT

    def f2_vec(e):
        for i in range(NT):
            e.wait_ge(ctx.sem["qS"], ld2[i])
            e.wait_ge(ctx.sem["eT"], eT20 + i + 1)
            for kc in range(NKD):
                kidx = i * NKD + kc
                if kidx >= 3:
                    e.wait_ge(ctx.sem["qG"], st0 + 16 * (kidx - 2))
                e.tensor_mul(xob[kidx % 3][:], xt[i % 2][kc][:], p_r[:]).then_inc(ctx.sem["eV"], 1)
    ctx.body("vector", f2_vec)
    ctx.cnt["eV"] = eV20 + NKD * NT

    def f2_gp(e):
        for i in range(NT):
            for kc in range(NKD):
                kidx = i * NKD + kc
                e.wait_ge(ctx.sem["eV"], eV20 + kidx + 1)
                if kidx >= 1:
                    e.wait_ge(ctx.sem["qG"], st0 + 16 * kidx)
                e.dma_start(out=dstb[kc * 128:(kc + 1) * 128, i * TC:(i + 1) * TC],
                            in_=xob[kidx % 3][:]).then_inc(ctx.sem["qG"], 16)
    ctx.body("gpsimd", f2_gp)
    ctx.cnt["qG"] = st0 + 16 * NKD * NT


def stage_wtr2(ctx):
    specs = [("F_wq", "wqB", "wqT", 1024), ("F_wk", "wkB", "wkT", 256),
             ("F_wv", "wvB", "wvT", 256), ("F_wo", "woB", "woT", 1024)]
    tmpf = ctx.sb((128, 1024), F32, "wf")
    tmpb = ctx.sb((128, 1024), BF16, "wb")
    ttile = [ctx.sb((128, 512), BF16, "tt") for _ in range(2)]
    b = dict(ctx.cnt)
    qS0, qG0, eV0 = b["qS"], b["qG"], b["eV"]

    n_ch = sum(s[3] // 128 for s in specs)
    ld = [qS0 + 16 * (j + 1) for j in range(n_ch)]
    st = [qG0 + 16 * (j + 1) for j in range(n_ch)]
    ctx.cnt["qS"] = qS0 + 16 * n_ch
    ctx.cnt["eV"] = eV0 + n_ch
    ctx.cnt["qG"] = qG0 + 16 * n_ch

    def f_sync(e):
        e.wait_ge(ctx.sem["qS"], qS0)
        j = 0
        for (fn, bn, tn, nr) in specs:
            for c in range(nr // 128):
                if j >= 1:
                    e.wait_ge(ctx.sem["qS"], ld[j - 1])
                if j >= 1:
                    e.wait_ge(ctx.sem["eV"], eV0 + j)
                e.dma_start(out=tmpf[:], in_=ctx.dram[fn][c * 128:(c + 1) * 128, :]).then_inc(ctx.sem["qS"], 16)
                j += 1
    ctx.body("sync", f_sync)

    def f_vec(e):
        j = 0
        for (fn, bn, tn, nr) in specs:
            for c in range(nr // 128):
                e.wait_ge(ctx.sem["qS"], ld[j])
                if j >= 1:
                    e.wait_ge(ctx.sem["qG"], st[j - 1])
                e.tensor_copy(tmpb[:], tmpf[:]).then_inc(ctx.sem["eV"], 1)
                j += 1
    ctx.body("vector", f_vec)

    def f_gp(e):
        j = 0
        for (fn, bn, tn, nr) in specs:
            for c in range(nr // 128):
                e.wait_ge(ctx.sem["eV"], eV0 + j + 1)
                if j >= 1:
                    e.wait_ge(ctx.sem["qG"], st[j - 1])
                e.dma_start(out=ctx.dram[bn][c * 128:(c + 1) * 128, :], in_=tmpb[:]).then_inc(ctx.sem["qG"], 16)
                j += 1
    ctx.body("gpsimd", f_gp)

    b2 = dict(ctx.cnt)
    qS20, qG20 = b2["qS"], b2["qG"]
    plan = []
    for (fn, bn, tn, nr) in specs:
        n_i = 1024 // 128
        n_o = nr // 512 if nr >= 512 else 0
        if nr >= 512:
            for i0 in range(n_i):
                for o0 in range(nr // 512):
                    plan.append((bn, tn, i0, o0, 512))
        else:
            for i0 in range(n_i):
                plan.append((bn, tn, i0, 0, nr))
    nP = len(plan)
    ld2 = [qS20 + 16 * (j + 1) for j in range(nP)]
    st2 = [qS20 + 16 * nP + 16 * (j + 1) for j in range(nP)]
    ctx.cnt["qS"] = qS20 + 16 * nP

    def f_sync2(e):
        e.wait_ge(ctx.sem["qG"], qG0 + 16 * n_ch)
        e.wait_ge(ctx.sem["qS"], qS20)
        for j, (bn, tn, i0, o0, ow) in enumerate(plan):
            if j >= 1:
                e.wait_ge(ctx.sem["qS"], ld2[j - 1])
            if j >= 2:
                e.wait_ge(ctx.sem["qG"], qG20 + 16 * (j - 1))
            e.dma_start(out=ttile[j % 2][:, 0:ow],
                        in_=ctx.dram[bn][o0 * 512:o0 * 512 + ow, i0 * 128:(i0 + 1) * 128],
                        transpose=True).then_inc(ctx.sem["qS"], 16)
    ctx.body("sync", f_sync2)

    def f_gp2(e):
        for j, (bn, tn, i0, o0, ow) in enumerate(plan):
            e.wait_ge(ctx.sem["qS"], ld2[j])
            if j >= 1:
                e.wait_ge(ctx.sem["qG"], qG20 + 16 * j)
            e.dma_start(out=ctx.dram[tn][i0 * 128:(i0 + 1) * 128, o0 * 512:o0 * 512 + ow],
                        in_=ttile[j % 2][:, 0:ow]).then_inc(ctx.sem["qG"], 16)
    ctx.body("gpsimd", f_gp2)
    ctx.cnt["qG"] = qG20 + 16 * nP


def stage_proj_fm(ctx, w_name, M, out_name, rhs_name):
    MC = M // 128
    wsb, wrdy = load_weights_resident(ctx, ctx.dram[w_name], D, M, "wp")
    od = ctx.dram[out_name]

    def stores(i, mc):
        return [od[mc * 128:(mc + 1) * 128, i * TC:(i + 1) * TC]]

    def epi(info):
        epi_act_store(ctx, info, stores, out_dtype=BF16)
    gemm_fm(ctx, wsb=wsb, w_ready=wrdy, NKC=NKD, MC=MC,
            rhs_dram=ctx.dram[rhs_name], epi_emit=epi)


def stage_rope2(ctx):
    qin, kin = ctx.dram["q2r"], ctx.dram["k2r"]
    qout, kout = ctx.dram["q2b"], ctx.dram["k2b"]
    NKK = 2

    tbl = {}
    ttmp = ctx.sb((128, T), F32, "ttmp")
    for nm in ("cq", "sq", "ck", "sk"):
        tbl[nm] = ctx.sb((128, T), BF16, "tb" + nm)
    iptP = ctx.sb((16, NKD * 128), BF16, "iptP")
    ipq = ctx.sb((128, NKD * H), BF16, "ipq")
    iptK = ctx.sb((4, NKK * 128), BF16, "iptK")
    ipk = ctx.sb((128, NKK * 4), BF16, "ipk")
    swp = ctx.sb((128, 128), BF16, "swp")
    gq = ctx.sb((H, 1), F32, "gq")
    gk = ctx.sb((4, 1), F32, "gk")
    qt = [ctx.sb((128, TC), BF16, "qt") for _ in range(NKD)]
    kt = [ctx.sb((128, TC), BF16, "kt") for _ in range(NKK)]
    sqb = [ctx.sb((128, TC), BF16, "sqb") for _ in range(NKD)]
    rows = ctx.sb((H, 2 * TC), F32, "rws")
    rowk = ctx.sb((4, 2 * TC), F32, "rwk")
    rqb = ctx.sb((H, TC), BF16, "rqb")
    rkb = ctx.sb((4, TC), BF16, "rkb")
    t1b = [ctx.sb((128, TC), BF16, "t1") for _ in range(2)]
    u1b = [ctx.sb((128, TC), BF16, "u1") for _ in range(2)]
    u2b = [ctx.sb((128, TC), BF16, "u2") for _ in range(2)]
    ob = [ctx.sb((128, TC), BF16, "rob") for _ in range(3)]
    p_qn = ctx.psum((H, TC), F32, "pqn")
    p_kn = ctx.psum((4, TC), F32, "pkn")
    p_rb = [ctx.psum((128, TC), F32, "prb") for _ in range(2)]
    p_sw = [ctx.psum((128, TC), F32, "psw") for _ in range(2)]

    b = dict(ctx.cnt)
    eT0, eS0, eV0, eG0 = b["eT"], b["eS"], b["eV"], b["eG"]
    qS0, qG0, qA0 = b["qS"], b["qG"], b["qA"]

    n_pre = 2 + NKD + NKD + NKK + NKK + 1

    def f_sc_pro(e):
        e.wait_ge(ctx.sem["qA"], qA0)
        e.dma_start(out=gq[:, 0], in_=ctx.dram["F_qg"][:]).then_inc(ctx.sem["qA"], 16)
        e.dma_start(out=gk[:, 0], in_=ctx.dram["F_kg"][:]).then_inc(ctx.sem["qA"], 16)
        for c in range(NKD):
            e.dma_start(out=iptP[:, c * 128:(c + 1) * 128], in_=ctx.dram["indPT"][c, :, :]).then_inc(ctx.sem["qA"], 16)
        for c in range(NKD):
            e.dma_start(out=ipq[:, c * H:(c + 1) * H], in_=ctx.dram["indP"][c, :, :]).then_inc(ctx.sem["qA"], 16)
        for c in range(NKK):
            e.dma_start(out=iptK[:, c * 128:(c + 1) * 128], in_=ctx.dram["indKT"][c, :, :]).then_inc(ctx.sem["qA"], 16)
        for c in range(NKK):
            e.dma_start(out=ipk[:, c * 4:(c + 1) * 4], in_=ctx.dram["indK"][c, :, :]).then_inc(ctx.sem["qA"], 16)
        e.dma_start(out=swp[:], in_=ctx.dram["swap64"][:, :]).then_inc(ctx.sem["qA"], 16)
        names = {"cq": "rope1_cq", "sq": "rope1_sq", "ck": "rope1_ck", "sk": "rope1_sk"}
        for j, nm in enumerate(("cq", "sq", "ck", "sk")):
            if j >= 1:
                e.wait_ge(ctx.sem["eV"], eV0 + j)
                e.wait_ge(ctx.sem["qA"], qA0 + 16 * (n_pre + j))
            e.dma_start(out=ttmp[:], in_=ctx.dram[names[nm]][:, :]).then_inc(ctx.sem["qA"], 16)
    ctx.body("scalar", f_sc_pro)
    ctx.cnt["qA"] = qA0 + 16 * (n_pre + 4)
    pro_tgt = ctx.cnt["qA"]

    def f_v_pro(e):
        for j, nm in enumerate(("cq", "sq", "ck", "sk")):
            e.wait_ge(ctx.sem["qA"], qA0 + 16 * (n_pre + j + 1))
            e.tensor_copy(tbl[nm][:], ttmp[:]).then_inc(ctx.sem["eV"], 1)
    ctx.body("vector", f_v_pro)
    ctx.cnt["eV"] = eV0 + 4
    eVp = ctx.cnt["eV"]

    ld1 = []
    cur = ctx.cnt["qS"]
    for ti in range(NT):
        cur += 16 * (NKD + NKK)
        ld1.append(cur)
    ctx.cnt["qS"] = cur

    NVsq = NKD + NKK
    NVrow = 10
    NVrow = 10
    NVtile = 4 * (NKD + NKK)
    NV = NVsq + NVrow + NVtile
    NSt = 2
    NTt = 2 + 2 * (NKD + NKK)
    eVb = lambda ti: eVp + NV * ti
    eTb = lambda ti: eT0 + NTt * ti
    eSb = lambda ti: eS0 + NSt * ti
    st0 = qG0
    ctx.cnt["eV"] = eVp + NV * NT
    ctx.cnt["eS"] = eS0 + NSt * NT
    ctx.cnt["eT"] = eT0 + NTt * NT
    ctx.cnt["qG"] = st0 + 16 * (NKD + NKK) * NT

    def f_sync(e):
        e.wait_ge(ctx.sem["qS"], qS0)
        for ti in range(NT):
            if ti >= 1:
                e.wait_ge(ctx.sem["qS"], ld1[ti - 1])
                e.wait_ge(ctx.sem["eV"], eVb(ti))
            for c in range(NKD):
                e.dma_start(out=qt[c][:], in_=qin[c * 128:(c + 1) * 128, ti * TC:(ti + 1) * TC]).then_inc(ctx.sem["qS"], 16)
            for c in range(NKK):
                e.dma_start(out=kt[c][:], in_=kin[c * 128:(c + 1) * 128, ti * TC:(ti + 1) * TC]).then_inc(ctx.sem["qS"], 16)
    ctx.body("sync", f_sync)

    def f_vec(e):
        w = WT(ctx, e)
        w("qA", pro_tgt)
        for ti in range(NT):
            bv = eVb(ti)
            w("qS", ld1[ti])
            if ti >= 1:
                w("eT", eTb(ti))
            for c in range(NKD):
                e.tensor_mul(sqb[c][:], qt[c][:], qt[c][:]).then_inc(ctx.sem["eV"], 1)
            w("eT", eTb(ti) + 1)
            for c in range(NKK):
                e.tensor_mul(sqb[c][:], kt[c][:], kt[c][:]).then_inc(ctx.sem["eV"], 1)
            w("eT", eTb(ti) + 1)
            e.tensor_scalar_mul(rows[:, 0:TC], p_qn[:], 1.0 / 64).then_inc(ctx.sem["eV"], 1)
            w("eV", bv + NVsq + 1)
            e.tensor_scalar_add(rows[:, 0:TC], rows[:, 0:TC], 1e-6).then_inc(ctx.sem["eV"], 1)
            w("eS", eSb(ti) + 1)
            e.reciprocal(rows[:, TC:2 * TC], rows[:, TC:2 * TC]).then_inc(ctx.sem["eV"], 1)
            w("eV", bv + NVsq + 3)
            e.tensor_scalar_mul(rows[:, TC:2 * TC], rows[:, TC:2 * TC],
                                gq[:, 0:1]).then_inc(ctx.sem["eV"], 1)
            w("eV", bv + NVsq + 4)
            e.tensor_copy(rqb[:], rows[:, TC:2 * TC]).then_inc(ctx.sem["eV"], 1)
            w("eT", eTb(ti) + 2)
            e.tensor_scalar_mul(rowk[:, 0:TC], p_kn[:], 1.0 / 64).then_inc(ctx.sem["eV"], 1)
            w("eV", bv + NVsq + 6)
            e.tensor_scalar_add(rowk[:, 0:TC], rowk[:, 0:TC], 1e-6).then_inc(ctx.sem["eV"], 1)
            w("eS", eSb(ti) + 2)
            e.reciprocal(rowk[:, TC:2 * TC], rowk[:, TC:2 * TC]).then_inc(ctx.sem["eV"], 1)
            w("eV", bv + NVsq + 8)
            e.tensor_scalar_mul(rowk[:, TC:2 * TC], rowk[:, TC:2 * TC],
                                gk[:, 0:1]).then_inc(ctx.sem["eV"], 1)
            w("eV", bv + NVsq + 9)
            e.tensor_copy(rkb[:], rowk[:, TC:2 * TC]).then_inc(ctx.sem["eV"], 1)
            for j in range(NKD + NKK):
                is_q = j < NKD
                c = j if is_q else j - NKD
                raw = qt[c] if is_q else kt[c]
                cs = tbl["cq"] if is_q else tbl["ck"]
                ss = tbl["sq"] if is_q else tbl["sk"]
                bt = bv + NVsq + NVrow + 4 * j
                tt = eTb(ti) + 2 + 2 * j
                w("eT", tt + 1)
                e.tensor_mul(t1b[j % 2][:], raw[:], p_rb[j % 2][:]).then_inc(ctx.sem["eV"], 1)
                w("eV", bt + 1)
                e.tensor_mul(u1b[j % 2][:], t1b[j % 2][:],
                             cs[:, ti * TC:(ti + 1) * TC]).then_inc(ctx.sem["eV"], 1)
                w("eT", tt + 2)
                e.tensor_mul(u2b[j % 2][:], p_sw[j % 2][:],
                             ss[:, ti * TC:(ti + 1) * TC]).then_inc(ctx.sem["eV"], 1)
                w("eV", bt + 3)
                jj = (NKD + NKK) * ti + j
                if jj >= 3:
                    w("qG", st0 + 16 * (jj - 2))
                e.tensor_add(ob[jj % 3][:], u1b[j % 2][:], u2b[j % 2][:]).then_inc(ctx.sem["eV"], 1)
    ctx.body("vector", f_vec)

    def f_tensor(e):
        w = WT(ctx, e)
        w("qA", pro_tgt)
        for ti in range(NT):
            bv = eVb(ti)
            w("qS", ld1[ti])
            w("eV", bv + NKD)
            if ti >= 1:
                w("eV", eVb(ti - 1) + NVsq + 6)
            ins = None
            for c in range(NKD):
                ins = e.matmul(p_qn[:], ipq[:, c * H:(c + 1) * H], sqb[c][:], start=(c == 0),
                               stop=(c == NKD - 1), skip_group_check=True)
            ins.then_inc(ctx.sem["eT"], 1)
            w("eV", bv + NKD + NKK)
            for c in range(NKK):
                ins = e.matmul(p_kn[:], ipk[:, c * 4:(c + 1) * 4], sqb[c][:], start=(c == 0),
                               stop=(c == NKK - 1), skip_group_check=True)
            ins.then_inc(ctx.sem["eT"], 1)
            for j in range(NKD + NKK):
                is_q = j < NKD
                c = j if is_q else j - NKD
                ipt_t = iptP[:, c * 128:(c + 1) * 128] if is_q else iptK[:, c * 128:(c + 1) * 128]
                rrow_b = rqb if is_q else rkb
                bt = bv + NVsq + NVrow + 4 * j
                w("eV", bv + NVsq + (5 if is_q else NVrow))
                if j >= 2:
                    w("eV", bv + NVsq + NVrow + 4 * (j - 2) + 2)
                e.matmul(p_rb[j % 2][:], ipt_t, rrow_b[:],
                         start=True, stop=True, skip_group_check=True).then_inc(ctx.sem["eT"], 1)
                w("eV", bt + 1)
                if j >= 2:
                    w("eV", bv + NVsq + NVrow + 4 * (j - 2) + 4)
                e.matmul(p_sw[j % 2][:], swp[:], t1b[j % 2][:],
                         start=True, stop=True, skip_group_check=True).then_inc(ctx.sem["eT"], 1)
    ctx.body("tensor", f_tensor)

    def f_scalar(e):
        w = WT(ctx, e)
        for ti in range(NT):
            bv = eVb(ti)
            w("eT", eTb(ti) + 1)
            w("eV", bv + NVsq + 2)
            e.activation(out=rows[:, TC:2 * TC], in_=rows[:, 0:TC],
                         func=AF.Sqrt).then_inc(ctx.sem["eS"], 1)
            w("eT", eTb(ti) + 2)
            w("eV", bv + NVsq + 7)
            e.activation(out=rowk[:, TC:2 * TC], in_=rowk[:, 0:TC],
                         func=AF.Sqrt).then_inc(ctx.sem["eS"], 1)
    ctx.body("scalar", f_scalar)

    def f_gp(e):
        for ti in range(NT):
            for j in range(NKD + NKK):
                is_q = j < NKD
                c = j if is_q else j - NKD
                dst = qout if is_q else kout
                e.wait_ge(ctx.sem["eV"], eVb(ti) + NVsq + NVrow + 4 * (j + 1))
                jj = (NKD + NKK) * ti + j
                if jj >= 1:
                    e.wait_ge(ctx.sem["qG"], st0 + 16 * jj)
                e.dma_start(out=dst[c * 128:(c + 1) * 128, ti * TC:(ti + 1) * TC],
                            in_=ob[jj % 3][:]).then_inc(ctx.sem["qG"], 16)
    ctx.body("gpsimd", f_gp)


def stage_gate(ctx):
    lrow = ctx.sb((1, 8), F32, "lg")
    iot = ctx.sb((1, 8), F32, "io")
    tmp = ctx.sb((1, 32), F32, "tm")
    outt = ctx.sb((1, 4), F32, "go")
    b = dict(ctx.cnt)
    qA0, eV0, eS0, eG0, qG0 = b["qA"], b["eV"], b["eS"], b["eG"], b["qG"]
    n_v = 14
    n_s = 1
    for bi, lname in enumerate(("F_flog", "F_plog")):
        qA_t = qA0 + 16 * (bi + 1)
        bv = eV0 + n_v * bi
        bs = eS0 + n_s * bi
        bg = eG0 + 1 + 2 * bi

        def f_sc(e, lname=lname, qA_t=qA_t, bs=bs, bv=bv):
            e.wait_ge(ctx.sem["qA"], qA_t - 16)
            e.dma_start(out=lrow[:], in_=ctx.dram[lname][:]).then_inc(ctx.sem["qA"], 16)
            e.wait_ge(ctx.sem["eV"], bv + 10)
            e.activation(out=tmp[:, 10:11], in_=tmp[:, 9:10], func=AF.Exp).then_inc(ctx.sem["eS"], 1)
        ctx.body("scalar", f_sc)

        def f_gp0(e, bg=bg, bv=bv, qG_t=qG0 + 32 * bi):
            if bi == 0:
                e.iota(iot[:], pattern=[[1, 8]], base=0, channel_multiplier=0,
                       allow_small_or_imprecise_dtypes=True).then_inc(ctx.sem["eG"], 1)
            e.wait_ge(ctx.sem["eV"], bv + n_v)
            if bi == 1:
                e.wait_ge(ctx.sem["qG"], qG0 + 32)
            e.dma_start(out=ctx.dram["gatev"][bi, :], in_=outt[:, 0:2]).then_inc(ctx.sem["qG"], 16)
            e.dma_start(out=ctx.dram["gidx"][bi, :], in_=outt[:, 2:4]).then_inc(ctx.sem["qG"], 16)
        ctx.body("gpsimd", f_gp0)

        def f_v(e, bv=bv, bs=bs, qA_t=qA_t, bi=bi):
            w = WT(ctx, e)
            w("qA", qA_t)
            w("eG", eG0 + 1)
            if bi == 1:
                w("qG", qG0 + 32)
            e.reduce_max(tmp[:, 0:1], lrow[:], axis=AX.X).then_inc(ctx.sem["eV"], 1)
            w("eV", bv + 1)
            e.tensor_scalar(tmp[:, 1:9], lrow[:], tmp[:, 0:1], None,
                            ALU.is_ge).then_inc(ctx.sem["eV"], 1)
            w("eV", bv + 2)
            e.tensor_mul(tmp[:, 11:19], tmp[:, 1:9], iot[:]).then_inc(ctx.sem["eV"], 1)
            w("eV", bv + 3)
            e.reduce_sum(outt[:, 2:3], tmp[:, 11:19], axis=AX.X).then_inc(ctx.sem["eV"], 1)
            w("eV", bv + 4)
            e.scalar_tensor_tensor(tmp[:, 19:27], tmp[:, 1:9], -1e9, lrow[:],
                                   ALU.mult, ALU.add).then_inc(ctx.sem["eV"], 1)
            w("eV", bv + 5)
            e.reduce_max(tmp[:, 27:28], tmp[:, 19:27], axis=AX.X).then_inc(ctx.sem["eV"], 1)
            w("eV", bv + 6)
            e.tensor_scalar(tmp[:, 1:9], tmp[:, 19:27], tmp[:, 27:28], None,
                            ALU.is_ge).then_inc(ctx.sem["eV"], 1)
            w("eV", bv + 7)
            e.tensor_mul(tmp[:, 11:19], tmp[:, 1:9], iot[:]).then_inc(ctx.sem["eV"], 1)
            w("eV", bv + 8)
            e.reduce_sum(outt[:, 3:4], tmp[:, 11:19], axis=AX.X).then_inc(ctx.sem["eV"], 1)
            w("eV", bv + 9)
            e.tensor_sub(tmp[:, 9:10], tmp[:, 27:28], tmp[:, 0:1]).then_inc(ctx.sem["eV"], 1)
            w("eS", bs + 1)
            e.tensor_scalar_add(tmp[:, 11:12], tmp[:, 10:11], 1.0).then_inc(ctx.sem["eV"], 1)
            w("eV", bv + 11)
            e.reciprocal(outt[:, 0:1], tmp[:, 11:12]).then_inc(ctx.sem["eV"], 1)
            w("eV", bv + 12)
            e.tensor_mul(outt[:, 1:2], tmp[:, 10:11], outt[:, 0:1]).then_inc(ctx.sem["eV"], 1)
            e.nop().then_inc(ctx.sem["eV"], 1)
        ctx.body("vector", f_v)
    ctx.cnt["qA"] = qA0 + 32
    ctx.cnt["eV"] = eV0 + 2 * n_v
    ctx.cnt["eS"] = eS0 + 2 * n_s
    ctx.cnt["eG"] = eG0 + 1 + 0
    ctx.cnt["qG"] = qG0 + 64
    ctx.cnt["eG"] = eG0 + 1


def _bcast_col(ctx, e_t, e_s, src_row_ap, dst_col, psum, onesr, wait_ev, eT_slot, eS_slot):
    raise NotImplementedError


def stage_mixprep(ctx):
    onesr = ctx.sb((1, 128), F32, "onr")
    gval = ctx.sb((1, 4), F32, "gv")
    scol = ctx.sb((128, 4), F32, "sc")
    iotc = ctx.sb((128, 1), F32, "ioc")
    idxi = ctx.sb((128, 4), I32, "idi")
    idxf = ctx.sb((128, 1), F32, "idf")
    gtile = ctx.sb((128, 4096), F32, "gt")
    btile = ctx.sb((128, 4096), BF16, "btl")
    ttile = [ctx.sb((128, 512), BF16, "tt") for _ in range(2)]
    p_b = ctx.psum((128, 4), F32, "pbc")
    b = dict(ctx.cnt)
    qA0, qS0, qG0, eT0, eS0, eV0, eG0 = b["qA"], b["qS"], b["qG"], b["eT"], b["eS"], b["eV"], b["eG"]

    def f_sc0(e):
        e.wait_ge(ctx.sem["qA"], qA0)
        e.dma_start(out=gval[:], in_=ctx.dram["gidx"][:, :].rearrange("a b -> () (a b)")).then_inc(ctx.sem["qA"], 16)
    ctx.body("scalar", f_sc0)
    ctx.cnt["qA"] = qA0 + 16

    def f_gp0(e):
        e.memset(onesr[:], 1.0).then_inc(ctx.sem["eG"], 1)
        e.iota(iotc[:], pattern=[[1, 1]], base=0, channel_multiplier=1,
               allow_small_or_imprecise_dtypes=True).then_inc(ctx.sem["eG"], 1)
    ctx.body("gpsimd", f_gp0)
    ctx.cnt["eG"] = eG0 + 2

    def f_t0(e):
        e.wait_ge(ctx.sem["qA"], qA0 + 16)
        e.wait_ge(ctx.sem["eG"], eG0 + 2)
        e.matmul(p_b[:], onesr[:], gval[:], start=True, stop=True,
                 skip_group_check=True).then_inc(ctx.sem["eT"], 1)
    ctx.body("tensor", f_t0)
    ctx.cnt["eT"] = eT0 + 1

    def f_s0(e):
        e.wait_ge(ctx.sem["eT"], eT0 + 1)
        e.copy(scol[:], p_b[:]).then_inc(ctx.sem["eS"], 1)
    ctx.body("scalar", f_s0)
    ctx.cnt["eS"] = eS0 + 1

    specs = []
    for bi, (mixname, Bname, Tname, NR, RL) in enumerate(
            (("F_fmix", "fmixB", "fmixT", 4096, 1024),
             ("F_pmix", "pmixB", "pmixT", 1024, 4096))):
        for k in range(2):
            specs.append((bi, k, mixname, Bname, Tname, NR, RL))

    eVc = eV0
    qGc = qG0
    qSc = ctx.cnt["qS"]
    for (bi, k, mixname, Bname, Tname, NR, RL) in specs:
        slot = 2 * bi + k
        nch = NR // 128
        eva = eVc

        def f_v1(e, slot=slot, eva=eva, NR=NR):
            e.wait_ge(ctx.sem["eS"], eS0 + 1)
            if eva > eV0:
                e.wait_ge(ctx.sem["eV"], eva)
            e.scalar_tensor_tensor(idxf[:], scol[:, slot:slot + 1], float(NR),
                                   iotc[:], ALU.mult, ALU.add).then_inc(ctx.sem["eV"], 1)
            e.wait_ge(ctx.sem["eV"], eva + 1)
            e.tensor_copy(idxi[:, slot:slot + 1], idxf[:]).then_inc(ctx.sem["eV"], 1)
        ctx.body("vector", f_v1)
        eVc += 2
        ctx.cnt["eV"] = eVc

        g0 = qGc
        mixd = ctx.dram[mixname].rearrange("b r c -> (b r) c")

        def f_gp1(e, slot=slot, nch=nch, g0=g0, RL=RL, Bname=Bname, eva=eVc - 2, evb=eVc):
            w = WT(ctx, e)
            w("eV", eva + 2)
            for c in range(nch):
                if c >= 1:
                    w("qG", g0 + 16 * (2 * c - 1))
                    w("eV", evb + c)
                e.indirect_dma_start(
                    gtile[:, 0:RL], None, mixd,
                    IndirectOffsetOnAxis(ap=idxi[:, slot:slot + 1], axis=0),
                    element_offset=c * 128 * RL,
                ).then_inc(ctx.sem["qG"], 16)
                w("qG", g0 + 16 * (2 * c + 1))
                w("eV", evb + c + 1)
                e.dma_start(out=ctx.dram[Bname][c * 128:(c + 1) * 128, :],
                            in_=btile[:, 0:RL]).then_inc(ctx.sem["qG"], 16)
        ctx.body("gpsimd", f_gp1)

        def f_v2(e, nch=nch, g0=g0, RL=RL, evb=eVc):
            w = WT(ctx, e)
            for c in range(nch):
                w("qG", g0 + 16 * (2 * c + 1))
                if c >= 1:
                    w("qG", g0 + 16 * 2 * c)
                e.tensor_copy(btile[:, 0:RL], gtile[:, 0:RL]).then_inc(ctx.sem["eV"], 1)
        ctx.body("vector", f_v2)
        qGc = g0 + 16 * 2 * nch
        eVc += nch
        ctx.cnt["eV"] = eVc
        ctx.cnt["qG"] = qGc

        n_i = RL // 128
        n_o = NR // 512
        nP = n_i * n_o
        ld2 = [qSc + 16 * (j + 1) for j in range(nP)]
        s0 = qGc

        def f_sy(e, ld2=ld2, s0=s0, n_i=n_i, n_o=n_o, Bname=Bname, qSc=qSc, g_all=qGc):
            e.wait_ge(ctx.sem["qS"], qSc)
            e.wait_ge(ctx.sem["qG"], g_all)
            j = 0
            for i0 in range(n_i):
                for o0 in range(n_o):
                    if j >= 1:
                        e.wait_ge(ctx.sem["qS"], ld2[j - 1])
                    if j >= 2:
                        e.wait_ge(ctx.sem["qG"], s0 + 16 * (j - 1))
                    e.dma_start(out=ttile[j % 2][:],
                                in_=ctx.dram[Bname][o0 * 512:(o0 + 1) * 512, i0 * 128:(i0 + 1) * 128],
                                transpose=True).then_inc(ctx.sem["qS"], 16)
                    j += 1
        ctx.body("sync", f_sy)

        def f_gp2(e, ld2=ld2, s0=s0, n_i=n_i, n_o=n_o, Tname=Tname, k=k):
            j = 0
            for i0 in range(n_i):
                for o0 in range(n_o):
                    e.wait_ge(ctx.sem["qS"], ld2[j])
                    if j >= 1:
                        e.wait_ge(ctx.sem["qG"], s0 + 16 * j)
                    e.dma_start(out=ctx.dram[Tname + str(k)][i0 * 128:(i0 + 1) * 128, o0 * 512:(o0 + 1) * 512],
                                in_=ttile[j % 2][:]).then_inc(ctx.sem["qG"], 16)
                    j += 1
        ctx.body("gpsimd", f_gp2)
        qSc += 16 * nP
        qGc = s0 + 16 * nP
        ctx.cnt["qS"] = qSc
        ctx.cnt["qG"] = qGc


def stage_member(ctx, branch, k):
    fc = branch == "fc"
    SIN = "F_fs1" if fc else "F_ps1"
    SOUT = "F_fs2" if fc else "F_ps2"
    MIXT = "fmixT" if fc else "pmixT"
    AN, BN = ("iotaA_fc", "iotaB_fc") if fc else ("iotaA_pj", "iotaB_pj")
    MIX_K = 1024 if fc else 4096
    MIX_M = 4096 if fc else 1024
    n_bi = 16 if fc else 64
    n_bo = 64 if fc else 16
    IN = "hb" if fc else "h2b"
    NKIN = MIX_K // 128
    NMO = MIX_M // 128
    slot = (0 if fc else 2) + k
    ind = ctx.dram[IN]

    onesr = ctx.sb((1, 128), F32, "onr")
    gv2 = ctx.sb((1, 8), F32, "gv2")
    ccol = ctx.sb((128, 8), F32, "cc")
    iotc = ctx.sb((128, 1), F32, "ioc")
    abt = ctx.sb((128, 2 * NKIN), F32, "abt")
    rxf = ctx.sb((128, NKIN), F32, "rxf")
    rxi = ctx.sb((128, NKIN + 2), I32, "rxi")
    sgf = ctx.sb((128, 64), F32, "sgf")
    sgb = ctx.sb((128, 64), BF16, "sgb")
    s1p = [ctx.sb((128, 128), BF16, "s1p") for _ in range(n_bi // 2)]
    s2p = [ctx.sb((128, 128), BF16, "s2p") for _ in range(n_bo // 2)]
    weff = ctx.sb((128, NKIN * MIX_M), BF16, "wef")
    int_ = [ctx.sb((128, TC), BF16, "int") for _ in range(NKIN)]
    h1 = [ctx.sb((128, TC), BF16, "h1m") for _ in range(NKIN)]
    mt = ctx.sb((128, TC), BF16, "mt")
    rt = [ctx.sb((128, TC), F32, "rt") for _ in range(2)]
    ob = [ctx.sb((128, TC), F32, "mo") for _ in range(3)]
    obb = [ctx.sb((128, TC), BF16, "mob") for _ in range(3)]
    p_1 = ctx.psum((128, TC), F32, "p1")
    p_m = [ctx.psum((128, TC), F32, "pm") for _ in range(2)]
    p_2 = [ctx.psum((128, TC), F32, "p2") for _ in range(2)]
    p_c = ctx.psum((128, 8), F32, "pc")

    b = dict(ctx.cnt)
    qA0, qS0, qG0 = b["qA"], b["qS"], b["qG"]
    eT0, eS0, eV0, eG0 = b["eT"], b["eS"], b["eV"], b["eG"]

    def f_sc0(e):
        e.wait_ge(ctx.sem["qA"], qA0)
        e.dma_start(out=gv2[:, 0:4], in_=ctx.dram["gatev"][:, :].rearrange("a b -> () (a b)")).then_inc(ctx.sem["qA"], 16)
        e.dma_start(out=gv2[:, 4:8], in_=ctx.dram["gidx"][:, :].rearrange("a b -> () (a b)")).then_inc(ctx.sem["qA"], 16)
        for c in range(NKIN):
            e.dma_start(out=abt[:, c], in_=ctx.dram[AN][c * 128:(c + 1) * 128]).then_inc(ctx.sem["qA"], 16)
        for c in range(NKIN):
            e.dma_start(out=abt[:, NKIN + c], in_=ctx.dram[BN][c * 128:(c + 1) * 128]).then_inc(ctx.sem["qA"], 16)
    ctx.body("scalar", f_sc0)
    ctx.cnt["qA"] = qA0 + 16 * (2 + 2 * NKIN)
    pro_qA = ctx.cnt["qA"]

    def f_gp0(e):
        e.memset(onesr[:], 1.0).then_inc(ctx.sem["eG"], 1)
        e.iota(iotc[:], pattern=[[1, 1]], base=0, channel_multiplier=1,
               allow_small_or_imprecise_dtypes=True).then_inc(ctx.sem["eG"], 1)
        for j in range(n_bi // 2):
            e.memset(s1p[j][:], 0.0).then_inc(ctx.sem["eG"], 1)
        for j in range(n_bo // 2):
            e.memset(s2p[j][:], 0.0).then_inc(ctx.sem["eG"], 1)
    ctx.body("gpsimd", f_gp0)
    n_ms = 2 + n_bi // 2 + n_bo // 2
    ctx.cnt["eG"] = eG0 + n_ms

    def f_t0(e):
        e.wait_ge(ctx.sem["qA"], pro_qA)
        e.wait_ge(ctx.sem["eG"], eG0 + 2)
        e.matmul(p_c[:], onesr[:], gv2[:], start=True, stop=True,
                 skip_group_check=True).then_inc(ctx.sem["eT"], 1)
    ctx.body("tensor", f_t0)
    ctx.cnt["eT"] = eT0 + 1

    def f_s1(e):
        e.wait_ge(ctx.sem["eT"], eT0 + 1)
        e.copy(ccol[:], p_c[:]).then_inc(ctx.sem["eS"], 1)
    ctx.body("scalar", f_s1)
    ctx.cnt["eS"] = eS0 + 1

    def f_v0(e):
        w = WT(ctx, e)
        w("eS", eS0 + 1)
        e.tensor_scalar_add(rxf[:], abt[:, NKIN:2 * NKIN],
                            ccol[:, 4 + slot:5 + slot]).then_inc(ctx.sem["eV"], 1)
        w("eV", eV0 + 1)
        e.tensor_scalar(abt[:, NKIN:2 * NKIN], rxf[:], float(n_bi), None,
                        ALU.is_ge).then_inc(ctx.sem["eV"], 1)
        w("eV", eV0 + 2)
        e.scalar_tensor_tensor(rxf[:], abt[:, NKIN:2 * NKIN], float(-n_bi), rxf[:],
                               ALU.mult, ALU.add).then_inc(ctx.sem["eV"], 1)
        w("eV", eV0 + 3)
        e.tensor_add(rxf[:], rxf[:], abt[:, 0:NKIN]).then_inc(ctx.sem["eV"], 1)
        w("eV", eV0 + 4)
        e.tensor_copy(rxi[:, 0:NKIN], rxf[:]).then_inc(ctx.sem["eV"], 1)
        w("eV", eV0 + 5)
        e.scalar_tensor_tensor(rxf[:, 0:1], ccol[:, 4 + slot:5 + slot],
                               float(n_bi * 64), iotc[:], ALU.mult,
                               ALU.add).then_inc(ctx.sem["eV"], 1)
        w("eV", eV0 + 6)
        e.tensor_copy(rxi[:, NKIN:NKIN + 1], rxf[:, 0:1]).then_inc(ctx.sem["eV"], 1)
        w("eV", eV0 + 7)
        e.scalar_tensor_tensor(rxf[:, 0:1], ccol[:, 4 + slot:5 + slot],
                               float(n_bo * 64), iotc[:], ALU.mult,
                               ALU.add).then_inc(ctx.sem["eV"], 1)
        w("eV", eV0 + 8)
        e.tensor_copy(rxi[:, NKIN + 1:NKIN + 2], rxf[:, 0:1]).then_inc(ctx.sem["eV"], 1)
    ctx.body("vector", f_v0)
    ctx.cnt["eV"] = eV0 + 9
    eVp = ctx.cnt["eV"]

    s1d = ctx.dram[SIN].rearrange("b r c -> (b r) c")
    s2d = ctx.dram[SOUT].rearrange("b r c -> (b r) c")
    g0 = ctx.cnt["qG"]
    n_sch = n_bi // 2 + n_bo // 2

    def f_gp1(e):
        w = WT(ctx, e)
        w("eV", eVp)
        j = 0
        for (src, n2, icol) in ((s1d, n_bi // 2, NKIN), (s2d, n_bo // 2, NKIN + 1)):
            for c in range(n2):
                if j >= 1:
                    w("eV", eVp + 3 * j)
                e.indirect_dma_start(
                    sgf[:, :], None, src,
                    IndirectOffsetOnAxis(ap=rxi[:, icol:icol + 1], axis=0),
                    element_offset=c * 128 * 64,
                ).then_inc(ctx.sem["qG"], 16)
                j += 1
    ctx.body("gpsimd", f_gp1)
    ctx.cnt["qG"] = g0 + 16 * n_sch

    def f_v1(e):
        w = WT(ctx, e)
        w("eG", eG0 + n_ms)
        j = 0
        for (plist, n2) in ((s1p, n_bi // 2), (s2p, n_bo // 2)):
            for c in range(n2):
                w("qG", g0 + 16 * (j + 1))
                e.tensor_copy(sgb[:], sgf[:]).then_inc(ctx.sem["eV"], 1)
                w("eV", eVp + 3 * j + 1)
                e.tensor_copy(plist[c][0:64, 0:64], sgb[0:64, :]).then_inc(ctx.sem["eV"], 1)
                w("eV", eVp + 3 * j + 2)
                e.tensor_copy(plist[c][64:128, 64:128], sgb[64:128, :]).then_inc(ctx.sem["eV"], 1)
                j += 1
    ctx.body("vector", f_v1)
    ctx.cnt["eV"] = eVp + 3 * n_sch
    eVs = ctx.cnt["eV"]

    g1 = ctx.cnt["qG"]
    mtd = ctx.dram[MIXT + str(k)]

    def f_gp2(e):
        w = WT(ctx, e)
        w("eV", eVp)
        w("qG", g1)
        for c in range(NKIN):
            if c >= 1:
                w("qG", g1 + 16 * c)
            e.indirect_dma_start(
                weff[:, c * MIX_M:(c + 1) * MIX_M], None, mtd,
                IndirectOffsetOnAxis(ap=rxi[:, c:c + 1], axis=0),
            ).then_inc(ctx.sem["qG"], 16)
    ctx.body("gpsimd", f_gp2)
    ctx.cnt["qG"] = g1 + 16 * NKIN
    weff_tgt = ctx.cnt["qG"]

    b2 = dict(ctx.cnt)
    eT20, eS20, eV20 = b2["eT"], b2["eS"], b2["eV"]
    st0 = b2["qG"]
    qA20 = b2["qA"]
    qS20 = b2["qS"]
    ld1 = [qS20 + 16 * NKIN * (ti + 1) for ti in range(NT)]
    ctx.cnt["qS"] = qS20 + 16 * NKIN * NT
    need_res = not (fc and k == 0)
    rld = [qA20 + 16 * NMO * (ti + 1) for ti in range(NT)] if need_res else None
    if need_res:
        ctx.cnt["qA"] = qA20 + 16 * NMO * NT
    res_name = {"fc1": "hmid", "pj0": "u", "pj1": "ffn1a"}.get(
        ("fc" if fc else "pj") + str(k))
    resd = ctx.dram[res_name] if need_res else None
    outd = ctx.dram[{"fc0": "hmid", "fc1": "h2b", "pj0": "ffn1a", "pj1": "ffn1"}[
        ("fc" if fc else "pj") + str(k)]]

    NTt = NKIN + NMO * (NKIN and 1) + NMO
    NTt = NKIN + 2 * NMO
    NSt = NKIN + NMO + (NMO if (fc and k == 1) else 0)
    NVt = NMO * (2 if (fc and k == 1) else (1 if need_res else 1))
    NVt = NMO * (2 if (fc and k == 1) else 1)
    ctx.cnt["eT"] = eT20 + NTt * NT
    ctx.cnt["eS"] = eS20 + NSt * NT
    ctx.cnt["eV"] = eV20 + NVt * NT
    ctx.cnt["qG"] = st0 + 16 * NMO * NT

    def f_sync(e):
        e.wait_ge(ctx.sem["qS"], qS20)
        for ti in range(NT):
            if ti >= 1:
                e.wait_ge(ctx.sem["qS"], ld1[ti - 1])
                e.wait_ge(ctx.sem["eT"], eT20 + NTt * (ti - 1) + NKIN)
            for c in range(NKIN):
                e.dma_start(out=int_[c][:], in_=ind[c * 128:(c + 1) * 128, ti * TC:(ti + 1) * TC]).then_inc(ctx.sem["qS"], 16)
    ctx.body("sync", f_sync)

    def f_tensor(e):
        w = WT(ctx, e)
        w("eV", eVs)
        w("qG", weff_tgt)
        for ti in range(NT):
            bt = eT20 + NTt * ti
            bs = eS20 + NSt * ti
            w("qS", ld1[ti])
            nvper = NVt // NMO
            nsper = 2 if (fc and k == 1) else 1
            bv = eV20 + NVt * ti
            for c in range(NKIN):
                if ti * NKIN + c >= 1:
                    w("eS", bs + c if c >= 1 else eS20 + NSt * (ti - 1) + NKIN)
                e.matmul(p_1[:], s1p[c][:], int_[c][:], start=True, stop=True,
                         skip_group_check=True).then_inc(ctx.sem["eT"], 1)
            for mo in range(NMO):
                it = ti * NMO + mo
                w("eS", bs + NKIN)
                if mo >= 2:
                    w("eS", bs + NKIN + nsper * (mo - 2) + 1)
                ins = None
                for kc in range(NKIN):
                    ins = e.matmul(p_m[mo % 2][:], weff[:, kc * MIX_M + mo * 128: kc * MIX_M + mo * 128 + 128],
                                   h1[kc][:], start=(kc == 0), stop=(kc == NKIN - 1),
                                   skip_group_check=True)
                ins.then_inc(ctx.sem["eT"], 1)
                w("eS", bs + NKIN + nsper * mo + 1)
                if mo >= 2:
                    w("eV", bv + nvper * (mo - 1))
                elif ti >= 1:
                    w("eV", eV20 + NVt * ti)
                e.matmul(p_2[mo % 2][:], s2p[mo][:], mt[:], start=True, stop=True,
                         skip_group_check=True).then_inc(ctx.sem["eT"], 1)
    ctx.body("tensor", f_tensor)


    def f_scalar(e):
        w = WT(ctx, e)
        if need_res:
            w("qA", qA20)
        nvper = NVt // NMO
        for ti in range(NT):
            bt = eT20 + NTt * ti
            bs = eS20 + NSt * ti
            bv = eV20 + NVt * ti
            for c in range(NKIN):
                w("eT", bt + c + 1)
                e.activation(out=h1[c][:], in_=p_1[:], func=AF.Copy).then_inc(ctx.sem["eS"], 1)
            for mo in range(NMO):
                it = ti * NMO + mo
                if need_res:
                    if it >= 1:
                        w("qA", qA20 + 16 * it)
                    if it >= 2:
                        w("eV", eV20 + nvper * (it - 2) + 1)
                    e.dma_start(out=rt[it % 2][:],
                                in_=resd[mo * 128:(mo + 1) * 128, ti * TC:(ti + 1) * TC]).then_inc(ctx.sem["qA"], 16)
                w("eT", bt + NKIN + 2 * mo + 1)
                e.activation(out=mt[:], in_=p_m[mo % 2][:], func=AF.Copy).then_inc(ctx.sem["eS"], 1)
                if fc and k == 1:
                    w("eV", bv + 2 * mo + 2)
                    e.activation(out=obb[it % 3][:], in_=ob[it % 3][:],
                                 func=AF.Square).then_inc(ctx.sem["eS"], 1)
    ctx.body("scalar", f_scalar)

    def f_vec(e):
        w = WT(ctx, e)
        for ti in range(NT):
            bt = eT20 + NTt * ti
            bv = eV20 + NVt * ti
            for mo in range(NMO):
                it = ti * NMO + mo
                w("eT", bt + NKIN + 2 * mo + 2)
                if need_res:
                    w("qA", qA20 + 16 * (it + 1))
                if it >= 3:
                    w("qG", st0 + 16 * (it - 2))
                if fc and k == 0:
                    e.tensor_scalar_mul(ob[it % 3][:], p_2[mo % 2][:],
                                        ccol[:, slot:slot + 1]).then_inc(ctx.sem["eV"], 1)
                elif fc and k == 1:
                    e.scalar_tensor_tensor(ob[it % 3][:], p_2[mo % 2][:],
                                           ccol[:, slot:slot + 1], rt[it % 2][:],
                                           ALU.mult, ALU.add).then_inc(ctx.sem["eV"], 1)
                    w("eV", bv + 2 * mo + 1)
                    e.tensor_scalar_max(ob[it % 3][:], ob[it % 3][:], 0.0).then_inc(ctx.sem["eV"], 1)
                else:
                    e.scalar_tensor_tensor(ob[it % 3][:], p_2[mo % 2][:],
                                           ccol[:, slot:slot + 1], rt[it % 2][:],
                                           ALU.mult, ALU.add).then_inc(ctx.sem["eV"], 1)
    ctx.body("vector", f_vec)

    def f_gp(e):
        w = WT(ctx, e)
        for ti in range(NT):
            bv = eV20 + NVt * ti
            bs = eS20 + NSt * ti
            for mo in range(NMO):
                it = ti * NMO + mo
                if fc and k == 1:
                    w("eS", bs + NKIN + 2 * mo + 2)
                    srcb = obb[it % 3]
                else:
                    w("eV", bv + (mo + 1) * (NVt // NMO))
                    srcb = ob[it % 3]
                if it >= 1:
                    w("qG", st0 + 16 * it)
                e.dma_start(out=outd[mo * 128:(mo + 1) * 128, ti * TC:(ti + 1) * TC],
                            in_=srcb[:]).then_inc(ctx.sem["qG"], 16)
    ctx.body("gpsimd", f_gp)


def stage_final(ctx):
    ident = ctx.dram["identity_f32"]
    src = ctx.dram["fsum"]
    dst = ctx.dram["out"]
    idt = ctx.sb((128, 128), F32, "id")
    tin = [ctx.sb((128, TC), F32, "ti") for _ in range(2)]
    tout = [ctx.sb((128, 128), F32, "to") for _ in range(4)]
    pss = [ctx.psum((128, 128), F32, "pt") for _ in range(2)]
    b = dict(ctx.cnt)
    qS0, qG0, eT0, eS0 = b["qS"], b["qG"], b["eT"], b["eS"]
    NTT = T // TC
    n_tiles = NKD * NTT * 4
    ld = []
    cur = qS0 + 16
    for j in range(NKD * NTT):
        cur += 16
        ld.append(cur)
    ctx.cnt["qS"] = cur
    ctx.cnt["eT"] = eT0 + n_tiles
    ctx.cnt["eS"] = eS0 + n_tiles
    ctx.cnt["qG"] = qG0 + 16 * n_tiles

    def f_sync(e):
        e.wait_ge(ctx.sem["qS"], qS0)
        e.dma_start(out=idt[:], in_=ident[:, :]).then_inc(ctx.sem["qS"], 16)
        j = 0
        for fch in range(NKD):
            for tc in range(NTT):
                if j >= 1:
                    e.wait_ge(ctx.sem["qS"], ld[j - 1])
                if j >= 2:
                    e.wait_ge(ctx.sem["eT"], eT0 + 4 * (j - 1))
                e.dma_start(out=tin[j % 2][:],
                            in_=src[fch * 128:(fch + 1) * 128, tc * TC:(tc + 1) * TC]).then_inc(ctx.sem["qS"], 16)
                j += 1
    ctx.body("sync", f_sync)

    def f_tensor(e):
        w = WT(ctx, e)
        j = 0
        for fch in range(NKD):
            for tc in range(NTT):
                w("qS", ld[j])
                for q in range(4):
                    kk = 4 * j + q
                    if kk >= 2:
                        w("eS", eS0 + kk - 1)
                    e.transpose(pss[kk % 2][:], tin[j % 2][:, q * 128:(q + 1) * 128],
                                idt[:]).then_inc(ctx.sem["eT"], 1)
                j += 1
    ctx.body("tensor", f_tensor)

    def f_scalar(e):
        w = WT(ctx, e)
        for kk in range(n_tiles):
            w("eT", eT0 + kk + 1)
            if kk >= 4:
                w("qG", qG0 + 16 * (kk - 3))
            e.copy(tout[kk % 4][:], pss[kk % 2][:]).then_inc(ctx.sem["eS"], 1)
    ctx.body("scalar", f_scalar)

    def f_gp(e):
        kk = 0
        for fch in range(NKD):
            for tc in range(NTT):
                for q in range(4):
                    e.wait_ge(ctx.sem["eS"], eS0 + kk + 1)
                    if kk >= 1:
                        e.wait_ge(ctx.sem["qG"], qG0 + 16 * kk)
                    t0 = tc * TC + q * 128
                    e.dma_start(out=dst[t0:t0 + 128, fch * 128:(fch + 1) * 128],
                                in_=tout[kk % 4][:]).then_inc(ctx.sem["qG"], 16)
                    kk += 1
    ctx.body("gpsimd", f_gp)


STAGES = {}


def _stage(name, fn, reads, writes, uses):
    STAGES[name] = dict(fn=fn, reads=list(reads), writes=list(writes), uses=list(uses))


_stage("ingest", stage_ingest, ["x_in"], ["x_fm"], ["identity_f32"])
_stage("ln0", lambda c: stage_ln(c, "x_fm", ["xn0", "xs0"], ["xn0b", "xs0b"]),
       ["x_fm"], ["xn0", "xs0", "xn0b", "xs0b"], ["zeros128"])
_stage("qkv0", stage_qkv0, ["xn0b"], ["q0b", "k0b"], ["L0qkw", "L0qkb", "zeros128b"])
_stage("vproj0", lambda c: stage_vproj(c, c.dram["L0wv"], "L0bv", "xn0b", "v0t"),
       ["xn0b"], ["v0t"], ["L0wv", "L0bv"])
_stage("cprep0", lambda c: stage_attnprep(c, "q0b", "k0b", "c0b", ind_q="indH",
                                          ind_k="indH", nh_k=H),
       ["q0b", "k0b"], ["c0b"], ["indH"])
_stage("attn0", lambda c: stage_attention(c, "q0b", "k0b", "v0t", "c0b", "ap0",
                                          mode="causal"),
       ["q0b", "k0b", "v0t", "c0b"], ["ap0"], ["cmask"])
_stage("wo0", lambda c: stage_wo(c, "L0wo", "L0bo", "ap0", "attn0"),
       ["ap0"], ["attn0"], ["L0wo", "L0bo", "zeros128b"])

_reg("svt", (T, D), BFNP)

for _i in range(6):
    _stage(f"s{_i}qkv", (lambda c, i=_i: stage_subqkv(c, i)),
           [f"xs{2 * _i}b"], ["sqr", "skr", "svf"], ["S_wqkv", "zeros128b"])
    _stage(f"s{_i}vtr", (lambda c, i=_i: stage_vtr(c, "svf", "svt")),
           ["svf"], ["svt"], [])
    _stage(f"s{_i}rope", (lambda c, i=_i: stage_rope(c, i, "xpos")),
           ["sqr", "skr"], ["sq", "sk"],
           ["S_qs", "S_ks", "xpos_cq", "xpos_sq", "xpos_ck", "xpos_sk",
            "indPT", "indP", "swap64"])
    _stage(f"s{_i}prep", (lambda c, i=_i: stage_attnprep(
               c, "sq", "sk", "scb", ind_q="indP", ind_k="indP", nh_k=H,
               a_const=("S_qs", _i, 8.0))),
           ["sq", "sk"], ["scb"], ["indP", "S_qs", "S_ks"])
    _stage(f"s{_i}att", (lambda c, i=_i: stage_attention(c, "sq", "sk", "svt", "scb",
                                                         "sap", mode="local",
                                                         perm_qk=True)),
           ["sq", "sk", "svt", "scb"], ["sap"], ["lmask"])
    _stage(f"s{_i}wo", (lambda c, i=_i: stage_subwo(c, i)),
           ["sap", f"xs{2 * _i}"], [f"xs{2 * _i + 1}", f"xs{2 * _i + 1}b"],
           ["S_wo", "zeros128b"])
    _stage(f"s{_i}f1", (lambda c, i=_i: stage_subf1(c, i)),
           [f"xs{2 * _i + 1}"], ["gact", "augrd"],
           ["S_w1", "S_g", "S_b", "S_b1", "zeros128"])
    _stage(f"s{_i}f2", (lambda c, i=_i: stage_subf2(c, i)),
           ["gact", f"xs{2 * _i + 1}"], [f"xs{2 * _i + 2}", f"xs{2 * _i + 2}b"],
           ["S_w2", "S_b2", "zeros128b"])

_stage("resid0", lambda c: stage_add3(c, "x_fm", "attn0", "xs12", "x1"),
       ["x_fm", "attn0", "xs12"], ["x1"], [])
_stage("ln1", lambda c: stage_ln(c, "x1", ["xn1"], ["xn1b"]),
       ["x1"], ["xn1", "xn1b"], ["zeros128"])
_stage("rms1", lambda c: stage_rms(c, "xn1", None, out_row="r1row"),
       ["xn1"], ["r1row"], [])
_stage("qkv1", lambda c: stage_qk(c, c.dram["L1qkv"][:, 0:2048], 2048, "q1b",
                                  "k1b", rhs_name="xn1b"),
       ["xn1b"], ["q1b", "k1b"], ["L1qkv", "zeros128b"])
_stage("vproj1", lambda c: stage_vproj(c, c.dram["L1qkv"][:, 2048:3072], None,
                                       "xn1b", "v1t"),
       ["xn1b"], ["v1t"], ["L1qkv"])
_stage("cprep1", lambda c: stage_attnprep(c, "q1b", "k1b", "c1m", ind_q="indH",
                                          ind_k="indH", nh_k=H,
                                          variants=[("c1n", 15), ("c1p", 31)]),
       ["q1b", "k1b"], ["c1m", "c1n", "c1p"], ["indH", "relT"])
_stage("tbias", stage_tbias, [], ["grev", "biasT"], ["relT", "rel32", "OHrev"])
_stage("attn1", lambda c: stage_attention(c, "q1b", "k1b", "v1t", "c1m", "ap1",
                                          mode="full",
                                          bias=("biasT", "c1n", "c1p")),
       ["q1b", "k1b", "v1t", "c1m", "c1n", "c1p", "biasT"], ["ap1"], [])
_stage("wo1", lambda c: stage_wo(c, "L1wo", None, "ap1", "attn1"),
       ["ap1"], ["attn1"], ["L1wo", "zeros128b"])


def stage_wo2(ctx):
    wsb, wrdy = load_weights_resident(ctx, ctx.dram["woT"], D, D, "wo2")

    def epi(info):
        epi_resid_store(ctx, info, "xn1", "u", None)
    gemm_fm(ctx, wsb=wsb, w_ready=wrdy, NKC=NKD, MC=8,
            rhs_dram=ctx.dram["ap2"], epi_emit=epi)


_stage("wtr2", stage_wtr2, [], ["wqB", "wqT", "wkB", "wkT", "wvB", "wvT", "woB", "woT"],
       ["F_wq", "F_wk", "F_wv", "F_wo"])
_stage("qproj2", lambda c: stage_proj_fm(c, "wqT", 1024, "q2r", "xn1b"),
       ["wqT", "xn1b"], ["q2r"], ["zeros128b"])
_stage("kproj2", lambda c: stage_proj_fm(c, "wkT", 256, "k2r", "xn1b"),
       ["wkT", "xn1b"], ["k2r"], ["zeros128b"])
_stage("vproj2", lambda c: stage_vproj(c, c.dram["wvT"], None, "xn1b", "v2t",
                                       scale_row="r1row"),
       ["wvT", "xn1b", "r1row"], ["v2t"], [])
_stage("rope2", stage_rope2, ["q2r", "k2r"], ["q2b", "k2b"],
       ["F_qg", "F_kg", "rope1_cq", "rope1_sq", "rope1_ck", "rope1_sk",
        "indPT", "indP", "indKT", "indK", "swap64"])
_stage("cprep2", lambda c: stage_attnprep(c, "q2b", "k2b", "c2b", ind_q="indP",
                                          ind_k="indK", nh_k=4, a_const=("gqa",)),
       ["q2b", "k2b"], ["c2b"], ["indP", "indK", "F_qg", "F_kg"])
_stage("attn2", lambda c: stage_attention(c, "q2b", "k2b", "v2t", "c2b", "ap2",
                                          mode="causal", perm_qk=True, nkv=4,
                                          heads_per_kv=4),
       ["q2b", "k2b", "v2t", "c2b"], ["ap2"], ["cmask"])
_stage("wo2", stage_wo2, ["ap2", "xn1", "woT"], ["u"], ["zeros128b"])
_stage("rmsu", lambda c: stage_rms(c, "u", "hb"), ["u"], ["hb"], [])
_stage("gate", stage_gate, [], ["gatev", "gidx"], ["F_flog", "F_plog"])
_stage("mixprep", stage_mixprep, ["gidx"],
       ["fmixB", "fmixT0", "fmixT1", "pmixB", "pmixT0", "pmixT1"],
       ["F_fmix", "F_pmix"])
_stage("fcm0", lambda c: stage_member(c, "fc", 0),
       ["hb", "gatev", "gidx", "fmixT0"], ["hmid"],
       ["F_fs1", "F_fs2", "iotaA_fc", "iotaB_fc"])
_stage("fcm1", lambda c: stage_member(c, "fc", 1),
       ["hb", "gatev", "gidx", "fmixT1", "hmid"], ["h2b"],
       ["F_fs1", "F_fs2", "iotaA_fc", "iotaB_fc"])
_stage("pjm0", lambda c: stage_member(c, "pj", 0),
       ["h2b", "gatev", "gidx", "pmixT0", "u"], ["ffn1a"],
       ["F_ps1", "F_ps2", "iotaA_pj", "iotaB_pj"])
_stage("pjm1", lambda c: stage_member(c, "pj", 1),
       ["h2b", "gatev", "gidx", "pmixT1", "ffn1a"], ["ffn1"],
       ["F_ps1", "F_ps2", "iotaA_pj", "iotaB_pj"])
_stage("fadd", lambda c: stage_add3(c, "x1", "attn1", "ffn1", "fsum"),
       ["x1", "attn1", "ffn1"], ["fsum"], [])
_stage("final", stage_final, ["fsum"], ["out"], ["identity_f32"])


def _subgroup(i):
    return [f"s{i}qkv", f"s{i}vtr", f"s{i}rope", f"s{i}prep", f"s{i}att",
            f"s{i}wo", f"s{i}f1", f"s{i}f2"]


GROUPS = [
    ["ingest", "ln0"],
    ["qkv0", "vproj0", "cprep0"],
    ["attn0", "wo0"],
] + [_subgroup(i) for i in range(6)] + [
    ["resid0", "ln1", "rms1"],
    ["qkv1", "vproj1", "cprep1", "tbias"],
    ["attn1", "wo1"],
    ["wtr2", "qproj2", "kproj2", "vproj2"],
    ["rope2", "cprep2"],
    ["attn2", "wo2"],
    ["rmsu", "gate", "mixprep"],
    ["fcm0", "fcm1"],
    ["pjm0", "pjm1"],
    ["fadd", "final"],
]


def _mybir_dt(np_dtype):
    if np_dtype == np.float32:
        return F32
    if np_dtype == BFNP:
        return BF16
    if np_dtype == np.int32:
        return I32
    raise ValueError(np_dtype)


def build_group(stage_names, feed, mode, export=None):
    reads, writes, uses = set(), set(), set()
    for sn in stage_names:
        st = STAGES[sn]
        for r in st["reads"]:
            if r not in writes and r != "x_in":
                reads.add(r)
        writes.update(st["writes"])
        uses.update(st["uses"])
        if "x_in" in st["reads"]:
            uses.add("x_in")
    nc = bass.Bass("TRN2", target_bir_lowering=False)
    aps = {}
    for name in sorted(uses):
        if name == "x_in":
            aps[name] = nc.dram_tensor("x_in", [T, D], F32, kind="ExternalInput").ap()
        else:
            arr = feed[name]
            aps[name] = nc.dram_tensor(name, list(arr.shape), _mybir_dt(arr.dtype),
                                       kind="ExternalInput").ap()
    for name in sorted(reads):
        shape, dt = SCRATCH[name]
        aps[name] = nc.dram_tensor(name, list(shape), _mybir_dt(dt), kind="ExternalInput").ap()
    if export is None:
        export = set(writes) if mode == "multi" else {"out"}
    for name in sorted(writes):
        shape, dt = SCRATCH[name]
        kind = "ExternalOutput" if name in export else "Internal"
        aps[name] = nc.dram_tensor(name, list(shape), _mybir_dt(dt), kind=kind).ap()

    with ExitStack() as stack:
        with nc.Block() as block:
            ctx = Ctx(nc, block, stack)
            ctx.dram = aps
            for sn in stage_names:
                ctx.sstack = ExitStack()
                with ctx.sstack:
                    STAGES[sn]["fn"](ctx)
                    emit_barrier(ctx)
    in_names = sorted(uses) + sorted(reads)
    out_names = sorted(writes & export)
    return nc, in_names, out_names


def run_groups(groups, inputs, n_cores=8, scratch=None, keep=("out",),
               mode="chain"):
    import gc
    consts = make_consts()
    w = make_weights(inputs)
    feed = {**consts, **w}
    x = np.ascontiguousarray(np.asarray(inputs["L0_attn_x"], np.float32))
    free_dead = scratch is None
    if scratch is None:
        scratch = [dict() for _ in range(n_cores)]
    keep = set(keep)
    last_use = {}
    for gi, grp in enumerate(groups):
        for sn in grp:
            for r in STAGES[sn]["reads"]:
                last_use[r] = gi
    exports = [None] * len(groups)
    if mode == "chain":
        need_after = {"out"}
        for gi in range(len(groups) - 1, -1, -1):
            wr = set()
            rd = set()
            for sn in groups[gi]:
                wr.update(STAGES[sn]["writes"])
                rd.update(STAGES[sn]["reads"])
            exports[gi] = wr & need_after
            need_after |= rd
    memlog = os.environ.get("KMEMLOG")
    for gi, grp in enumerate(groups):
        nc, in_names, out_names = build_group(grp, feed, mode,
                                              export=exports[gi])
        in_maps = []
        for c in range(n_cores):
            m = {}
            for nm in in_names:
                if nm == "x_in":
                    m[nm] = x[c]
                elif nm in feed:
                    m[nm] = feed[nm]
                else:
                    m[nm] = scratch[c][nm]
            in_maps.append(m)
        res = run_bass_kernel_spmd(nc, in_maps, core_ids=list(range(n_cores)))
        for c in range(n_cores):
            for nm in out_names:
                shape, dt = SCRATCH[nm]
                a = np.asarray(res.results[c][nm])
                if dt == BFNP and a.dtype != BFNP:
                    a = a.view(BFNP)
                scratch[c][nm] = np.array(a.reshape(shape))
        del res, in_maps, nc
        if free_dead:
            for c in range(n_cores):
                for nm in list(scratch[c]):
                    if nm not in keep and last_use.get(nm, -1) <= gi:
                        del scratch[c][nm]
        gc.collect()
        if memlog:
            import resource
            print(f"[mem] group {gi} {grp[0]}.. rss="
                  f"{resource.getrusage(resource.RUSAGE_SELF).ru_maxrss / 1e6:.1f}GB",
                  flush=True)
    return scratch


MGROUPS = [
    [s for g in GROUPS[0:3] for s in g],
    [s for g in GROUPS[3:9] for s in g],
    [s for g in GROUPS[9:12] for s in g],
    [s for g in GROUPS[12:19] for s in g],
]


def kernel(**inputs):
    n_cores = 8
    scratch = run_groups(MGROUPS, inputs, n_cores=n_cores)
    out = np.stack([scratch[c]["out"].astype(np.float32) for c in range(n_cores)])
    return out


# version 04 (final)
# speedup vs baseline: 1.1561x; 1.1561x over previous
import math
import os
from contextlib import ExitStack

import numpy as np
import ml_dtypes

import concourse.bass as bass
import concourse.mybir as mybir
from concourse.bass_utils import run_bass_kernel_spmd
from concourse.bass_core_rust import IndirectOffsetOnAxis

F32 = mybir.dt.float32
BF16 = mybir.dt.bfloat16
I32 = mybir.dt.int32
AF = mybir.ActivationFunctionType
ALU = mybir.AluOpType
AX = mybir.AxisListType
BFNP = ml_dtypes.bfloat16

T = 4096
D = 1024
TC = 512
NT = T // TC
NKD = D // 128
H = 16
HD = 64

MODE = "fused"

ENGS = ["sync", "tensor", "scalar", "vector", "gpsimd"]
_QUEUE_OF = {"sync": "qS", "gpsimd": "qG", "scalar": "qA"}

SCRATCH = {}


def _reg(name, shape, dtype):
    SCRATCH[name] = (tuple(shape), dtype)
    return name


_reg("x_fm", (D, T), np.float32)
_reg("xn0", (D, T), np.float32)
_reg("xn0b", (D, T), BFNP)
_reg("q0b", (D, T), BFNP)
_reg("k0b", (D, T), BFNP)
_reg("v0t", (T, D), BFNP)
_reg("c0b", (H, T), BFNP)
_reg("ap0", (D, T), BFNP)
_reg("attn0", (D, T), np.float32)
for i in range(13):
    _reg(f"xs{i}", (D, T), np.float32)
    _reg(f"xs{i}b", (D, T), BFNP)
_reg("sqr", (D, T), BFNP)
_reg("skr", (D, T), BFNP)
_reg("svf", (D, T), BFNP)
_reg("gact", (2816, T), BFNP)
_reg("augrd", (2, T), BFNP)
_reg("x1", (D, T), np.float32)
_reg("sq", (D, T), BFNP)
_reg("sk", (D, T), BFNP)
_reg("svt", (T, D), BFNP)
_reg("scb", (H, T), BFNP)
_reg("sap", (D, T), BFNP)
_reg("r1row", (1, T), np.float32)
_reg("xn1", (D, T), np.float32)
_reg("xn1b", (D, T), BFNP)
_reg("q1b", (D, T), BFNP)
_reg("k1b", (D, T), BFNP)
_reg("v1t", (T, D), BFNP)
_reg("c1m", (H, T), BFNP)
_reg("c1n", (H, T), BFNP)
_reg("c1p", (H, T), BFNP)
_reg("grev", (H, 1280), np.float32)
_reg("biasT", (H, 6, 128, TC), np.float32)
_reg("ap1", (D, T), BFNP)
_reg("attn1", (D, T), np.float32)
_reg("wqB", (1024, 1024), BFNP)
_reg("wqT", (1024, 1024), BFNP)
_reg("wkB", (256, 1024), BFNP)
_reg("wkT", (1024, 256), BFNP)
_reg("wvB", (256, 1024), BFNP)
_reg("wvT", (1024, 256), BFNP)
_reg("woB", (1024, 1024), BFNP)
_reg("woT", (1024, 1024), BFNP)
_reg("q2r", (D, T), BFNP)
_reg("k2r", (256, T), BFNP)
_reg("q2b", (D, T), BFNP)
_reg("k2b", (256, T), BFNP)
_reg("v2t", (T, 256), BFNP)
_reg("c2b", (H, T), BFNP)
_reg("ap2", (D, T), BFNP)
_reg("u", (D, T), np.float32)
_reg("gatev", (2, 2), np.float32)
_reg("gidx", (2, 2), np.float32)
_reg("fmixB", (4096, 1024), BFNP)
_reg("fmixT0", (1024, 4096), BFNP)
_reg("fmixT1", (1024, 4096), BFNP)
_reg("pmixB", (1024, 4096), BFNP)
_reg("pmixT0", (4096, 1024), BFNP)
_reg("pmixT1", (4096, 1024), BFNP)
_reg("hb", (D, T), BFNP)
_reg("hmid", (4096, T), np.float32)
_reg("h2b", (4096, T), BFNP)
_reg("ffn1a", (D, T), np.float32)
_reg("ffn1", (D, T), np.float32)
_reg("fsum", (D, T), np.float32)
_reg("out", (T, D), np.float32)



def _pair_perm(nrows):
    perm = np.zeros(nrows, dtype=np.int64)
    for c in range(nrows // 128):
        h0, h1 = 2 * c, 2 * c + 1
        b = c * 128
        perm[b + 0:b + 32] = h0 * 64 + np.arange(32)
        perm[b + 32:b + 64] = h1 * 64 + np.arange(32)
        perm[b + 64:b + 96] = h0 * 64 + 32 + np.arange(32)
        perm[b + 96:b + 128] = h1 * 64 + 32 + np.arange(32)
    return perm


ROPE_PERM = _pair_perm(D)
ROW_D = (ROPE_PERM % 64).astype(np.int64)
KPERM4 = _pair_perm(256)


def make_consts():
    c = {}
    c["identity_f32"] = np.eye(128, dtype=np.float32)
    c["zeros128"] = np.zeros((128, 1), dtype=np.float32)
    c["zeros128b"] = np.zeros((128, 1), dtype=np.float32).astype(BFNP)

    half = np.arange(0, HD, 2, dtype=np.float64)
    inv_freq = 1.0 / 10000.0 ** (half / HD)
    t = np.arange(T, dtype=np.float64)
    freqs = np.concatenate([t[:, None] * inv_freq[None, :]] * 2, -1)
    base_scale = (half + 0.4 * HD) / (1.4 * HD)
    power = (t - T // 2) / 128.0
    scale = np.concatenate([base_scale[None, :] ** power[:, None]] * 2, -1)
    cos, sin = np.cos(freqs), np.sin(freqs)
    dr = ROW_D[:128]
    sgn_x = np.where(np.arange(128) % 128 < 64, -1.0, 1.0)[:, None]
    c["xpos_cq"] = (8.0 * cos * scale)[:, dr].T.astype(np.float32)
    c["xpos_sq"] = (sgn_x * (8.0 * sin * scale)[:, dr].T).astype(np.float32)
    c["xpos_ck"] = (cos / scale)[:, dr].T.astype(np.float32)
    c["xpos_sk"] = (sgn_x * (sin / scale)[:, dr].T).astype(np.float32)

    fr = t[:, None] * (1.0 / 10000.0 ** (np.arange(0, HD, 2, dtype=np.float64) / HD))[None, :]
    d32 = dr % 32
    sgn_r = -sgn_x
    c["rope1_cq"] = (0.125 * np.cos(fr))[:, d32].T.astype(np.float32)
    c["rope1_sq"] = (sgn_r * (0.125 * np.sin(fr))[:, d32].T).astype(np.float32)
    c["rope1_ck"] = np.cos(fr)[:, d32].T.astype(np.float32)
    c["rope1_sk"] = (sgn_r * np.sin(fr)[:, d32].T).astype(np.float32)

    sw = np.zeros((128, 128), dtype=np.float32)
    for r in range(128):
        sw[r ^ 64, r] = 1.0
    c["swap64"] = sw.astype(BFNP)

    indH = np.zeros((NKD, 128, H), dtype=np.float32)
    for ch in range(NKD):
        for r in range(128):
            indH[ch, r, (ch * 128 + r) // 64] = 1.0
    c["indH"] = indH.astype(BFNP)
    indP = np.zeros((NKD, 128, H), dtype=np.float32)
    for ch in range(NKD):
        for r in range(128):
            indP[ch, r, 2 * ch + (1 if (32 <= r < 64 or 96 <= r < 128) else 0)] = 1.0
    c["indP"] = indP.astype(BFNP)
    indK = np.zeros((2, 128, 4), dtype=np.float32)
    for ch in range(2):
        for r in range(128):
            indK[ch, r, 2 * ch + (1 if (32 <= r < 64 or 96 <= r < 128) else 0)] = 1.0
    c["indK"] = indK.astype(BFNP)
    c["indPT"] = np.ascontiguousarray(indP.transpose(0, 2, 1)).astype(BFNP)
    c["indKT"] = np.ascontiguousarray(indK.transpose(0, 2, 1)).astype(BFNP)

    kk = np.arange(128)[:, None]
    qq = np.arange(TC)[None, :]
    cm = np.zeros((4, 128, TC), dtype=np.float32)
    for j, d0 in enumerate((0, 128, 256, 384)):
        cm[j] = ((kk + d0) <= qq)
    c["cmask"] = cm.astype(BFNP)
    qq2 = np.arange(256)[None, :]
    lm = np.zeros((4, 128, 256), dtype=np.float32)
    lm[0] = (qq2 <= kk)
    lm[1] = (qq2 <= kk + 128)
    lm[2] = (kk <= qq2)
    lm[3] = (kk + 128 <= qq2)
    c["lmask"] = lm.astype(BFNP)

    def bucket(d):
        nb = 16
        rb = nb if d > 0 else 0
        ad = abs(d)
        mx = nb // 2
        if ad < mx:
            return rb + ad
        rl = mx + int(math.log(ad / mx) / math.log(128 / mx) * (nb - mx))
        return rb + min(rl, nb - 1)
    oh = np.zeros((32, 255), dtype=np.float32)
    for i in range(255):
        oh[bucket(127 - i), i] = 1.0
    c["OHrev"] = oh

    f = np.arange(1024)
    c["iotaA_fc"] = (16 * (f % 64)).astype(np.float32)
    c["iotaB_fc"] = (f // 64).astype(np.float32)
    f4 = np.arange(4096)
    c["iotaA_pj"] = (64 * (f4 % 64)).astype(np.float32)
    c["iotaB_pj"] = (f4 // 64).astype(np.float32)
    return c


def make_weights(inp):
    w = {}

    def A(x):
        return np.ascontiguousarray(np.asarray(x, dtype=np.float32))

    w["L0qkw"] = np.ascontiguousarray(np.concatenate([A(inp["L0_attn_Wq"]), A(inp["L0_attn_Wk"])], axis=1))
    w["L0qkb"] = np.concatenate([A(inp["L0_attn_bq"]), A(inp["L0_attn_bk"])])
    w["L0wv"] = A(inp["L0_attn_Wv"])
    w["L0bv"] = A(inp["L0_attn_bv"])
    w["L0wo"] = A(inp["L0_attn_Wo"])
    w["L0bo"] = A(inp["L0_attn_bo"])

    Wqkv = A(inp["L0_ffn_Wqkv"])
    perm_cols = np.concatenate([ROPE_PERM, 1024 + ROPE_PERM, 2048 + np.arange(1024)])
    w["S_wqkv"] = np.ascontiguousarray(Wqkv[:, :, perm_cols])
    w["S_qs"] = np.ascontiguousarray(A(inp["L0_ffn_q_scale"])[:, ROW_D[:128]])
    w["S_ks"] = np.ascontiguousarray(A(inp["L0_ffn_k_scale"])[:, ROW_D[:128]])
    w["S_wo"] = A(inp["L0_ffn_Wo"])
    w["S_g"] = A(inp["L0_ffn_ln_g"])
    w["S_b"] = A(inp["L0_ffn_ln_b"])
    W1 = A(inp["L0_ffn_W1"])
    W1p = np.zeros((6, 1024, 5632), dtype=np.float32)
    W1p[:, :, 0:2730] = W1[:, :, 0:2730]
    W1p[:, :, 2816:2816 + 2730] = W1[:, :, 2730:5460]
    w["S_w1"] = W1p
    b1 = A(inp["L0_ffn_b1"])
    b1p = np.zeros((6, 5632), dtype=np.float32)
    b1p[:, 0:2730] = b1[:, 0:2730]
    b1p[:, 2816:2816 + 2730] = b1[:, 2730:5460]
    w["S_b1"] = b1p
    W2 = A(inp["L0_ffn_W2"])
    W2p = np.zeros((6, 2816, 1024), dtype=np.float32)
    W2p[:, 0:2730, :] = W2
    w["S_w2"] = W2p
    w["S_b2"] = A(inp["L0_ffn_b2"])

    w["L1qkv"] = A(inp["L1_attn_W_qkv"])
    w["L1wo"] = A(inp["L1_attn_W_out"])
    w["relT"] = np.ascontiguousarray(A(inp["L1_attn_rel_emb"]).T)
    w["rel32"] = A(inp["L1_attn_rel_emb"])

    w["F_wq"] = np.ascontiguousarray(A(inp["L1_ffn_wq"])[ROPE_PERM, :])
    w["F_wk"] = np.ascontiguousarray(A(inp["L1_ffn_wk"])[KPERM4, :])
    w["F_wv"] = A(inp["L1_ffn_wv"])
    w["F_wo"] = A(inp["L1_ffn_wo"])
    w["F_qg"] = A(inp["L1_ffn_q_gain"])
    w["F_kg"] = A(inp["L1_ffn_k_gain"])
    w["F_fs1"] = A(inp["L1_ffn_fc_s1"])
    w["F_fs2"] = A(inp["L1_ffn_fc_s2"])
    w["F_fmix"] = A(inp["L1_ffn_fc_mix"])
    w["F_flog"] = A(inp["L1_ffn_fc_logits"])
    w["F_ps1"] = A(inp["L1_ffn_pj_s1"])
    w["F_ps2"] = A(inp["L1_ffn_pj_s2"])
    w["F_pmix"] = A(inp["L1_ffn_pj_mix"])
    w["F_plog"] = A(inp["L1_ffn_pj_logits"])
    return w


class Ctx:
    def __init__(self, nc, block, stack):
        self.nc = nc
        self.block = block
        self.stack = stack
        self.sstack = None
        self.sem = {}
        self.cnt = {}
        for s in ("qS", "qG", "qA", "eT", "eS", "eV", "eG", "bar"):
            self.sem[s] = stack.enter_context(nc.semaphore(s))
            self.cnt[s] = 0
        self.dram = {}
        self.stage_idx = 0
        self._uid = 0

    def uid(self):
        self._uid += 1
        return self._uid

    def body(self, eng, fn):
        getattr(self.block, eng)(fn)

    def sb(self, shape, dtype, name="t"):
        return self.sstack.enter_context(self.nc.sbuf_tensor(f"{name}{self.uid()}", list(shape), dtype))

    def psum(self, shape, dtype=F32, name="p"):
        return self.sstack.enter_context(self.nc.psum_tensor(f"{name}{self.uid()}", list(shape), dtype))


class WT:

    def __init__(self, ctx, e):
        self.ctx = ctx
        self.e = e
        self.last = {}

    def __call__(self, sem, val):
        if val > self.last.get(sem, 0):
            self.e.wait_ge(self.ctx.sem[sem], val)
            self.last[sem] = val


def emit_barrier(ctx):
    ctx.stage_idx += 1
    target = 5 * ctx.stage_idx
    snap = dict(ctx.cnt)
    for eng in ENGS:
        q = _QUEUE_OF.get(eng)

        def fn(e, q=q, t=target, snap=snap):
            if q is not None:
                e.wait_ge(ctx.sem[q], snap[q])
            e.nop().then_inc(ctx.sem["bar"], 1)
            e.wait_ge(ctx.sem["bar"], t)
        ctx.body(eng, fn)



def load_weights_resident(ctx, w_dram, K, M, name="w", nbuf=2):
    NKC = K // 128
    wsb = ctx.sb((128, NKC * M), BF16, name)
    if w_dram.dtype == BF16:
        b_qS = ctx.cnt["qS"]
        ctx.cnt["qS"] = b_qS + 16 * NKC
        tgt = ctx.cnt["qS"]
        b_eV = ctx.cnt["eV"]
        ctx.cnt["eV"] = b_eV + 1

        def f_syncb(e):
            e.wait_ge(ctx.sem["qS"], b_qS)
            for kc in range(NKC):
                e.dma_start(out=wsb[:, kc * M:(kc + 1) * M],
                            in_=w_dram[kc * 128:(kc + 1) * 128, :]).then_inc(ctx.sem["qS"], 16)
        ctx.body("sync", f_syncb)

        def f_vecb(e):
            e.wait_ge(ctx.sem["qS"], tgt)
            e.nop().then_inc(ctx.sem["eV"], 1)
        ctx.body("vector", f_vecb)
        return wsb, ctx.cnt["eV"]
    wtmp = [ctx.sb((128, M), F32, name + "f") for _ in range(nbuf)]
    b_qS = ctx.cnt["qS"]
    ld = [b_qS + 16 * (kc + 1) for kc in range(NKC)]
    ctx.cnt["qS"] = b_qS + 16 * NKC
    b_eV = ctx.cnt["eV"]
    ctx.cnt["eV"] = b_eV + NKC
    done = ctx.cnt["eV"]

    def f_sync(e):
        for kc in range(NKC):
            if kc >= 1:
                e.wait_ge(ctx.sem["qS"], ld[kc - 1])
            if kc >= nbuf:
                e.wait_ge(ctx.sem["eV"], b_eV + kc - nbuf + 1)
            e.dma_start(out=wtmp[kc % nbuf][:], in_=w_dram[kc * 128:(kc + 1) * 128, :]).then_inc(ctx.sem["qS"], 16)
    ctx.body("sync", f_sync)

    def f_vec(e):
        for kc in range(NKC):
            e.wait_ge(ctx.sem["qS"], ld[kc])
            e.tensor_copy(wsb[:, kc * M:(kc + 1) * M], wtmp[kc % nbuf][:]).then_inc(ctx.sem["eV"], 1)
    ctx.body("vector", f_vec)
    load_weights_resident.last_tmp = wtmp
    return wsb, done


def load_bias_cols(ctx, b_dram, n_cols, scale_first=None):
    tile = ctx.sb((128, n_cols), F32, "bias")
    b_qS = ctx.cnt["qS"]
    ctx.cnt["qS"] = b_qS + 16 * n_cols
    b_eV = ctx.cnt["eV"]
    ctx.cnt["eV"] = b_eV + 1

    def f_sync(e):
        e.wait_ge(ctx.sem["qS"], b_qS)
        for j in range(n_cols):
            e.dma_start(out=tile[:, j], in_=b_dram[j * 128:(j + 1) * 128]).then_inc(ctx.sem["qS"], 16)
    ctx.body("sync", f_sync)

    def f_vec(e):
        e.wait_ge(ctx.sem["qS"], b_qS + 16 * n_cols)
        if scale_first is not None:
            ncs, s = scale_first
            e.tensor_scalar_mul(tile[:, 0:ncs], tile[:, 0:ncs], s).then_inc(ctx.sem["eV"], 1)
        else:
            e.nop().then_inc(ctx.sem["eV"], 1)
    ctx.body("vector", f_vec)
    return tile, ctx.cnt["eV"]


def gemm_fm(ctx, *, wsb, w_ready, NKC, MC, rhs_dram, epi_emit, rhs_shift=False,
            aug=None):
    M = MC * 128
    NB = 2
    psums = [ctx.psum((128, TC), F32, "gp") for _ in range(NB)]
    rbuf = [[ctx.sb((128, TC), BF16, "rb") for _ in range(NKC)] for _ in range(2)]
    zcol = ctx.dram["zeros128b"]

    b = dict(ctx.cnt)
    per_tc = [NKC + ((NKC - NKC // 2) if (rhs_shift and i == 0) else 0) for i in range(NT)]
    ld_t = []
    cur = b["qS"]
    for i in range(NT):
        cur += 16 * per_tc[i]
        ld_t.append(cur)
    ctx.cnt["qS"] = cur
    mm0 = b["eT"]
    n_iter = NT * MC
    ctx.cnt["eT"] = mm0 + n_iter
    epi0 = b["eS"]
    ctx.cnt["eS"] = epi0 + n_iter

    info = dict(psums=psums, mm0=mm0, epi0=epi0, MC=MC, NB=NB, b=b)

    def f_sync(e):
        e.wait_ge(ctx.sem["qS"], b["qS"])
        for i in range(NT):
            if i >= 1:
                e.wait_ge(ctx.sem["qS"], ld_t[i - 1])
            if i >= 2:
                e.wait_ge(ctx.sem["eT"], mm0 + (i - 1) * MC)
            for kc in range(NKC):
                dst = rbuf[i % 2][kc]
                if rhs_shift and kc >= NKC // 2:
                    if i == 0:
                        e.dma_start(out=dst[:, 0:1], in_=zcol[:, :]).then_inc(ctx.sem["qS"], 16)
                        e.dma_start(out=dst[:, 1:TC],
                                    in_=rhs_dram[kc * 128:(kc + 1) * 128, 0:TC - 1]).then_inc(ctx.sem["qS"], 16)
                    else:
                        e.dma_start(out=dst[:],
                                    in_=rhs_dram[kc * 128:(kc + 1) * 128, i * TC - 1:(i + 1) * TC - 1]).then_inc(ctx.sem["qS"], 16)
                else:
                    e.dma_start(out=dst[:],
                                in_=rhs_dram[kc * 128:(kc + 1) * 128, i * TC:(i + 1) * TC]).then_inc(ctx.sem["qS"], 16)
    ctx.body("sync", f_sync)

    def f_tensor(e):
        e.wait_ge(ctx.sem["eV"], w_ready)
        if aug is not None:
            e.wait_ge(aug[2][0], aug[2][1])
        for i in range(NT):
            e.wait_ge(ctx.sem["qS"], ld_t[i])
            for mc in range(MC):
                it = i * MC + mc
                bank = psums[it % NB]
                if it >= NB:
                    e.wait_ge(ctx.sem["eS"], epi0 + it - NB + 1)
                ins = None
                for kc in range(NKC):
                    ins = e.matmul(bank[:], wsb[:, kc * M + mc * 128: kc * M + mc * 128 + 128],
                                   rbuf[i % 2][kc][:], start=(kc == 0),
                                   stop=(kc == NKC - 1 and aug is None),
                                   skip_group_check=True)
                if aug is not None:
                    ins = e.matmul(bank[:], aug[0][:, mc * 128:mc * 128 + 128], aug[1](i),
                                   start=False, stop=True, skip_group_check=True)
                ins.then_inc(ctx.sem["eT"], 1)
    ctx.body("tensor", f_tensor)
    epi_emit(info)
    return info


def epi_act_store(ctx, info, stores, bias=None, scale=None, func=None, out_dtype=BF16):
    MC, NB, psums = info["MC"], info["NB"], info["psums"]
    obuf = [ctx.sb((128, TC), out_dtype, "ob") for _ in range(4)]
    st0 = ctx.cnt["qG"]
    plan = [stores(i, mc) for i in range(NT) for mc in range(MC)]
    ctx.cnt["qG"] = st0 + 16 * sum(len(p) for p in plan)
    epi0, mm0 = info["epi0"], info["mm0"]
    cum = [0]
    for p in plan:
        cum.append(cum[-1] + len(p))

    def f_scalar(e):
        for it in range(len(plan)):
            mc = it % MC
            e.wait_ge(ctx.sem["eT"], mm0 + it + 1)
            if it >= 4:
                e.wait_ge(ctx.sem["qG"], st0 + 16 * cum[it - 3])
            f = func(mc) if func else AF.Copy
            bb = bias(mc) if bias else 0.0
            if f == AF.Copy and not isinstance(bb, float):
                f = AF.Identity
            e.activation(out=obuf[it % 4][:], in_=psums[it % NB][:, :],
                         func=f, bias=bb,
                         scale=(scale(mc) if scale else 1.0)).then_inc(ctx.sem["eS"], 1)
    ctx.body("scalar", f_scalar)

    def f_gp(e):
        for it in range(len(plan)):
            e.wait_ge(ctx.sem["eS"], epi0 + it + 1)
            if it >= 1:
                e.wait_ge(ctx.sem["qG"], st0 + 16 * cum[it])
            for d in plan[it]:
                e.dma_start(out=d, in_=obuf[it % 4][:]).then_inc(ctx.sem["qG"], 16)
    ctx.body("gpsimd", f_gp)



def stage_ingest(ctx):
    transpose_td_to_dt(ctx, src=ctx.dram["x_in"], dst=ctx.dram["x_fm"])


def transpose_td_to_dt(ctx, src, dst):
    ident = ctx.dram["identity_f32"]
    idt = ctx.sb((128, 128), F32, "id")
    NBUF = 3
    tin = [ctx.sb((128, D), F32, "ti") for _ in range(NBUF)]
    tout = [ctx.sb((128, 128), F32, "to") for _ in range(4)]
    pss = [ctx.psum((128, 128), F32, "pt") for _ in range(2)]
    NTT = T // 128

    b = dict(ctx.cnt)
    ld = [b["qS"] + 16 * (1 + i + 1) for i in range(NTT)]
    ctx.cnt["qS"] = b["qS"] + 16 * (1 + NTT)
    mm0 = b["eT"]
    ctx.cnt["eT"] = mm0 + NTT * 8
    ev0 = b["eS"]
    ctx.cnt["eS"] = ev0 + NTT * 8
    st0 = b["qG"]
    ctx.cnt["qG"] = st0 + 16 * NTT * 8

    def f_sync(e):
        e.dma_start(out=idt[:], in_=ident[:, :]).then_inc(ctx.sem["qS"], 16)
        for tt in range(NTT):
            if tt >= 1:
                e.wait_ge(ctx.sem["qS"], ld[tt - 1])
            if tt >= NBUF:
                e.wait_ge(ctx.sem["eT"], mm0 + (tt - NBUF + 1) * 8)
            e.dma_start(out=tin[tt % NBUF][:], in_=src[tt * 128:(tt + 1) * 128, :]).then_inc(ctx.sem["qS"], 16)
    ctx.body("sync", f_sync)

    def f_tensor(e):
        for tt in range(NTT):
            e.wait_ge(ctx.sem["qS"], ld[tt])
            for fc in range(8):
                k = tt * 8 + fc
                if k >= 2:
                    e.wait_ge(ctx.sem["eS"], ev0 + k - 1)
                e.transpose(pss[k % 2][:], tin[tt % NBUF][:, fc * 128:(fc + 1) * 128],
                            idt[:]).then_inc(ctx.sem["eT"], 1)
    ctx.body("tensor", f_tensor)

    def f_scalar(e):
        for k in range(NTT * 8):
            e.wait_ge(ctx.sem["eT"], mm0 + k + 1)
            if k >= 4:
                e.wait_ge(ctx.sem["qG"], st0 + 16 * (k - 3))
            e.copy(tout[k % 4][:], pss[k % 2][:]).then_inc(ctx.sem["eS"], 1)
    ctx.body("scalar", f_scalar)

    def f_gp(e):
        for tt in range(NTT):
            for fc in range(8):
                k = tt * 8 + fc
                e.wait_ge(ctx.sem["eS"], ev0 + k + 1)
                if k >= 1:
                    e.wait_ge(ctx.sem["qG"], st0 + 16 * k)
                e.dma_start(out=dst[fc * 128:(fc + 1) * 128, tt * 128:(tt + 1) * 128],
                            in_=tout[k % 4][:]).then_inc(ctx.sem["qG"], 16)
    ctx.body("gpsimd", f_gp)


def stage_ln(ctx, src, dsts_f32, dsts_bf16, eps=1e-5, shift=False):
    xfm = ctx.dram[src]
    d_f = [ctx.dram[n] for n in dsts_f32]
    d_b = [ctx.dram[n] for n in dsts_bf16]
    zcol = ctx.dram["zeros128"]

    ones1 = ctx.sb((128, 1), F32, "on1")
    onesr = ctx.sb((1, 128), F32, "onr")
    xt = [[ctx.sb((128, TC), F32, "x") for _ in range(NKD)] for _ in range(2)]
    sq = [ctx.sb((128, TC), F32, "sq") for _ in range(NKD)]
    rows = ctx.sb((1, 4 * TC), F32, "rw")
    rstdA = ctx.sb((1, T), F32, "rstdA")
    nmrA = ctx.sb((1, T), F32, "nmrA")
    xo = [ctx.sb((128, TC), F32, "xo") for _ in range(3)]
    xob = [ctx.sb((128, TC), BF16, "xob") for _ in range(3)]
    p_s = ctx.psum((1, TC), F32, "ps")
    p_q = ctx.psum((1, TC), F32, "pq")
    p_r = ctx.psum((128, TC), F32, "pr")
    p_m = ctx.psum((128, TC), F32, "pm")

    def n_loads(i):
        return NKD + ((NKD - NKD // 2) if (shift and i == 0) else 0)

    def load_tiles(e, i, buf):
        for kc in range(NKD):
            dst = xt[buf][kc]
            if shift and kc >= NKD // 2:
                if i == 0:
                    e.dma_start(out=dst[:, 0:1], in_=zcol[:, :]).then_inc(ctx.sem["qS"], 16)
                    e.dma_start(out=dst[:, 1:TC],
                                in_=xfm[kc * 128:(kc + 1) * 128, 0:TC - 1]).then_inc(ctx.sem["qS"], 16)
                else:
                    e.dma_start(out=dst[:],
                                in_=xfm[kc * 128:(kc + 1) * 128, i * TC - 1:(i + 1) * TC - 1]).then_inc(ctx.sem["qS"], 16)
            else:
                e.dma_start(out=dst[:],
                            in_=xfm[kc * 128:(kc + 1) * 128, i * TC:(i + 1) * TC]).then_inc(ctx.sem["qS"], 16)

    b = dict(ctx.cnt)
    eT0, eS0, eV0, eG0 = b["eT"], b["eS"], b["eV"], b["eG"]
    NV1 = 7
    NS1 = 9
    ld1 = []
    cur = b["qS"]
    for i in range(NT):
        cur += 16 * n_loads(i)
        ld1.append(cur)
    ctx.cnt["qS"] = cur

    def f1_sync(e):
        for i in range(NT):
            if i >= 1:
                e.wait_ge(ctx.sem["qS"], ld1[i - 1])
            if i >= 2:
                e.wait_ge(ctx.sem["eT"], eT0 + (i - 1))
            load_tiles(e, i, i % 2)
    ctx.body("sync", f1_sync)

    def f1_gp(e):
        e.memset(ones1[:], 1.0).then_inc(ctx.sem["eG"], 1)
        e.memset(onesr[:], 1.0).then_inc(ctx.sem["eG"], 1)
    ctx.body("gpsimd", f1_gp)
    ctx.cnt["eG"] = eG0 + 2

    def f1_scalar(e):
        for i in range(NT):
            e.wait_ge(ctx.sem["qS"], ld1[i])
            if i >= 1:
                e.wait_ge(ctx.sem["eT"], eT0 + i)
            for kc in range(NKD):
                e.square(sq[kc][:], xt[i % 2][kc][:]).then_inc(ctx.sem["eS"], 1)
            e.wait_ge(ctx.sem["eV"], eV0 + NV1 * i + 5)
            e.activation(out=rows[:, 3 * TC:4 * TC], in_=rows[:, 2 * TC:3 * TC],
                         func=AF.Sqrt).then_inc(ctx.sem["eS"], 1)
    ctx.body("scalar", f1_scalar)
    ctx.cnt["eS"] = eS0 + NS1 * NT

    def f1_tensor(e):
        e.wait_ge(ctx.sem["eG"], eG0 + 2)
        for i in range(NT):
            e.wait_ge(ctx.sem["qS"], ld1[i])
            if i >= 1:
                e.wait_ge(ctx.sem["eV"], eV0 + NV1 * (i - 1) + 2)
            for kc in range(NKD):
                e.matmul(p_s[:], ones1[:], xt[i % 2][kc][:], start=(kc == 0),
                         stop=(kc == NKD - 1), skip_group_check=True)
            e.wait_ge(ctx.sem["eS"], eS0 + NS1 * i + 8)
            ins = None
            for kc in range(NKD):
                ins = e.matmul(p_q[:], ones1[:], sq[kc][:], start=(kc == 0),
                               stop=(kc == NKD - 1), skip_group_check=True)
            ins.then_inc(ctx.sem["eT"], 1)
    ctx.body("tensor", f1_tensor)
    ctx.cnt["eT"] = eT0 + NT

    def f1_vec(e):
        for i in range(NT):
            bv = eV0 + NV1 * i
            e.wait_ge(ctx.sem["eT"], eT0 + i + 1)
            if i >= 1:
                e.wait_ge(ctx.sem["eV"], bv)
            e.tensor_scalar_mul(rows[:, 0:TC], p_s[:], 1.0 / D).then_inc(ctx.sem["eV"], 1)
            e.tensor_scalar_mul(rows[:, TC:2 * TC], p_q[:], 1.0 / D).then_inc(ctx.sem["eV"], 1)
            e.wait_ge(ctx.sem["eV"], bv + 2)
            e.tensor_tensor(rows[:, 2 * TC:3 * TC], rows[:, 0:TC], rows[:, 0:TC],
                            ALU.mult).then_inc(ctx.sem["eV"], 1)
            e.wait_ge(ctx.sem["eV"], bv + 3)
            e.tensor_sub(rows[:, 2 * TC:3 * TC], rows[:, TC:2 * TC],
                         rows[:, 2 * TC:3 * TC]).then_inc(ctx.sem["eV"], 1)
            e.wait_ge(ctx.sem["eV"], bv + 4)
            e.tensor_scalar_add(rows[:, 2 * TC:3 * TC], rows[:, 2 * TC:3 * TC],
                                eps).then_inc(ctx.sem["eV"], 1)
            e.wait_ge(ctx.sem["eS"], eS0 + NS1 * i + 9)
            e.reciprocal(rstdA[:, i * TC:(i + 1) * TC],
                         rows[:, 3 * TC:4 * TC]).then_inc(ctx.sem["eV"], 1)
            e.wait_ge(ctx.sem["eV"], bv + 6)
            e.scalar_tensor_tensor(nmrA[:, i * TC:(i + 1) * TC], rows[:, 0:TC], -1.0,
                                   rstdA[:, i * TC:(i + 1) * TC], ALU.mult,
                                   ALU.mult).then_inc(ctx.sem["eV"], 1)
    ctx.body("vector", f1_vec)
    ctx.cnt["eV"] = eV0 + NV1 * NT
    ph1_done = ctx.cnt["eV"]

    b2 = dict(ctx.cnt)
    eT20, eV20 = b2["eT"], b2["eV"]
    st0 = b2["qG"]
    n_dst = len(d_f) + len(d_b)
    ld2 = []
    cur = ctx.cnt["qS"]
    for i in range(NT):
        cur += 16 * n_loads(i)
        ld2.append(cur)
    ctx.cnt["qS"] = cur

    def f2_sync(e):
        e.wait_ge(ctx.sem["eV"], ph1_done)
        e.wait_ge(ctx.sem["qS"], b2["qS"])
        for i in range(NT):
            if i >= 1:
                e.wait_ge(ctx.sem["qS"], ld2[i - 1])
            if i >= 2:
                e.wait_ge(ctx.sem["eV"], eV20 + 3 * NKD * (i - 1))
            load_tiles(e, i, i % 2)
    ctx.body("sync", f2_sync)

    def f2_tensor(e):
        e.wait_ge(ctx.sem["eV"], ph1_done)
        for i in range(NT):
            if i >= 1:
                e.wait_ge(ctx.sem["eV"], eV20 + 3 * NKD * i)
            e.matmul(p_r[:], onesr[:], rstdA[:, i * TC:(i + 1) * TC], start=True,
                     stop=True, skip_group_check=True)
            e.matmul(p_m[:], onesr[:], nmrA[:, i * TC:(i + 1) * TC], start=True,
                     stop=True, skip_group_check=True).then_inc(ctx.sem["eT"], 1)
    ctx.body("tensor", f2_tensor)
    ctx.cnt["eT"] = eT20 + NT

    def f2_vec(e):
        for i in range(NT):
            e.wait_ge(ctx.sem["qS"], ld2[i])
            e.wait_ge(ctx.sem["eT"], eT20 + i + 1)
            for kc in range(NKD):
                kidx = i * NKD + kc
                bv = eV20 + 3 * kidx
                if kidx >= 3:
                    e.wait_ge(ctx.sem["qG"], st0 + 16 * n_dst * (kidx - 2))
                e.tensor_mul(xo[kidx % 3][:], xt[i % 2][kc][:], p_r[:]).then_inc(ctx.sem["eV"], 1)
                e.wait_ge(ctx.sem["eV"], bv + 1)
                e.tensor_add(xo[kidx % 3][:], xo[kidx % 3][:], p_m[:]).then_inc(ctx.sem["eV"], 1)
                e.wait_ge(ctx.sem["eV"], bv + 2)
                e.tensor_copy(xob[kidx % 3][:], xo[kidx % 3][:]).then_inc(ctx.sem["eV"], 1)
    ctx.body("vector", f2_vec)
    ctx.cnt["eV"] = eV20 + 3 * NKD * NT

    def f2_gp(e):
        for i in range(NT):
            for kc in range(NKD):
                kidx = i * NKD + kc
                e.wait_ge(ctx.sem["eV"], eV20 + 3 * (kidx + 1))
                if kidx >= 1:
                    e.wait_ge(ctx.sem["qG"], st0 + 16 * n_dst * kidx)
                r0, c0 = kc * 128, i * TC
                for d in d_f:
                    e.dma_start(out=d[r0:r0 + 128, c0:c0 + TC], in_=xo[kidx % 3][:]).then_inc(ctx.sem["qG"], 16)
                for d in d_b:
                    e.dma_start(out=d[r0:r0 + 128, c0:c0 + TC], in_=xob[kidx % 3][:]).then_inc(ctx.sem["qG"], 16)
    ctx.body("gpsimd", f2_gp)
    ctx.cnt["qG"] = st0 + 16 * n_dst * NKD * NT


def stage_qk(ctx, w_ap, M2, q_out, k_out, bias_vec=None, q_scale=1.0,
             rhs_name="xn0b"):
    MC = M2 // 128
    MCh = MC // 2
    wsb, wrdy = load_weights_resident(ctx, w_ap, D, M2, "wqk")
    if bias_vec is not None:
        btile, brdy = load_bias_cols(ctx, ctx.dram[bias_vec], MC,
                                     scale_first=(MCh, q_scale) if q_scale != 1.0 else None)
        wrdy = max(wrdy, brdy)
    qd, kd = ctx.dram[q_out], ctx.dram[k_out]

    def stores(i, mc):
        if mc < MCh:
            return [qd[mc * 128:(mc + 1) * 128, i * TC:(i + 1) * TC]]
        return [kd[(mc - MCh) * 128:(mc - MCh + 1) * 128, i * TC:(i + 1) * TC]]

    def epi(info):
        epi_act_store(ctx, info, stores,
                      bias=(lambda mc: btile[:, mc:mc + 1]) if bias_vec else None,
                      scale=lambda mc: q_scale if mc < MCh else 1.0,
                      out_dtype=BF16)
    gemm_fm(ctx, wsb=wsb, w_ready=wrdy, NKC=NKD, MC=MC,
            rhs_dram=ctx.dram[rhs_name], epi_emit=epi)


def stage_qkv0(ctx):
    stage_qk(ctx, ctx.dram["L0qkw"], 2048, "q0b", "k0b", bias_vec="L0qkb",
             q_scale=0.125, rhs_name="xn0b")


def stage_vproj(ctx, w_ap, b_name, rhs_name, out_name, scale_row=None):
    Dout = SCRATCH[out_name][0][1]
    WN = min(TC, Dout)
    MCn = Dout // WN
    wsb, wrdy = load_weights_resident(ctx, w_ap, D, Dout, "wv")
    vout = ctx.dram[out_name]
    srow = ctx.dram[scale_row] if scale_row else None

    xres = ctx.sb((128, NKD * T), BF16, "xres")
    b_qS = ctx.cnt["qS"]
    ctx.cnt["qS"] = b_qS + 16 * NKD
    xl = ctx.cnt["qS"]

    def f_sync(e):
        e.wait_ge(ctx.sem["qS"], b_qS)
        for kc in range(NKD):
            e.dma_start(out=xres[:, kc * T:(kc + 1) * T],
                        in_=ctx.dram[rhs_name][kc * 128:(kc + 1) * 128, :]).then_inc(ctx.sem["qS"], 16)
    ctx.body("sync", f_sync)

    NTT = T // 128
    use_bias = b_name is not None
    onesr = ctx.sb((1, 128), F32, "onr")
    brow = ctx.sb((1, Dout), F32, "brow") if use_bias else None
    bvb = ctx.sb((128, Dout), F32, "bvb") if use_bias else None
    scol = [ctx.sb((128, 1), F32, "scl") for _ in range(2)] if srow is not None else None
    p_b = ctx.psum((128, TC), F32, "pbb") if use_bias else None
    b_qS2 = ctx.cnt["qS"]
    eG0 = ctx.cnt["eG"]
    eT0 = ctx.cnt["eT"]
    eS0 = ctx.cnt["eS"]
    bias_rdy = eS0

    if use_bias:
        ctx.cnt["qS"] = b_qS2 + 16

        def f_sync2(e):
            e.wait_ge(ctx.sem["qS"], xl)
            e.dma_start(out=brow[:], in_=ctx.dram[b_name][:]).then_inc(ctx.sem["qS"], 16)
        ctx.body("sync", f_sync2)

        def f_gp0(e):
            e.memset(onesr[:], 1.0).then_inc(ctx.sem["eG"], 1)
        ctx.body("gpsimd", f_gp0)
        ctx.cnt["eG"] = eG0 + 1

        def f_t0(e):
            e.wait_ge(ctx.sem["eG"], eG0 + 1)
            e.wait_ge(ctx.sem["qS"], b_qS2 + 16)
            for j in range(MCn):
                if j >= 1:
                    e.wait_ge(ctx.sem["eS"], eS0 + j)
                e.matmul(p_b[:], onesr[:], brow[:, j * TC:(j + 1) * TC], start=True,
                         stop=True, skip_group_check=True).then_inc(ctx.sem["eT"], 1)
        ctx.body("tensor", f_t0)

        def f_s0(e):
            for j in range(MCn):
                e.wait_ge(ctx.sem["eT"], eT0 + j + 1)
                e.copy(bvb[:, j * TC:(j + 1) * TC], p_b[:]).then_inc(ctx.sem["eS"], 1)
        ctx.body("scalar", f_s0)
        ctx.cnt["eT"] = eT0 + MCn
        ctx.cnt["eS"] = eS0 + MCn
        bias_rdy = ctx.cnt["eS"]

    NB = 2
    psums = [ctx.psum((128, TC), F32, "vp") for _ in range(NB)]
    obuf = [ctx.sb((128, TC), BF16, "vo") for _ in range(4)]
    b3 = dict(ctx.cnt)
    mm0 = b3["eT"]
    ev0 = b3["eV"]
    st0 = b3["qG"]
    qA0 = b3["qA"]
    n_iter = NTT * MCn
    ctx.cnt["eT"] = mm0 + n_iter
    nv_per = 1 if (use_bias or srow is not None) else 1
    ctx.cnt["eV"] = ev0 + n_iter
    ctx.cnt["qG"] = st0 + 16 * n_iter
    if srow is not None:
        srld = [qA0 + 16 * (tt + 1) for tt in range(NTT)]
        ctx.cnt["qA"] = qA0 + 16 * NTT

        def f_sc_s(e):
            w = WT(ctx, e)
            w("qA", qA0)
            for tt in range(NTT):
                if tt >= 1:
                    w("qA", srld[tt - 1])
                if tt >= 2:
                    w("eV", ev0 + MCn * (tt - 1))
                e.dma_start(out=scol[tt % 2][:, 0:1],
                            in_=srow[:, tt * 128:(tt + 1) * 128].rearrange("o t -> t o")).then_inc(ctx.sem["qA"], 16)
        ctx.body("scalar", f_sc_s)

    def f_tensor(e):
        e.wait_ge(ctx.sem["eV"], wrdy)
        e.wait_ge(ctx.sem["qS"], xl)
        for tt in range(NTT):
            for j in range(MCn):
                it = tt * MCn + j
                bank = psums[it % NB]
                if it >= NB:
                    e.wait_ge(ctx.sem["eV"], ev0 + it - NB + 1)
                ins = None
                for kc in range(NKD):
                    ins = e.matmul(bank[:, 0:WN], xres[:, kc * T + tt * 128: kc * T + tt * 128 + 128],
                                   wsb[:, kc * Dout + j * WN: kc * Dout + (j + 1) * WN],
                                   start=(kc == 0), stop=(kc == NKD - 1),
                                   skip_group_check=True)
                ins.then_inc(ctx.sem["eT"], 1)
    ctx.body("tensor", f_tensor)

    def f_vec(e):
        w = WT(ctx, e)
        if use_bias:
            w("eS", bias_rdy)
        for it in range(n_iter):
            tt, j = it // MCn, it % MCn
            w("eT", mm0 + it + 1)
            if srow is not None:
                w("qA", srld[tt])
            if it >= 4:
                w("qG", st0 + 16 * (it - 3))
            if use_bias:
                e.tensor_add(obuf[it % 4][:, 0:WN], psums[it % NB][:, 0:WN],
                             bvb[:, j * TC:(j + 1) * TC]).then_inc(ctx.sem["eV"], 1)
            elif srow is not None:
                e.tensor_scalar_mul(obuf[it % 4][:, 0:WN], psums[it % NB][:, 0:WN],
                                    scol[tt % 2][:, 0:1]).then_inc(ctx.sem["eV"], 1)
            else:
                e.tensor_copy(obuf[it % 4][:, 0:WN], psums[it % NB][:, 0:WN]).then_inc(ctx.sem["eV"], 1)
    ctx.body("vector", f_vec)

    def f_gp(e):
        for tt in range(NTT):
            for j in range(MCn):
                it = tt * MCn + j
                e.wait_ge(ctx.sem["eV"], ev0 + it + 1)
                if it >= 1:
                    e.wait_ge(ctx.sem["qG"], st0 + 16 * it)
                e.dma_start(out=vout[tt * 128:(tt + 1) * 128, j * WN:(j + 1) * WN],
                            in_=obuf[it % 4][:, 0:WN]).then_inc(ctx.sem["qG"], 16)
    ctx.body("gpsimd", f_gp)


def stage_attnprep(ctx, qname, kname, cb_out, *, ind_q, ind_k, nh_k,
                   window=None, a_coef=0.35, a_const=None, variants=None):
    qb = ctx.dram[qname]
    kb = ctx.dram[kname]
    NKQ = SCRATCH[qname][0][0] // 128
    NKK = SCRATCH[kname][0][0] // 128
    indq = ctx.dram[ind_q]
    indk = ctx.dram[ind_k]
    const_mode = a_const is not None
    gqa = const_mode and a_const[0] == "gqa"

    iq = ctx.sb((128, NKQ * H), BF16, "iq")
    ik = ctx.sb((128, NKK * nh_k), BF16, "ik")
    qt = [ctx.sb((128, TC), BF16, "qt") for _ in range(NKQ)]
    kt = [ctx.sb((128, TC), BF16, "kt") for _ in range(NKK)]
    prod = [ctx.sb((128, TC), BF16, "pr") for _ in range(max(NKQ, NKK))]
    krepb = [ctx.sb((128, TC), BF16, "krp") for _ in range(NKQ)] if gqa else None
    qnR = ctx.sb((H, T), F32, "qnR")
    dgR = ctx.sb((H, T), F32, "dgR")
    knR = ctx.sb((nh_k, T), F32, "knR")
    kmax = ctx.sb((nh_k, 16), F32, "kmax")
    arow = ctx.sb((1, 260), F32, "arow")
    acol = ctx.sb((H, 1), F32, "acol")
    onesr = ctx.sb((1, H), F32, "onh")
    cw = ctx.sb((H, TC), F32, "cw")
    n_var = 1 + (len(variants) if variants else 0)
    cb = [ctx.sb((H, TC), BF16, "cb") for _ in range(2 * n_var)]
    relc = ctx.sb((H, 32), F32, "relc") if variants else None
    p_qn = ctx.psum((H, TC), F32, "pqn")
    p_kn = ctx.psum((nh_k, TC), F32, "pkn")
    p_dg = ctx.psum((H, TC), F32, "pdg")
    p_a = ctx.psum((H, 1), F32, "pac")

    b = dict(ctx.cnt)
    eT0, eS0, eV0, eG0 = b["eT"], b["eS"], b["eV"], b["eG"]
    ctx.cnt["qS"] = b["qS"] + 16 * (NKQ + NKK)
    ind_ld = ctx.cnt["qS"]

    def f_sync0(e):
        e.wait_ge(ctx.sem["qS"], b["qS"])
        for c in range(NKQ):
            e.dma_start(out=iq[:, c * H:(c + 1) * H], in_=indq[c, :, :]).then_inc(ctx.sem["qS"], 16)
        for c in range(NKK):
            e.dma_start(out=ik[:, c * nh_k:(c + 1) * nh_k], in_=indk[c, :, :]).then_inc(ctx.sem["qS"], 16)
    ctx.body("sync", f_sync0)

    ld1 = []
    cur = ctx.cnt["qS"]
    for i in range(NT):
        cur += 16 * (NKQ + NKK)
        ld1.append(cur)
    ctx.cnt["qS"] = cur

    NVG = NKQ if const_mode else (NKQ + NKK + NKQ)
    NTG = 1 if const_mode else 3
    NSG = 1 if const_mode else 3

    def f_sync1(e):
        e.wait_ge(ctx.sem["qS"], ind_ld)
        for i in range(NT):
            if i >= 1:
                e.wait_ge(ctx.sem["qS"], ld1[i - 1])
                e.wait_ge(ctx.sem["eV"], eV0 + NVG * i)
            for c in range(NKQ):
                e.dma_start(out=qt[c][:], in_=qb[c * 128:(c + 1) * 128, i * TC:(i + 1) * TC]).then_inc(ctx.sem["qS"], 16)
            for c in range(NKK):
                e.dma_start(out=kt[c][:], in_=kb[c * 128:(c + 1) * 128, i * TC:(i + 1) * TC]).then_inc(ctx.sem["qS"], 16)
    ctx.body("sync", f_sync1)

    if gqa:
        qG0g = ctx.cnt["qG"]
        krld = [qG0g + 16 * 4 * NKQ * (i + 1) for i in range(NT)]
        ctx.cnt["qG"] = qG0g + 16 * 4 * NKQ * NT

        def f_gp_kr(e):
            w = WT(ctx, e)
            w("qG", qG0g)
            for i in range(NT):
                if i >= 1:
                    w("qG", krld[i - 1])
                    w("eV", eV0 + NVG * i)
                for c in range(NKQ):
                    kv = c // 2
                    ch = kv // 2
                    base = ch * 128 + (0 if kv % 2 == 0 else 32)
                    for (dst0, src0) in ((0, base), (32, base), (64, base + 64), (96, base + 64)):
                        e.dma_start(out=krepb[c][dst0:dst0 + 32, :],
                                    in_=kb[src0:src0 + 32, i * TC:(i + 1) * TC]).then_inc(ctx.sem["qG"], 16)
        ctx.body("gpsimd", f_gp_kr)

    def f_vec1(e):
        for i in range(NT):
            e.wait_ge(ctx.sem["qS"], ld1[i])
            if gqa:
                e.wait_ge(ctx.sem["qG"], krld[i])
            if i >= 1:
                e.wait_ge(ctx.sem["eT"], eT0 + NTG * i)
            if not const_mode:
                for c in range(NKQ):
                    e.tensor_mul(prod[c][:], qt[c][:], qt[c][:]).then_inc(ctx.sem["eV"], 1)
                e.wait_ge(ctx.sem["eT"], eT0 + NTG * i + 1)
                for c in range(NKK):
                    e.tensor_mul(prod[c][:], kt[c][:], kt[c][:]).then_inc(ctx.sem["eV"], 1)
                e.wait_ge(ctx.sem["eT"], eT0 + NTG * i + 2)
            for c in range(NKQ):
                e.tensor_mul(prod[c][:], qt[c][:],
                             (krepb[c] if gqa else kt[c])[:]).then_inc(ctx.sem["eV"], 1)
    ctx.body("vector", f_vec1)
    ctx.cnt["eV"] = eV0 + NVG * NT

    def f_tensor1(e):
        e.wait_ge(ctx.sem["qS"], ind_ld)
        for i in range(NT):
            bv = eV0 + NVG * i
            if i >= 1:
                e.wait_ge(ctx.sem["eS"], eS0 + NSG * i)
            ins = None
            if not const_mode:
                e.wait_ge(ctx.sem["eV"], bv + NKQ)
                for c in range(NKQ):
                    ins = e.matmul(p_qn[:], iq[:, c * H:(c + 1) * H], prod[c][:], start=(c == 0),
                                   stop=(c == NKQ - 1), skip_group_check=True)
                ins.then_inc(ctx.sem["eT"], 1)
                e.wait_ge(ctx.sem["eV"], bv + NKQ + NKK)
                for c in range(NKK):
                    ins = e.matmul(p_kn[:], ik[:, c * nh_k:(c + 1) * nh_k], prod[c][:], start=(c == 0),
                                   stop=(c == NKK - 1), skip_group_check=True)
                ins.then_inc(ctx.sem["eT"], 1)
            e.wait_ge(ctx.sem["eV"], bv + NVG)
            for c in range(NKQ):
                ins = e.matmul(p_dg[:], iq[:, c * H:(c + 1) * H], prod[c][:], start=(c == 0),
                               stop=(c == NKQ - 1), skip_group_check=True)
            ins.then_inc(ctx.sem["eT"], 1)
    ctx.body("tensor", f_tensor1)
    ctx.cnt["eT"] = eT0 + NTG * NT

    def f_scalar1(e):
        for i in range(NT):
            if not const_mode:
                e.wait_ge(ctx.sem["eT"], eT0 + NTG * i + 2)
                e.activation(out=qnR[:, i * TC:(i + 1) * TC], in_=p_qn[:], func=AF.Sqrt).then_inc(ctx.sem["eS"], 1)
                e.activation(out=knR[:, i * TC:(i + 1) * TC], in_=p_kn[:], func=AF.Sqrt).then_inc(ctx.sem["eS"], 1)
            e.wait_ge(ctx.sem["eT"], eT0 + NTG * (i + 1))
            e.copy(dgR[:, i * TC:(i + 1) * TC], p_dg[:]).then_inc(ctx.sem["eS"], 1)
    ctx.body("scalar", f_scalar1)
    ctx.cnt["eS"] = eS0 + NSG * NT

    b2 = dict(ctx.cnt)
    eV20, eT20, eS20 = b2["eV"], b2["eT"], b2["eS"]
    st0 = b2["qG"]
    qA0 = b2["qA"]

    if const_mode:
        if gqa:
            gcolq = ctx.sb((H, 2), F32, "gcq")
            ld_a = b2["qA"] + 16 * 17
            ctx.cnt["qA"] = ld_a

            def f_sc2(e):
                e.wait_ge(ctx.sem["qA"], b2["qA"])
                e.dma_start(out=gcolq[:, 0], in_=ctx.dram["F_qg"][:]).then_inc(ctx.sem["qA"], 16)
                for h in range(H):
                    e.dma_start(out=gcolq[h:h + 1, 1],
                                in_=ctx.dram["F_kg"][h // 4:h // 4 + 1]).then_inc(ctx.sem["qA"], 16)
            ctx.body("scalar", f_sc2)

            def f_vec2a(e):
                e.wait_ge(ctx.sem["qA"], ld_a)
                e.tensor_tensor(acol[:, 0:1], gcolq[:, 0:1], gcolq[:, 1:2],
                                ALU.mult).then_inc(ctx.sem["eV"], 1)
            ctx.body("vector", f_vec2a)
            ctx.cnt["eV"] = eV20 + 1

            def f_sc2b(e):
                e.wait_ge(ctx.sem["eV"], eV20 + 1)
                e.activation(out=acol[:, 0:1], in_=acol[:, 0:1], func=AF.Abs,
                             scale=a_coef * 8.0).then_inc(ctx.sem["eS"], 1)
            ctx.body("scalar", f_sc2b)
            ctx.cnt["eS"] = eS20 + 1
            eVb2 = ctx.cnt["eV"]
        else:
            qs_name, li, a_scale = a_const
            qs_ap = ctx.dram["S_qs"][li, :]
            ks_ap = ctx.dram["S_ks"][li, :]

            def f_sc2(e):
                e.wait_ge(ctx.sem["qA"], b2["qA"])
                e.dma_start(out=arow[:, 0:128], in_=qs_ap).then_inc(ctx.sem["qA"], 16)
                e.dma_start(out=arow[:, 128:256], in_=ks_ap).then_inc(ctx.sem["qA"], 16)
            ctx.body("scalar", f_sc2)
            ld_a = b2["qA"] + 32
            ctx.cnt["qA"] = ld_a

            def f_gp2a(e):
                e.memset(onesr[:], 1.0).then_inc(ctx.sem["eG"], 1)
            ctx.body("gpsimd", f_gp2a)
            ctx.cnt["eG"] = ctx.cnt["eG"] + 1
            eG2a = ctx.cnt["eG"]

            def f_vec2a(e):
                e.wait_ge(ctx.sem["qA"], ld_a)
                e.reduce_max(arow[:, 256:257], arow[:, 0:128], axis=AX.X,
                             apply_absolute_value=True).then_inc(ctx.sem["eV"], 1)
                e.reduce_max(arow[:, 257:258], arow[:, 128:256], axis=AX.X,
                             apply_absolute_value=True).then_inc(ctx.sem["eV"], 1)
                e.wait_ge(ctx.sem["eV"], eV20 + 2)
                e.tensor_tensor(arow[:, 258:259], arow[:, 256:257], arow[:, 257:258],
                                ALU.mult).then_inc(ctx.sem["eV"], 1)
                e.wait_ge(ctx.sem["eV"], eV20 + 3)
                e.tensor_scalar_mul(arow[:, 258:259], arow[:, 258:259],
                                    a_coef * a_scale).then_inc(ctx.sem["eV"], 1)
            ctx.body("vector", f_vec2a)
            ctx.cnt["eV"] = eV20 + 4

            def f_t2(e):
                e.wait_ge(ctx.sem["eV"], eV20 + 4)
                e.wait_ge(ctx.sem["eG"], eG2a)
                e.matmul(p_a[:], onesr[:], arow[:, 258:259], start=True, stop=True,
                         skip_group_check=True).then_inc(ctx.sem["eT"], 1)
            ctx.body("tensor", f_t2)
            ctx.cnt["eT"] = eT20 + 1

            def f_sc3(e):
                e.wait_ge(ctx.sem["eT"], eT20 + 1)
                e.copy(acol[:], p_a[:]).then_inc(ctx.sem["eS"], 1)
            ctx.body("scalar", f_sc3)
            ctx.cnt["eS"] = eS20 + 1
        eVb2 = ctx.cnt["eV"]

        def f_vec2(e):
            w = WT(ctx, e)
            w("eS", eS20 + 1)
            for i in range(NT):
                bv = eVb2 + 2 * i
                w("eV", bv)
                if i >= 2:
                    w("qG", st0 + 16 * (i - 1))
                e.tensor_scalar_max(cw[:], dgR[:, i * TC:(i + 1) * TC],
                                    acol[:, 0:1]).then_inc(ctx.sem["eV"], 1)
                w("eV", bv + 1)
                e.tensor_scalar_mul(cb[i % 2][:], cw[:], -1.0).then_inc(ctx.sem["eV"], 1)
        ctx.body("vector", f_vec2)
        ctx.cnt["eV"] = eVb2 + 2 * NT
        ev_done = lambda i: eVb2 + 2 * (i + 1)
    else:
        nwin = (T // window) if window else 1
        npc = (TC // window) if window else 1
        NV2 = npc + 1 + n_var
        if variants:
            def f_sc_rel(e):
                e.wait_ge(ctx.sem["qA"], qA0)
                e.dma_start(out=relc[:], in_=ctx.dram["relT"][:, :]).then_inc(ctx.sem["qA"], 16)
            ctx.body("scalar", f_sc_rel)
            ctx.cnt["qA"] = qA0 + 16
            rel_tgt = ctx.cnt["qA"]

        def f_vec2(e):
            w = WT(ctx, e)
            w("eS", eS0 + NSG * NT)
            if variants:
                w("qA", rel_tgt)
            for wi in range(nwin):
                j0 = max(0, (wi - 1) * window) if window else 0
                j1 = (wi + 1) * window if window else T
                e.reduce_max(kmax[:, wi:wi + 1], knR[:, j0:j1], axis=AX.X).then_inc(ctx.sem["eV"], 1)
            for i in range(NT):
                bv = eV20 + nwin + NV2 * i
                w("eV", bv)
                if i >= 2:
                    w("qG", st0 + 16 * n_var * (i - 1))
                for wj in range(npc):
                    wi = i * npc + wj
                    e.tensor_scalar(cw[:, wj * (window or TC):(wj + 1) * (window or TC)],
                                    qnR[:, i * TC + wj * (window or TC): i * TC + (wj + 1) * (window or TC)],
                                    kmax[:, wi:wi + 1] if window else kmax[:, 0:1],
                                    a_coef, ALU.mult, ALU.mult).then_inc(ctx.sem["eV"], 1)
                w("eV", bv + npc)
                e.tensor_tensor(cw[:], cw[:], dgR[:, i * TC:(i + 1) * TC],
                                ALU.max).then_inc(ctx.sem["eV"], 1)
                w("eV", bv + npc + 1)
                e.tensor_scalar_mul(cb[n_var * (i % 2)][:], cw[:], -1.0).then_inc(ctx.sem["eV"], 1)
                if variants:
                    for vi, (vname, rcol) in enumerate(variants):
                        w("eV", bv + npc + 2 + vi)
                        e.tensor_scalar(cb[n_var * (i % 2) + 1 + vi][:], cw[:], -1.0,
                                        relc[:, rcol:rcol + 1], ALU.mult,
                                        ALU.add).then_inc(ctx.sem["eV"], 1)
        ctx.body("vector", f_vec2)
        ctx.cnt["eV"] = eV20 + nwin + NV2 * NT
        ev_done = lambda i: eV20 + nwin + NV2 * (i + 1)

    def f_gp2(e):
        for i in range(NT):
            e.wait_ge(ctx.sem["eV"], ev_done(i))
            if i >= 1:
                e.wait_ge(ctx.sem["qG"], st0 + 16 * n_var * i)
            e.dma_start(out=ctx.dram[cb_out][:, i * TC:(i + 1) * TC],
                        in_=cb[n_var * (i % 2)][:]).then_inc(ctx.sem["qG"], 16)
            if variants:
                for vi, (vname, rcol) in enumerate(variants):
                    e.dma_start(out=ctx.dram[vname][:, i * TC:(i + 1) * TC],
                                in_=cb[n_var * (i % 2) + 1 + vi][:]).then_inc(ctx.sem["qG"], 16)
    ctx.body("gpsimd", f_gp2)
    ctx.cnt["qG"] = st0 + 16 * n_var * NT


def stage_attention(ctx, qname, kname, vname, cbname, outname, *, mode,
                    perm_qk=False, nkv=H, heads_per_kv=1, vcols=None,
                    bias=None):
    qb, kb, vt = ctx.dram[qname], ctx.dram[kname], ctx.dram[vname]
    outd = ctx.dram[outname]
    NKT = T // 128
    if vcols is None:
        vcols = lambda kv: kv * 64
    QW = 256 if mode == "local" else TC
    NQC = T // QW
    n_var = 3 if bias else 1
    cbrs = [ctx.dram[cbname]] + ([ctx.dram[bias[1]], ctx.dram[bias[2]]] if bias else [])
    biasTd = ctx.dram[bias[0]] if bias else None

    def ktl_of(qc):
        if mode == "causal":
            return list(range(0, (qc + 1) * QW // 128))
        if mode == "full":
            return list(range(NKT))
        return list(range(max(0, 2 * qc - 2), 2 * qc + 2))

    def mask_of(qc, kt):
        if mode == "causal":
            d0 = kt * 128 - qc * QW
            return d0 // 128 if d0 >= 0 else None
        if mode == "local":
            return kt - (2 * qc - 2)
        return None

    def var_of(qc, kt):
        if not bias:
            return 0, None
        d0 = kt * 128 - qc * TC
        if d0 <= -256:
            return 1, None
        if d0 >= 640:
            return 2, None
        return 0, (d0 + 128) // 128

    mask_name = "lmask" if mode == "local" else "cmask"
    has_mask = mode in ("causal", "local")

    Kaug = ctx.sb((65, T), BF16, "Kaug")
    Vaug = [ctx.sb((128, 65), BF16, "Va") for _ in range(NKT)]
    Qaug = [ctx.sb((65, QW), BF16, "Qa") for _ in range(2 * n_var)]
    pbuf = [ctx.sb((128, QW), BF16, "pb") for _ in range(3)]
    masks = ctx.sb((128, 4 * QW), BF16, "msk") if has_mask else None
    bbuf = [ctx.sb((128, QW), F32, "bb") for _ in range(2)] if bias else None
    rrow = ctx.sb((65, QW), F32, "rr")
    aux1 = ctx.sb((65, 64), F32, "ax")
    of32 = ctx.sb((64, QW), F32, "of")
    obuf = [ctx.sb((64, QW), BF16, "oo") for _ in range(2)]
    ps_s = [ctx.psum((128, QW), F32, "ss") for _ in range(2)]
    ps_o = ctx.psum((65, QW), F32, "so")
    ps_b = ctx.psum((64, QW), F32, "sbb")

    def qrows(h):
        if not perm_qk:
            return [(h * 64, 64)]
        c = h // 2
        if h % 2 == 0:
            return [(c * 128, 32), (c * 128 + 64, 32)]
        return [(c * 128 + 32, 32), (c * 128 + 96, 32)]

    def krows(kv):
        if not perm_qk:
            return [(kv * 64, 64)]
        c = kv // 2
        if kv % 2 == 0:
            return [(c * 128, 32), (c * 128 + 64, 32)]
        return [(c * 128 + 32, 32), (c * 128 + 96, 32)]

    heads = [(kv, kv * heads_per_kv + j) for kv in range(nkv) for j in range(heads_per_kv)]
    hq_order = [(kv, h, qc) for kv, h in heads for qc in range(NQC)]

    b = dict(ctx.cnt)
    eT0, eS0, eV0, eG0 = b["eT"], b["eS"], b["eV"], b["eG"]
    qS0, qA0, qG0 = b["qS"], b["qA"], b["qG"]

    qload_tgt = {}
    cq = qS0
    for kv, h in heads:
        cq += 16 * (len(krows(kv)) + NKT)
        for qc in range(NQC):
            cq += 16 * n_var * (len(qrows(h)) + 1)
            qload_tgt[(h, qc)] = cq
    ctx.cnt["qS"] = cq

    t_scores, t_av, t_bcast = {}, {}, {}
    s_exp, s_copy = {}, {}
    v_recip, v_mul, v_badd = {}, {}, {}
    g_mask = {}
    a_bload = {}
    n_memset = 2 + NKT
    cT, cS, cV, cG = eT0, eS0, eV0, eG0 + n_memset
    cA = qA0 + (16 * 4 if has_mask else 0)
    mask_tgt = cA
    n_band = 0
    for gi, (kv, h, qc) in enumerate(hq_order):
        for kt in ktl_of(qc):
            vi, bi = var_of(qc, kt)
            cT += 1
            t_scores[(h, qc, kt)] = cT
            if bi is not None:
                cA += 16
                a_bload[(h, qc, kt)] = cA
                n_band += 1
                cV += 1
                v_badd[(h, qc, kt)] = cV
            cS += 1
            s_exp[(h, qc, kt)] = cS
            if mask_of(qc, kt) is not None:
                cG += 1
                g_mask[(h, qc, kt)] = cG
            cT += 1
            t_av[(h, qc, kt)] = cT
        cV += 1
        v_recip[(h, qc)] = cV
        cT += 1
        t_bcast[(h, qc)] = cT
        cS += 1
        s_copy[(h, qc)] = cS
        cV += 1
        v_mul[(h, qc)] = cV
    ctx.cnt["eT"] = cT
    ctx.cnt["eS"] = cS
    ctx.cnt["eV"] = cV
    ctx.cnt["eG"] = cG
    ctx.cnt["qA"] = cA
    n_hq = len(hq_order)
    ctx.cnt["qG"] = qG0 + 16 * n_hq
    band_list = sorted(a_bload.keys(), key=lambda k: a_bload[k])

    def f_sync(e):
        w = WT(ctx, e)
        gi = 0
        for kv, h in heads:
            if gi > 0:
                e.wait_ge(ctx.sem["qS"], qload_tgt[(hq_order[gi - 1][1], hq_order[gi - 1][2])])
                w("eV", v_mul[hq_order[gi - 1][1], hq_order[gi - 1][2]])
            r0 = 0
            for (rs, n) in krows(kv):
                e.dma_start(out=Kaug[r0:r0 + n, :], in_=kb[rs:rs + n, :]).then_inc(ctx.sem["qS"], 16)
                r0 += n
            for kt in range(NKT):
                e.dma_start(out=Vaug[kt][:, 0:64],
                            in_=vt[kt * 128:(kt + 1) * 128, vcols(kv):vcols(kv) + 64]).then_inc(ctx.sem["qS"], 16)
            for qc in range(NQC):
                if gi >= 1:
                    e.wait_ge(ctx.sem["qS"], qload_tgt[(hq_order[gi - 1][1], hq_order[gi - 1][2])])
                if gi >= 2:
                    w("eV", v_mul[hq_order[gi - 2][1], hq_order[gi - 2][2]])
                for vi in range(n_var):
                    qa = Qaug[n_var * (gi % 2) + vi]
                    r0 = 0
                    for (rs, n) in qrows(h):
                        e.dma_start(out=qa[r0:r0 + n, :],
                                    in_=qb[rs:rs + n, qc * QW:(qc + 1) * QW]).then_inc(ctx.sem["qS"], 16)
                        r0 += n
                    e.dma_start(out=qa[64:65, :],
                                in_=cbrs[vi][h:h + 1, qc * QW:(qc + 1) * QW]).then_inc(ctx.sem["qS"], 16)
                gi += 1
    ctx.body("sync", f_sync)

    def f_gp(e):
        e.memset(aux1[64:65, :], 1.0).then_inc(ctx.sem["eG"], 1)
        e.memset(Kaug[64:65, :], 1.0).then_inc(ctx.sem["eG"], 1)
        for kt in range(NKT):
            e.memset(Vaug[kt][:, 64:65], 1.0).then_inc(ctx.sem["eG"], 1)
        w = WT(ctx, e)
        if has_mask:
            w("qA", mask_tgt)
        for gi, (kv, h, qc) in enumerate(hq_order):
            ktl = ktl_of(qc)
            for idx, kt in enumerate(ktl):
                key = (h, qc, kt)
                if key in g_mask:
                    w("eS", s_exp[key])
                    mi = mask_of(qc, kt)
                    e.tensor_mul(pbuf[idx % 3][:], pbuf[idx % 3][:],
                                 masks[:, mi * QW:(mi + 1) * QW]).then_inc(ctx.sem["eG"], 1)
            w("eV", v_mul[(h, qc)])
            if gi >= 1:
                w("qG", qG0 + 16 * gi)
            e.dma_start(out=outd[h * 64:(h + 1) * 64, qc * QW:(qc + 1) * QW],
                        in_=obuf[gi % 2][:]).then_inc(ctx.sem["qG"], 16)
    ctx.body("gpsimd", f_gp)

    if has_mask:
        def f_sc_ld(e):
            w = WT(ctx, e)
            w("qA", qA0)
            for j in range(4):
                e.dma_start(out=masks[:, j * QW:(j + 1) * QW],
                            in_=ctx.dram[mask_name][j, :, :]).then_inc(ctx.sem["qA"], 16)
        ctx.body("scalar", f_sc_ld)

    def f_tensor(e):
        w = WT(ctx, e)
        w("eG", eG0 + n_memset)
        for gi, (kv, h, qc) in enumerate(hq_order):
            ktl = ktl_of(qc)
            w("qS", qload_tgt[(h, qc)])
            if gi >= 1:
                ph, pqc = hq_order[gi - 1][1], hq_order[gi - 1][2]
                w("eS", s_copy[(ph, pqc)])
                w("eV", v_recip[(ph, pqc)])
            for idx, kt in enumerate(ktl):
                vi, bi = var_of(qc, kt)
                qa = Qaug[n_var * (gi % 2) + vi]
                if idx >= 2:
                    w("eS", s_exp[(h, qc, ktl[idx - 2])])
                e.matmul(ps_s[idx % 2][:], Kaug[:, kt * 128:(kt + 1) * 128], qa[:],
                         start=True, stop=True, skip_group_check=True).then_inc(ctx.sem["eT"], 1)
                key = (h, qc, kt)
                if key in g_mask:
                    w("eG", g_mask[key])
                else:
                    w("eS", s_exp[key])
                e.matmul(ps_o[:], Vaug[kt][:], pbuf[idx % 3][:],
                         start=(idx == 0), stop=(idx == len(ktl) - 1),
                         skip_group_check=True).then_inc(ctx.sem["eT"], 1)
            w("eV", v_recip[(h, qc)])
            e.matmul(ps_b[:], aux1[64:65, 0:64], rrow[64:65, :], start=True,
                     stop=True, skip_group_check=True).then_inc(ctx.sem["eT"], 1)
    ctx.body("tensor", f_tensor)

    def f_scalar(e):
        w = WT(ctx, e)
        if not has_mask:
            w("qA", qA0)
        bj_c = [0]

        def maybe_bload(key):
            if key in a_bload:
                bj = bj_c[0]
                if bj >= 1:
                    w("qA", a_bload[band_list[bj - 1]])
                if bj >= 2:
                    w("eV", v_badd[band_list[bj - 2]])
                h2, qc2, kt2 = key
                _, bi2 = var_of(qc2, kt2)
                e.dma_start(out=bbuf[bj % 2][:],
                            in_=biasTd[h2, bi2, :, :]).then_inc(ctx.sem["qA"], 16)
                bj_c[0] += 1

        for gi, (kv, h, qc) in enumerate(hq_order):
            ktl = ktl_of(qc)
            for idx, kt in enumerate(ktl):
                key = (h, qc, kt)
                if bias:
                    if idx == 0:
                        maybe_bload(key)
                    if idx + 1 < len(ktl):
                        maybe_bload((h, qc, ktl[idx + 1]))
                if key in v_badd:
                    w("eV", v_badd[key])
                else:
                    w("eT", t_scores[key])
                if idx >= 3:
                    pk = (h, qc, ktl[idx - 3])
                    w("eT", t_av[pk])
                    if pk in g_mask:
                        w("eG", g_mask[pk])
                elif gi >= 1:
                    ph, pqc = hq_order[gi - 1][1], hq_order[gi - 1][2]
                    w("eT", t_av[(ph, pqc, ktl_of(pqc)[-1])])
                e.activation(out=pbuf[idx % 3][:], in_=ps_s[idx % 2][:],
                             func=AF.Exp).then_inc(ctx.sem["eS"], 1)
            w("eT", t_av[(h, qc, ktl[-1])])
            if gi >= 1:
                ph, pqc = hq_order[gi - 1][1], hq_order[gi - 1][2]
                w("eV", v_mul[(ph, pqc)])
            e.copy(of32[:], ps_o[0:64, :]).then_inc(ctx.sem["eS"], 1)
    ctx.body("scalar", f_scalar)

    def f_vec(e):
        w = WT(ctx, e)
        for gi, (kv, h, qc) in enumerate(hq_order):
            ktl = ktl_of(qc)
            if bias:
                for idx, kt in enumerate(ktl):
                    key = (h, qc, kt)
                    if key in v_badd:
                        w("eT", t_scores[key])
                        w("qA", a_bload[key])
                        bj = band_list.index(key)
                        e.tensor_add(ps_s[idx % 2][:], ps_s[idx % 2][:],
                                     bbuf[bj % 2][:]).then_inc(ctx.sem["eV"], 1)
            w("eT", t_av[(h, qc, ktl[-1])])
            e.reciprocal(rrow[64:65, :], ps_o[64:65, :]).then_inc(ctx.sem["eV"], 1)
            w("eT", t_bcast[(h, qc)])
            w("eS", s_copy[(h, qc)])
            if gi >= 2:
                w("qG", qG0 + 16 * (gi - 1))
            e.tensor_mul(obuf[gi % 2][:], of32[:], ps_b[:]).then_inc(ctx.sem["eV"], 1)
    ctx.body("vector", f_vec)


def stage_wo(ctx, w_name, b_name, rhs_name, out_name, out_dtype=F32):
    wsb, wrdy = load_weights_resident(ctx, ctx.dram[w_name], D, D, "wo")
    if b_name is not None:
        btile, brdy = load_bias_cols(ctx, ctx.dram[b_name], 8)
        wrdy = max(wrdy, brdy)
    outd = ctx.dram[out_name]

    def stores(i, mc):
        return [outd[mc * 128:(mc + 1) * 128, i * TC:(i + 1) * TC]]

    def epi(info):
        epi_act_store(ctx, info, stores,
                      bias=(lambda mc: btile[:, mc:mc + 1]) if b_name else None,
                      out_dtype=out_dtype)
    gemm_fm(ctx, wsb=wsb, w_ready=wrdy, NKC=NKD, MC=8,
            rhs_dram=ctx.dram[rhs_name], epi_emit=epi)


def stage_vtr(ctx, src_name, dst_name, rows=D):
    srcd = ctx.dram[src_name]
    dstd = ctx.dram[dst_name]
    NTT = T // 128
    NDC = rows // TC
    tb = [ctx.sb((128, TC), BF16, "vt") for _ in range(3)]
    b = dict(ctx.cnt)
    n_tiles = NTT * NDC
    ld = [b["qS"] + 16 * (j + 1) for j in range(n_tiles)]
    ctx.cnt["qS"] = b["qS"] + 16 * n_tiles
    st0 = b["qG"]
    ctx.cnt["qG"] = st0 + 16 * n_tiles

    def f_sync(e):
        j = 0
        for tt in range(NTT):
            for dc in range(NDC):
                if j >= 1:
                    e.wait_ge(ctx.sem["qS"], ld[j - 1])
                if j >= 3:
                    e.wait_ge(ctx.sem["qG"], st0 + 16 * (j - 2))
                e.dma_start(out=tb[j % 3][:],
                            in_=srcd[dc * TC:(dc + 1) * TC, tt * 128:(tt + 1) * 128],
                            transpose=True).then_inc(ctx.sem["qS"], 16)
                j += 1
    ctx.body("sync", f_sync)

    def f_gp(e):
        j = 0
        for tt in range(NTT):
            for dc in range(NDC):
                e.wait_ge(ctx.sem["qS"], ld[j])
                if j >= 1:
                    e.wait_ge(ctx.sem["qG"], st0 + 16 * j)
                e.dma_start(out=dstd[tt * 128:(tt + 1) * 128, dc * TC:(dc + 1) * TC],
                            in_=tb[j % 3][:]).then_inc(ctx.sem["qG"], 16)
                j += 1
    ctx.body("gpsimd", f_gp)


def stage_subqkv(ctx, i):
    wsb, wrdy = load_weights_resident(ctx, ctx.dram["S_wqkv"][i], D, 3072, "wsq")
    sqr, skr, svf = ctx.dram["sqr"], ctx.dram["skr"], ctx.dram["svf"]

    def stores(ti, mc):
        dst = sqr if mc < 8 else (skr if mc < 16 else svf)
        r = (mc % 8) * 128
        return [dst[r:r + 128, ti * TC:(ti + 1) * TC]]

    def epi(info):
        epi_act_store(ctx, info, stores, out_dtype=BF16)
    gemm_fm(ctx, wsb=wsb, w_ready=wrdy, NKC=NKD, MC=24,
            rhs_dram=ctx.dram[f"xs{2 * i}b"], epi_emit=epi, rhs_shift=True)


def stage_rope(ctx, i, kind):
    assert kind == "xpos"
    qin, kin = ctx.dram["sqr"], ctx.dram["skr"]
    qout, kout = ctx.dram["sq"], ctx.dram["sk"]
    indPT = ctx.dram["indPT"]
    indP = ctx.dram["indP"]

    tbl = {}
    ttmp = ctx.sb((128, T), F32, "ttmp")
    for nm in ("cq", "sq", "ck", "sk"):
        tbl[nm] = ctx.sb((128, T), BF16, "tb" + nm)
    scl = ctx.sb((128, 2), F32, "scl")
    scl_sw = ctx.sb((128, 2), F32, "sclsw")
    ipt = ctx.sb((16, NKD * 128), BF16, "ipt")
    ipq = ctx.sb((128, NKD * H), BF16, "ipq")
    swp = ctx.sb((128, 128), BF16, "swp")
    qt = [ctx.sb((128, TC), BF16, "qt") for _ in range(NKD)]
    kt = [ctx.sb((128, TC), BF16, "kt") for _ in range(NKD)]
    sqb = [ctx.sb((128, TC), BF16, "sqb") for _ in range(NKD)]
    rows = ctx.sb((H, 2 * TC), F32, "rws")
    rqb = ctx.sb((H, TC), BF16, "rqb")
    rkb = ctx.sb((H, TC), BF16, "rkb")
    t1b = [ctx.sb((128, TC), BF16, "t1") for _ in range(2)]
    u1b = [ctx.sb((128, TC), BF16, "u1") for _ in range(2)]
    u2b = [ctx.sb((128, TC), BF16, "u2") for _ in range(2)]
    ob = [ctx.sb((128, TC), BF16, "rob") for _ in range(3)]
    p_qn = ctx.psum((H, TC), F32, "pqn")
    p_kn = ctx.psum((H, TC), F32, "pkn")
    p_rb = [ctx.psum((128, TC), F32, "prb") for _ in range(2)]
    p_sw = [ctx.psum((128, TC), F32, "psw") for _ in range(2)]

    b = dict(ctx.cnt)
    eT0, eS0, eV0, eG0 = b["eT"], b["eS"], b["eV"], b["eG"]
    qS0, qG0, qA0 = b["qS"], b["qG"], b["qA"]

    n_pro = 4 + 2 + NKD + NKD + 1
    pro_tgt = qA0 + 16 * n_pro
    ctx.cnt["qA"] = pro_tgt

    def f_sc_pro(e):
        e.wait_ge(ctx.sem["qA"], qA0)
        e.dma_start(out=scl[:, 0], in_=ctx.dram["S_qs"][i, :]).then_inc(ctx.sem["qA"], 16)
        e.dma_start(out=scl[:, 1], in_=ctx.dram["S_ks"][i, :]).then_inc(ctx.sem["qA"], 16)
        e.dma_start(out=scl_sw[0:64, 0], in_=ctx.dram["S_qs"][i, 64:128]).then_inc(ctx.sem["qA"], 16)
        e.dma_start(out=scl_sw[64:128, 0], in_=ctx.dram["S_qs"][i, 0:64]).then_inc(ctx.sem["qA"], 16)
        e.dma_start(out=scl_sw[0:64, 1], in_=ctx.dram["S_ks"][i, 64:128]).then_inc(ctx.sem["qA"], 16)
        e.dma_start(out=scl_sw[64:128, 1], in_=ctx.dram["S_ks"][i, 0:64]).then_inc(ctx.sem["qA"], 16)
        for c in range(NKD):
            e.dma_start(out=ipt[:, c * 128:(c + 1) * 128], in_=indPT[c, :, :]).then_inc(ctx.sem["qA"], 16)
        for c in range(NKD):
            e.dma_start(out=ipq[:, c * H:(c + 1) * H], in_=indP[c, :, :]).then_inc(ctx.sem["qA"], 16)
        e.dma_start(out=swp[:], in_=ctx.dram["swap64"][:, :]).then_inc(ctx.sem["qA"], 16)
        names = {"cq": "xpos_cq", "sq": "xpos_sq", "ck": "xpos_ck", "sk": "xpos_sk"}
        for j, nm in enumerate(("cq", "sq", "ck", "sk")):
            if j >= 1:
                e.wait_ge(ctx.sem["eV"], eV0 + j)
                e.wait_ge(ctx.sem["qA"], qA0 + 16 * (23 + j))
            e.dma_start(out=ttmp[:], in_=ctx.dram[names[nm]][:, :]).then_inc(ctx.sem["qA"], 16)
    ctx.body("scalar", f_sc_pro)
    ctx.cnt["qA"] = qA0 + 16 * (23 + 4)
    pro_tgt = ctx.cnt["qA"]

    def f_v_pro(e):
        for j, nm in enumerate(("cq", "sq", "ck", "sk")):
            e.wait_ge(ctx.sem["qA"], qA0 + 16 * (23 + j + 1))
            col = 0 if nm in ("cq", "sq") else 1
            stile = scl if nm in ("cq", "ck") else scl_sw
            e.tensor_scalar_mul(tbl[nm][:], ttmp[:], stile[:, col:col + 1]).then_inc(ctx.sem["eV"], 1)
    ctx.body("vector", f_v_pro)
    ctx.cnt["eV"] = eV0 + 4
    eVp = ctx.cnt["eV"]

    ld1 = []
    cur = ctx.cnt["qS"]
    for ti in range(NT):
        cur += 16 * (2 * NKD)
        ld1.append(cur)
    ctx.cnt["qS"] = cur

    NVsq = 2 * NKD
    NVrow = 6
    NVtile = 4 * 2 * NKD
    NV = NVsq + NVrow + NVtile
    NSt = 2
    NTt = 2 + 2 * 2 * NKD
    eVb = lambda ti: eVp + NV * ti
    eTb = lambda ti: eT0 + NTt * ti
    eSb = lambda ti: eS0 + NSt * ti
    st0 = qG0
    ctx.cnt["eV"] = eVp + NV * NT
    ctx.cnt["eS"] = eS0 + NSt * NT
    ctx.cnt["eT"] = eT0 + NTt * NT
    ctx.cnt["qG"] = st0 + 16 * 2 * NKD * NT

    def f_sync(e):
        e.wait_ge(ctx.sem["qS"], qS0)
        for ti in range(NT):
            if ti >= 1:
                e.wait_ge(ctx.sem["qS"], ld1[ti - 1])
                e.wait_ge(ctx.sem["eV"], eVb(ti))
            for c in range(NKD):
                e.dma_start(out=qt[c][:], in_=qin[c * 128:(c + 1) * 128, ti * TC:(ti + 1) * TC]).then_inc(ctx.sem["qS"], 16)
            for c in range(NKD):
                e.dma_start(out=kt[c][:], in_=kin[c * 128:(c + 1) * 128, ti * TC:(ti + 1) * TC]).then_inc(ctx.sem["qS"], 16)
    ctx.body("sync", f_sync)

    def f_vec(e):
        w = WT(ctx, e)
        for ti in range(NT):
            bv = eVb(ti)
            w("qS", ld1[ti])
            if ti >= 1:
                w("eT", eTb(ti))
            for c in range(NKD):
                e.tensor_mul(sqb[c][:], qt[c][:], qt[c][:]).then_inc(ctx.sem["eV"], 1)
            w("eT", eTb(ti) + 1)
            for c in range(NKD):
                e.tensor_mul(sqb[c][:], kt[c][:], kt[c][:]).then_inc(ctx.sem["eV"], 1)
            w("eS", eSb(ti) + 2)
            e.tensor_scalar_max(rows[:, 0:TC], rows[:, 0:TC], 1e-12).then_inc(ctx.sem["eV"], 1)
            w("eV", bv + NVsq + 1)
            e.reciprocal(rows[:, 0:TC], rows[:, 0:TC]).then_inc(ctx.sem["eV"], 1)
            w("eV", bv + NVsq + 2)
            e.tensor_copy(rqb[:], rows[:, 0:TC]).then_inc(ctx.sem["eV"], 1)
            e.tensor_scalar_max(rows[:, TC:2 * TC], rows[:, TC:2 * TC], 1e-12).then_inc(ctx.sem["eV"], 1)
            w("eV", bv + NVsq + 4)
            e.reciprocal(rows[:, TC:2 * TC], rows[:, TC:2 * TC]).then_inc(ctx.sem["eV"], 1)
            w("eV", bv + NVsq + 5)
            e.tensor_copy(rkb[:], rows[:, TC:2 * TC]).then_inc(ctx.sem["eV"], 1)
            for j in range(2 * NKD):
                c = j % NKD
                is_q = j < NKD
                raw = qt[c] if is_q else kt[c]
                cs = tbl["cq"] if is_q else tbl["ck"]
                ss = tbl["sq"] if is_q else tbl["sk"]
                bt = bv + NVsq + NVrow + 4 * j
                tt = eTb(ti) + 2 + 2 * j
                w("eT", tt + 1)
                e.tensor_mul(t1b[j % 2][:], raw[:], p_rb[j % 2][:]).then_inc(ctx.sem["eV"], 1)
                w("eV", bt + 1)
                e.tensor_mul(u1b[j % 2][:], t1b[j % 2][:],
                             cs[:, ti * TC:(ti + 1) * TC]).then_inc(ctx.sem["eV"], 1)
                w("eT", tt + 2)
                e.tensor_mul(u2b[j % 2][:], p_sw[j % 2][:],
                             ss[:, ti * TC:(ti + 1) * TC]).then_inc(ctx.sem["eV"], 1)
                w("eV", bt + 3)
                jj = 2 * NKD * ti + j
                if jj >= 3:
                    w("qG", st0 + 16 * (jj - 2))
                e.tensor_add(ob[jj % 3][:], u1b[j % 2][:], u2b[j % 2][:]).then_inc(ctx.sem["eV"], 1)
    ctx.body("vector", f_vec)

    def f_tensor(e):
        w = WT(ctx, e)
        w("qA", pro_tgt)
        for ti in range(NT):
            bv = eVb(ti)
            w("qS", ld1[ti])
            w("eV", bv + NKD)
            if ti >= 1:
                w("eS", eSb(ti))
            ins = None
            for c in range(NKD):
                ins = e.matmul(p_qn[:], ipq[:, c * H:(c + 1) * H], sqb[c][:], start=(c == 0),
                               stop=(c == NKD - 1), skip_group_check=True)
            ins.then_inc(ctx.sem["eT"], 1)
            w("eV", bv + 2 * NKD)
            for c in range(NKD):
                ins = e.matmul(p_kn[:], ipq[:, c * H:(c + 1) * H], sqb[c][:], start=(c == 0),
                               stop=(c == NKD - 1), skip_group_check=True)
            ins.then_inc(ctx.sem["eT"], 1)
            for j in range(2 * NKD):
                c = j % NKD
                is_q = j < NKD
                raw = qt[c] if is_q else kt[c]
                rrow_b = rqb if is_q else rkb
                bt = bv + NVsq + NVrow + 4 * j
                w("eV", bv + NVsq + (3 if is_q else NVrow))
                if j >= 2:
                    w("eV", eVb(ti) + NVsq + NVrow + 4 * (j - 2) + 2)
                e.matmul(p_rb[j % 2][:], ipt[:, c * 128:(c + 1) * 128], rrow_b[:],
                         start=True, stop=True, skip_group_check=True).then_inc(ctx.sem["eT"], 1)
                w("eV", bt + 1)
                if j >= 2:
                    w("eV", eVb(ti) + NVsq + NVrow + 4 * (j - 2) + 4)
                e.matmul(p_sw[j % 2][:], swp[:], t1b[j % 2][:],
                         start=True, stop=True, skip_group_check=True).then_inc(ctx.sem["eT"], 1)
    ctx.body("tensor", f_tensor)

    def f_scalar(e):
        w = WT(ctx, e)
        for ti in range(NT):
            w("eT", eTb(ti) + 1)
            e.activation(out=rows[:, 0:TC], in_=p_qn[:], func=AF.Sqrt).then_inc(ctx.sem["eS"], 1)
            w("eT", eTb(ti) + 2)
            e.activation(out=rows[:, TC:2 * TC], in_=p_kn[:], func=AF.Sqrt).then_inc(ctx.sem["eS"], 1)
    ctx.body("scalar", f_scalar)

    def f_gp(e):
        for ti in range(NT):
            for j in range(2 * NKD):
                c = j % NKD
                dst = qout if j < NKD else kout
                e.wait_ge(ctx.sem["eV"], eVb(ti) + NVsq + NVrow + 4 * (j + 1))
                jj = 2 * NKD * ti + j
                if jj >= 1:
                    e.wait_ge(ctx.sem["qG"], st0 + 16 * jj)
                e.dma_start(out=dst[c * 128:(c + 1) * 128, ti * TC:(ti + 1) * TC],
                            in_=ob[jj % 3][:]).then_inc(ctx.sem["qG"], 16)
    ctx.body("gpsimd", f_gp)


def epi_resid_store(ctx, info, res_name, out_f32_name, out_bf16_name, bias_tile=None):
    MC, NB, psums = info["MC"], info["NB"], info["psums"]
    resd = ctx.dram[res_name]
    of = ctx.dram[out_f32_name]
    obf = ctx.dram[out_bf16_name] if out_bf16_name else None
    rbufs = [ctx.sb((128, TC), F32, "res") for _ in range(3)]
    xo = [ctx.sb((128, TC), F32, "eo") for _ in range(3)]
    xob = [ctx.sb((128, TC), BF16, "eob") for _ in range(3)]
    epi0, mm0 = info["epi0"], info["mm0"]
    b = dict(ctx.cnt)
    qA0, eV0_, st0 = b["qA"], b["eV"], b["qG"]
    n_iter = NT * MC
    n_dst = 2 if obf is not None else 1
    rld = [qA0 + 16 * (it + 1) for it in range(n_iter)]
    ctx.cnt["qA"] = qA0 + 16 * n_iter
    ctx.cnt["eV"] = eV0_ + (n_iter if obf is not None else 0)
    ctx.cnt["qG"] = st0 + 16 * n_dst * n_iter

    def f_sc(e):
        w = WT(ctx, e)
        w("qA", qA0)
        for it in range(n_iter):
            i, mc = it // MC, it % MC
            if it >= 1:
                w("qA", rld[it - 1])
            if it >= 3:
                if obf is not None:
                    w("eV", eV0_ + it - 2)
                else:
                    w("eS", epi0 + it - 2)
            e.dma_start(out=rbufs[it % 3][:],
                        in_=resd[mc * 128:(mc + 1) * 128, i * TC:(i + 1) * TC]).then_inc(ctx.sem["qA"], 16)
    ctx.body("scalar", f_sc)

    def f_vec(e):
        w = WT(ctx, e)
        for it in range(n_iter):
            i, mc = it // MC, it % MC
            w("eT", mm0 + it + 1)
            w("qA", rld[it])
            if it >= 3:
                w("qG", st0 + 16 * n_dst * (it - 2))
            if bias_tile is not None:
                ins = e.scalar_tensor_tensor(xo[it % 3][:], psums[it % NB][:],
                                             bias_tile[:, mc:mc + 1], rbufs[it % 3][:],
                                             ALU.add, ALU.add)
            else:
                ins = e.tensor_add(xo[it % 3][:], psums[it % NB][:], rbufs[it % 3][:])
            ins.then_inc(ctx.sem["eS"], 1)
            if obf is not None:
                w("eS", epi0 + it + 1)
                e.tensor_copy(xob[it % 3][:], xo[it % 3][:]).then_inc(ctx.sem["eV"], 1)
    ctx.body("vector", f_vec)

    def f_gp(e):
        w = WT(ctx, e)
        for it in range(n_iter):
            i, mc = it // MC, it % MC
            w("eS", epi0 + it + 1)
            if obf is not None:
                w("eV", eV0_ + it + 1)
            if it >= 1:
                w("qG", st0 + 16 * n_dst * it)
            r0, c0 = mc * 128, i * TC
            e.dma_start(out=of[r0:r0 + 128, c0:c0 + TC], in_=xo[it % 3][:]).then_inc(ctx.sem["qG"], 16)
            if obf is not None:
                e.dma_start(out=obf[r0:r0 + 128, c0:c0 + TC], in_=xob[it % 3][:]).then_inc(ctx.sem["qG"], 16)
    ctx.body("gpsimd", f_gp)


def stage_subwo(ctx, i):
    wsb, wrdy = load_weights_resident(ctx, ctx.dram["S_wo"][i], D, D, "wso")

    def epi(info):
        epi_resid_store(ctx, info, f"xs{2 * i}", f"xs{2 * i + 1}", f"xs{2 * i + 1}b")
    gemm_fm(ctx, wsb=wsb, w_ready=wrdy, NKC=NKD, MC=8,
            rhs_dram=ctx.dram["sap"], epi_emit=epi)


def stage_subf1(ctx, i):
    xsrc = ctx.dram[f"xs{2 * i + 1}"]
    gactd = ctx.dram["gact"]
    augrd = ctx.dram["augrd"]
    M1 = 5632
    NPAIR = 22
    wsb, wrdy = load_weights_resident(ctx, ctx.dram["S_w1"][i], D, M1, "w1r", nbuf=1)
    zcol = ctx.dram["zeros128"]

    gb = ctx.sb((128, 2 * NKD), BF16, "gb")
    gcol = ctx.sb((128, NKD), F32, "gcol")
    blncol = ctx.sb((128, NKD), F32, "blnc")
    augw = ctx.sb((2, M1), BF16, "augw")
    ones1 = ctx.sb((128, 1), F32, "on1")
    onesr = ctx.sb((1, 128), F32, "onr")
    xt = [ctx.sb((128, TC), F32, "x") for _ in range(NKD)]
    sq = [ctx.sb((128, TC), F32, "sq") for _ in range(NKD)]
    xg = [[ctx.sb((128, TC), BF16, "xg") for _ in range(NKD)] for _ in range(2)]
    rows = ctx.sb((1, 4 * TC), F32, "rw")
    rows2 = ctx.sb((1, 2 * TC), BF16, "rw2")
    augr = [ctx.sb((2, TC), BF16, "ar") for _ in range(2)]
    rstdb = ctx.sb((128, TC), F32, "rsb")
    tg = ctx.sb((128, TC), F32, "tg")
    gg = ctx.sb((128, TC), BF16, "gg")
    ta = ctx.sb((128, TC), F32, "ta")
    ob = [ctx.sb((128, TC), BF16, "go") for _ in range(3)]
    p_a = ctx.psum((128, TC), F32, "pa")
    p_g = ctx.psum((128, TC), F32, "pg")
    p_w = ctx.psum((2, TC), F32, "pw")
    p_s = ctx.psum((1, TC), F32, "ps")
    p_q = ctx.psum((1, TC), F32, "pq")
    p_r = ctx.psum((128, TC), F32, "pr")

    b = dict(ctx.cnt)
    eG0 = b["eG"]
    qS0, qA0 = b["qS"], b["qA"]
    eV0 = ctx.cnt["eV"]
    eS0 = ctx.cnt["eS"]

    def f_sc_pro(e):
        e.wait_ge(ctx.sem["qA"], qA0)
        for c in range(NKD):
            e.dma_start(out=gcol[:, c], in_=ctx.dram["S_g"][i, c * 128:(c + 1) * 128]).then_inc(ctx.sem["qA"], 16)
        for c in range(NKD):
            e.dma_start(out=blncol[:, c], in_=ctx.dram["S_b"][i, c * 128:(c + 1) * 128]).then_inc(ctx.sem["qA"], 16)
    ctx.body("scalar", f_sc_pro)
    ctx.cnt["qA"] = qA0 + 16 * 2 * NKD
    pro1 = ctx.cnt["qA"]

    def f_v_pro(e):
        e.wait_ge(ctx.sem["qA"], pro1)
        for c in range(NKD):
            e.tensor_copy(gb[:, 2 * c:2 * c + 1], gcol[:, c:c + 1]).then_inc(ctx.sem["eV"], 1)
            e.tensor_copy(gb[:, 2 * c + 1:2 * c + 2], blncol[:, c:c + 1]).then_inc(ctx.sem["eV"], 1)
    ctx.body("vector", f_v_pro)
    ctx.cnt["eV"] = eV0 + 2 * NKD
    eVp = ctx.cnt["eV"]

    def f_g_pro(e):
        e.memset(ones1[:], 1.0).then_inc(ctx.sem["eG"], 1)
        e.memset(onesr[:], 1.0).then_inc(ctx.sem["eG"], 1)
    ctx.body("gpsimd", f_g_pro)
    ctx.cnt["eG"] = eG0 + 2

    NCH = M1 // TC
    eTp0 = ctx.cnt["eT"]

    def f_t_pro(e):
        e.wait_ge(ctx.sem["eV"], eVp)
        for nch in range(NCH):
            if nch >= 1:
                e.wait_ge(ctx.sem["eS"], eS0 + nch)
            ins = None
            for kc in range(NKD):
                ins = e.matmul(p_w[:], gb[:, 2 * kc:2 * kc + 2],
                               wsb[:, kc * M1 + nch * TC: kc * M1 + (nch + 1) * TC],
                               start=(kc == 0), stop=(kc == NKD - 1), skip_group_check=True)
            ins.then_inc(ctx.sem["eT"], 1)
    ctx.body("tensor", f_t_pro)
    ctx.cnt["eT"] = eTp0 + NCH

    def f_s_pro(e):
        for nch in range(NCH):
            e.wait_ge(ctx.sem["eT"], eTp0 + nch + 1)
            e.activation(out=augw[:, nch * TC:(nch + 1) * TC], in_=p_w[:],
                         func=AF.Copy).then_inc(ctx.sem["eS"], 1)
    ctx.body("scalar", f_s_pro)
    ctx.cnt["eS"] = eS0 + NCH

    def f_g_pro2(e):
        e.wait_ge(ctx.sem["eS"], eS0 + NCH)
        e.dma_start(out=augw[1:2, :], in_=ctx.dram["S_b1"][i, :],
                    accum_op=ALU.add).then_inc(ctx.sem["qG"], 16)
    ctx.body("gpsimd", f_g_pro2)
    ctx.cnt["qG"] += 16
    aug_qg = ctx.cnt["qG"]

    b2 = dict(ctx.cnt)
    eT20, eS20, eV20 = b2["eT"], b2["eS"], b2["eV"]
    st0 = b2["qG"]
    ld1 = []
    cur = b2["qS"]
    for ti in range(NT):
        cur += 16 * (NKD + (NKD // 2 if ti == 0 else 0))
        ld1.append(cur)
    ctx.cnt["qS"] = cur
    NS2 = 10 + NPAIR
    NT2 = 2 + 2 * NPAIR
    NV2 = 8 + NKD + 3 * NPAIR
    ctx.cnt["qA"] = pro1 + 16 * 3 * NT
    ctx.cnt["eS"] = eS20 + NS2 * NT
    ctx.cnt["eT"] = eT20 + NT2 * NT
    ctx.cnt["eV"] = eV20 + NV2 * NT
    ctx.cnt["qG"] = st0 + 16 * NPAIR * NT

    def load_shift(e, ti, kc, dst):
        if kc >= NKD // 2:
            if ti == 0:
                e.dma_start(out=dst[:, 0:1], in_=zcol[:, :]).then_inc(ctx.sem["qS"], 16)
                e.dma_start(out=dst[:, 1:TC], in_=xsrc[kc * 128:(kc + 1) * 128, 0:TC - 1]).then_inc(ctx.sem["qS"], 16)
            else:
                e.dma_start(out=dst[:], in_=xsrc[kc * 128:(kc + 1) * 128, ti * TC - 1:(ti + 1) * TC - 1]).then_inc(ctx.sem["qS"], 16)
        else:
            e.dma_start(out=dst[:], in_=xsrc[kc * 128:(kc + 1) * 128, ti * TC:(ti + 1) * TC]).then_inc(ctx.sem["qS"], 16)

    def f_sync(e):
        e.wait_ge(ctx.sem["qS"], qS0)
        for ti in range(NT):
            if ti >= 1:
                e.wait_ge(ctx.sem["qS"], ld1[ti - 1])
                e.wait_ge(ctx.sem["eV"], eV20 + NV2 * (ti - 1) + 8 + NKD)
            for kc in range(NKD):
                load_shift(e, ti, kc, xt[kc])
    ctx.body("sync", f_sync)

    def f_scalar(e):
        w = WT(ctx, e)
        for ti in range(NT):
            bs = eS20 + NS2 * ti
            bv = eV20 + NV2 * ti
            bt = eT20 + NT2 * ti
            w("qS", ld1[ti])
            if ti >= 1:
                w("eT", eT20 + NT2 * ti)
            for kc in range(NKD):
                e.square(sq[kc][:], xt[kc][:]).then_inc(ctx.sem["eS"], 1)
            w("eV", bv + 5)
            e.activation(out=rows[:, 3 * TC:4 * TC], in_=rows[:, 2 * TC:3 * TC],
                         func=AF.Sqrt).then_inc(ctx.sem["eS"], 1)
            w("eT", bt + 2)
            if ti >= 1:
                w("eV", eV20 + NV2 * (ti - 1) + NV2)
            e.copy(rstdb[:], p_r[:]).then_inc(ctx.sem["eS"], 1)
            qa_b = pro1 + 16 * 3 * ti
            w("eV", bv + 8)
            if ti >= 2:
                w("eT", eT20 + NT2 * (ti - 1))
            e.dma_start(out=augrd[0:1, ti * TC:(ti + 1) * TC], in_=rows2[:, 0:TC]).then_inc(ctx.sem["qA"], 16)
            e.dma_start(out=augrd[1:2, ti * TC:(ti + 1) * TC], in_=rows2[:, TC:2 * TC]).then_inc(ctx.sem["qA"], 16)
            w("qA", qa_b + 32)
            e.dma_start(out=augr[ti % 2][:], in_=augrd[:, ti * TC:(ti + 1) * TC]).then_inc(ctx.sem["qA"], 16)
            for pj in range(NPAIR):
                pv = bv + 8 + NKD + 3 * pj
                w("eV", pv + 2)
                e.activation(out=gg[:], in_=tg[:], func=AF.Gelu).then_inc(ctx.sem["eS"], 1)
    ctx.body("scalar", f_scalar)

    def f_tensor(e):
        w = WT(ctx, e)
        w("eV", wrdy)
        w("qG", aug_qg)
        w("eG", eG0 + 2)
        for ti in range(NT):
            bt = eT20 + NT2 * ti
            bv = eV20 + NV2 * ti
            bs = eS20 + NS2 * ti
            w("qS", ld1[ti])
            w("eS", bs + 8)
            if ti >= 1:
                w("eV", eV20 + NV2 * (ti - 1) + 2)
            for kc in range(NKD):
                e.matmul(p_s[:], ones1[:], xt[kc][:], start=(kc == 0),
                         stop=(kc == NKD - 1), skip_group_check=True)
            ins = None
            for kc in range(NKD):
                ins = e.matmul(p_q[:], ones1[:], sq[kc][:], start=(kc == 0),
                               stop=(kc == NKD - 1), skip_group_check=True)
            ins.then_inc(ctx.sem["eT"], 1)
            w("eV", bv + 6)
            if ti >= 1:
                w("eS", eS20 + NS2 * (ti - 1) + 10)
            e.matmul(p_r[:], onesr[:], rows[:, TC:2 * TC], start=True,
                     stop=True, skip_group_check=True).then_inc(ctx.sem["eT"], 1)
            w("eV", bv + 8 + NKD)
            w("qA", pro1 + 16 * 3 * (ti + 1))
            for pj in range(NPAIR):
                if pj >= 1:
                    w("eV", bv + 8 + NKD + 3 * (pj - 1) + 2)
                for (mc, bank) in ((pj, p_a), (NPAIR + pj, p_g)):
                    ins = None
                    for kc in range(NKD):
                        ins = e.matmul(bank[:], wsb[:, kc * M1 + mc * 128: kc * M1 + mc * 128 + 128],
                                       xg[ti % 2][kc][:], start=(kc == 0), stop=False,
                                       skip_group_check=True)
                    ins = e.matmul(bank[:], augw[:, mc * 128:mc * 128 + 128],
                                   augr[ti % 2][:], start=False, stop=True,
                                   skip_group_check=True)
                    ins.then_inc(ctx.sem["eT"], 1)
    ctx.body("tensor", f_tensor)

    def f_vec(e):
        w = WT(ctx, e)
        w("qA", pro1)
        for ti in range(NT):
            bv = eV20 + NV2 * ti
            bt = eT20 + NT2 * ti
            bs = eS20 + NS2 * ti
            w("eT", bt + 1)
            e.tensor_scalar_mul(rows[:, 0:TC], p_s[:], 1.0 / D).then_inc(ctx.sem["eV"], 1)
            e.tensor_scalar_mul(rows[:, TC:2 * TC], p_q[:], 1.0 / D).then_inc(ctx.sem["eV"], 1)
            w("eV", bv + 2)
            e.tensor_tensor(rows[:, 2 * TC:3 * TC], rows[:, 0:TC], rows[:, 0:TC],
                            ALU.mult).then_inc(ctx.sem["eV"], 1)
            w("eV", bv + 3)
            e.tensor_sub(rows[:, 2 * TC:3 * TC], rows[:, TC:2 * TC],
                         rows[:, 2 * TC:3 * TC]).then_inc(ctx.sem["eV"], 1)
            w("eV", bv + 4)
            e.tensor_scalar_add(rows[:, 2 * TC:3 * TC], rows[:, 2 * TC:3 * TC],
                                1e-5).then_inc(ctx.sem["eV"], 1)
            w("eS", bs + 9)
            e.reciprocal(rows[:, TC:2 * TC], rows[:, 3 * TC:4 * TC]).then_inc(ctx.sem["eV"], 1)
            e.tensor_scalar_mul(rows2[:, 0:TC], rows[:, 0:TC], -1.0).then_inc(ctx.sem["eV"], 1)
            e.tensor_copy(rows2[:, TC:2 * TC], rows[:, 3 * TC:4 * TC]).then_inc(ctx.sem["eV"], 1)
            for kc in range(NKD):
                e.tensor_scalar_mul(xg[ti % 2][kc][:], xt[kc][:],
                                    gcol[:, kc:kc + 1]).then_inc(ctx.sem["eV"], 1)
            for pj in range(NPAIR):
                pv = bv + 8 + NKD + 3 * pj
                if pj >= 1:
                    w("eV", pv)
                w("eS", bs + 10)
                w("eT", bt + 2 + 2 * pj + 1)
                e.tensor_mul(ta[:], p_a[:], rstdb[:]).then_inc(ctx.sem["eV"], 1)
                w("eT", bt + 2 + 2 * pj + 2)
                e.tensor_mul(tg[:], p_g[:], rstdb[:]).then_inc(ctx.sem["eV"], 1)
                w("eS", bs + 10 + pj + 1)
                jj = NPAIR * ti + pj
                if jj >= 3:
                    w("qG", st0 + 16 * (jj - 2))
                e.tensor_mul(ob[jj % 3][:], ta[:], gg[:]).then_inc(ctx.sem["eV"], 1)
    ctx.body("vector", f_vec)

    def f_gp(e):
        for ti in range(NT):
            for pj in range(NPAIR):
                jj = NPAIR * ti + pj
                e.wait_ge(ctx.sem["eV"], eV20 + NV2 * ti + 8 + NKD + 3 * pj + 3)
                if jj >= 1:
                    e.wait_ge(ctx.sem["qG"], st0 + 16 * jj)
                e.dma_start(out=gactd[pj * 128:(pj + 1) * 128, ti * TC:(ti + 1) * TC],
                            in_=ob[jj % 3][:]).then_inc(ctx.sem["qG"], 16)
    ctx.body("gpsimd", f_gp)


def stage_subf2(ctx, i):
    wsb, wrdy = load_weights_resident(ctx, ctx.dram["S_w2"][i], 2816, D, "w2r")
    btile, brdy = load_bias_cols(ctx, ctx.dram["S_b2"][i], 8)

    def epi(info):
        epi_resid_store(ctx, info, f"xs{2 * i + 1}", f"xs{2 * i + 2}",
                        f"xs{2 * i + 2}b", bias_tile=btile)
    gemm_fm(ctx, wsb=wsb, w_ready=max(wrdy, brdy), NKC=22, MC=8,
            rhs_dram=ctx.dram["gact"], epi_emit=epi)


def stage_add3(ctx, a_name, b_name, c_name, out_f32, out_bf16=None):
    A, B, C = ctx.dram[a_name], ctx.dram[b_name], ctx.dram[c_name]
    OF = ctx.dram[out_f32]
    OB = ctx.dram[out_bf16] if out_bf16 else None
    at = [ctx.sb((128, TC), F32, "aa") for _ in range(2)]
    bt = [ctx.sb((128, TC), F32, "bb") for _ in range(2)]
    ct = [ctx.sb((128, TC), F32, "cc") for _ in range(2)]
    xo = [ctx.sb((128, TC), F32, "xo") for _ in range(2)]
    xob = [ctx.sb((128, TC), BF16, "xb") for _ in range(2)] if OB else None
    b = dict(ctx.cnt)
    qS0, st0, eV0 = b["qS"], b["qG"], b["eV"]
    n_it = NT * NKD
    n_dst = 2 if OB else 1
    nv = 3 if OB else 2
    ld = [qS0 + 16 * 3 * (j + 1) for j in range(n_it)]
    ctx.cnt["qS"] = qS0 + 16 * 3 * n_it
    ctx.cnt["eV"] = eV0 + nv * n_it
    ctx.cnt["qG"] = st0 + 16 * n_dst * n_it

    def f_sync(e):
        e.wait_ge(ctx.sem["qS"], qS0)
        for j in range(n_it):
            i, kc = j // NKD, j % NKD
            if j >= 1:
                e.wait_ge(ctx.sem["qS"], ld[j - 1])
            if j >= 2:
                e.wait_ge(ctx.sem["eV"], eV0 + nv * (j - 1))
            sl = (slice(kc * 128, (kc + 1) * 128), slice(i * TC, (i + 1) * TC))
            e.dma_start(out=at[j % 2][:], in_=A[sl[0], sl[1]]).then_inc(ctx.sem["qS"], 16)
            e.dma_start(out=bt[j % 2][:], in_=B[sl[0], sl[1]]).then_inc(ctx.sem["qS"], 16)
            e.dma_start(out=ct[j % 2][:], in_=C[sl[0], sl[1]]).then_inc(ctx.sem["qS"], 16)
    ctx.body("sync", f_sync)

    def f_vec(e):
        w = WT(ctx, e)
        for j in range(n_it):
            bv = eV0 + nv * j
            w("qS", ld[j])
            if j >= 2:
                w("qG", st0 + 16 * n_dst * (j - 1))
            e.tensor_add(xo[j % 2][:], at[j % 2][:], bt[j % 2][:]).then_inc(ctx.sem["eV"], 1)
            w("eV", bv + 1)
            e.tensor_add(xo[j % 2][:], xo[j % 2][:], ct[j % 2][:]).then_inc(ctx.sem["eV"], 1)
            if OB:
                w("eV", bv + 2)
                e.tensor_copy(xob[j % 2][:], xo[j % 2][:]).then_inc(ctx.sem["eV"], 1)
    ctx.body("vector", f_vec)

    def f_gp(e):
        for j in range(n_it):
            i, kc = j // NKD, j % NKD
            e.wait_ge(ctx.sem["eV"], eV0 + nv * (j + 1))
            if j >= 1:
                e.wait_ge(ctx.sem["qG"], st0 + 16 * n_dst * j)
            sl = (slice(kc * 128, (kc + 1) * 128), slice(i * TC, (i + 1) * TC))
            e.dma_start(out=OF[sl[0], sl[1]], in_=xo[j % 2][:]).then_inc(ctx.sem["qG"], 16)
            if OB:
                e.dma_start(out=OB[sl[0], sl[1]], in_=xob[j % 2][:]).then_inc(ctx.sem["qG"], 16)
    ctx.body("gpsimd", f_gp)


def stage_qk_generic(ctx, w_ap, K, M2, q_out, k_out, bias_name=None, q_scale=1.0):
    raise NotImplementedError


def stage_tbias(ctx):
    relT = ctx.dram["relT"]
    ohr = ctx.dram["OHrev"]
    grevd = ctx.dram["grev"]
    biasTd = ctx.dram["biasT"]

    rel = ctx.sb((32, 16), F32, "rel")
    relTt = ctx.sb((16, 32), F32, "relt")
    ohrt = ctx.sb((32, 255), F32, "ohr")
    gr = ctx.sb((16, 1280), F32, "gr")
    ones = ctx.sb((16, 512), F32, "onid")
    idxv = ctx.sb((128, 1), I32, "idx")
    btile = [ctx.sb((128, TC), F32, "bt") for _ in range(2)]
    p_g = ctx.psum((16, 255), F32, "pgm")

    b = dict(ctx.cnt)
    qS0, qG0, eT0, eS0, eV0, eG0 = b["qS"], b["qG"], b["eT"], b["eS"], b["eV"], b["eG"]

    def f_sync(e):
        e.wait_ge(ctx.sem["qS"], qS0)
        e.dma_start(out=relTt[:], in_=relT[:, :]).then_inc(ctx.sem["qS"], 16)
        e.dma_start(out=ohrt[:], in_=ohr[:, :]).then_inc(ctx.sem["qS"], 16)
        e.dma_start(out=rel[:], in_=ctx.dram["rel32"][:, :]).then_inc(ctx.sem["qS"], 16)
    ctx.body("sync", f_sync)
    ctx.cnt["qS"] = qS0 + 48

    def f_gp0(e):
        e.memset(ones[:], 1.0).then_inc(ctx.sem["eG"], 1)
    ctx.body("gpsimd", f_gp0)
    ctx.cnt["eG"] = eG0 + 1

    def f_t(e):
        e.wait_ge(ctx.sem["qS"], qS0 + 48)
        e.matmul(p_g[:], rel[:], ohrt[:], start=True, stop=True,
                 skip_group_check=True).then_inc(ctx.sem["eT"], 1)
    ctx.body("tensor", f_t)
    ctx.cnt["eT"] = eT0 + 1

    def f_v(e):
        e.wait_ge(ctx.sem["qS"], qS0 + 48)
        e.wait_ge(ctx.sem["eG"], eG0 + 1)
        e.tensor_scalar_mul(gr[:, 0:512], ones[:, 0:512], relTt[:, 31:32]).then_inc(ctx.sem["eV"], 1)
        e.tensor_scalar_mul(gr[:, 512:513], ones[:, 0:1], relTt[:, 31:32]).then_inc(ctx.sem["eV"], 1)
        e.tensor_scalar_mul(gr[:, 768:1280], ones[:, 0:512], relTt[:, 15:16]).then_inc(ctx.sem["eV"], 1)
        e.wait_ge(ctx.sem["eT"], eT0 + 1)
        e.tensor_copy(gr[:, 513:768], p_g[:]).then_inc(ctx.sem["eV"], 1)
    ctx.body("vector", f_v)
    ctx.cnt["eV"] = eV0 + 4

    def f_gp1(e):
        e.wait_ge(ctx.sem["eV"], eV0 + 4)
        e.dma_start(out=grevd[:, :], in_=gr[:]).then_inc(ctx.sem["qG"], 16)
        e.wait_ge(ctx.sem["qG"], qG0 + 16)
        grf = grevd.rearrange("h n -> (h n) ()")
        j = 0
        for h in range(H):
            for bi, d0 in enumerate((-128, 0, 128, 256, 384, 512)):
                base = h * 1280 + 640 - d0 - 127
                e.iota(idxv[:, 0:1], pattern=[[1, 1]], base=h * 1280 + 640 - d0,
                       channel_multiplier=-1).then_inc(ctx.sem["eG"], 1)
                e.wait_ge(ctx.sem["eG"], eG0 + 1 + j + 1)
                if j >= 2:
                    e.wait_ge(ctx.sem["qG"], qG0 + 16 + 16 * 2 * (j - 1))
                e.indirect_dma_start(
                    btile[j % 2][:, :], None,
                    grf,
                    IndirectOffsetOnAxis(ap=idxv[:, 0:1], axis=0),
                ).then_inc(ctx.sem["qG"], 16)
                e.wait_ge(ctx.sem["qG"], qG0 + 16 + 16 * (2 * j + 1))
                e.dma_start(out=biasTd[h, bi, :, :], in_=btile[j % 2][:]).then_inc(ctx.sem["qG"], 16)
                j += 1
    ctx.body("gpsimd", f_gp1)
    ctx.cnt["eG"] = eG0 + 1 + 96
    ctx.cnt["qG"] = qG0 + 16 + 16 * 2 * 96


def stage_rms(ctx, src, dst_bf16=None, out_row=None, eps=1e-6):
    xfm = ctx.dram[src]
    dstb = ctx.dram[dst_bf16] if dst_bf16 else None
    rowd = ctx.dram[out_row] if out_row else None

    ones1 = ctx.sb((128, 1), F32, "on1")
    onesr = ctx.sb((1, 128), F32, "onr")
    xt = [[ctx.sb((128, TC), F32, "x") for _ in range(NKD)] for _ in range(2)]
    sq = [ctx.sb((128, TC), F32, "sq") for _ in range(NKD)]
    rows = ctx.sb((1, 2 * TC), F32, "rw")
    rstdA = ctx.sb((1, T), F32, "rsA")
    xob = [ctx.sb((128, TC), BF16, "xob") for _ in range(3)]
    p_q = ctx.psum((1, TC), F32, "pq")
    p_r = ctx.psum((128, TC), F32, "pr")

    b = dict(ctx.cnt)
    eT0, eS0, eV0, eG0 = b["eT"], b["eS"], b["eV"], b["eG"]
    qS0, qG0 = b["qS"], b["qG"]
    ld1 = [qS0 + 16 * NKD * (i + 1) for i in range(NT)]
    ctx.cnt["qS"] = qS0 + 16 * NKD * NT
    NV1 = 3

    def f1_sync(e):
        e.wait_ge(ctx.sem["qS"], qS0)
        for i in range(NT):
            if i >= 1:
                e.wait_ge(ctx.sem["qS"], ld1[i - 1])
            if i >= 2:
                e.wait_ge(ctx.sem["eT"], eT0 + (i - 1))
            for kc in range(NKD):
                e.dma_start(out=xt[i % 2][kc][:],
                            in_=xfm[kc * 128:(kc + 1) * 128, i * TC:(i + 1) * TC]).then_inc(ctx.sem["qS"], 16)
    ctx.body("sync", f1_sync)

    def f1_gp(e):
        e.memset(ones1[:], 1.0).then_inc(ctx.sem["eG"], 1)
        e.memset(onesr[:], 1.0).then_inc(ctx.sem["eG"], 1)
    ctx.body("gpsimd", f1_gp)
    ctx.cnt["eG"] = eG0 + 2

    def f1_scalar(e):
        for i in range(NT):
            e.wait_ge(ctx.sem["qS"], ld1[i])
            if i >= 1:
                e.wait_ge(ctx.sem["eT"], eT0 + i)
            for kc in range(NKD):
                e.square(sq[kc][:], xt[i % 2][kc][:]).then_inc(ctx.sem["eS"], 1)
            e.wait_ge(ctx.sem["eV"], eV0 + NV1 * i + 2)
            e.activation(out=rows[:, TC:2 * TC], in_=rows[:, 0:TC],
                         func=AF.Sqrt).then_inc(ctx.sem["eS"], 1)
    ctx.body("scalar", f1_scalar)
    ctx.cnt["eS"] = eS0 + (NKD + 1) * NT

    def f1_tensor(e):
        e.wait_ge(ctx.sem["eG"], eG0 + 2)
        for i in range(NT):
            e.wait_ge(ctx.sem["eS"], eS0 + (NKD + 1) * i + NKD)
            if i >= 1:
                e.wait_ge(ctx.sem["eV"], eV0 + NV1 * (i - 1) + 1)
            ins = None
            for kc in range(NKD):
                ins = e.matmul(p_q[:], ones1[:], sq[kc][:], start=(kc == 0),
                               stop=(kc == NKD - 1), skip_group_check=True)
            ins.then_inc(ctx.sem["eT"], 1)
    ctx.body("tensor", f1_tensor)
    ctx.cnt["eT"] = eT0 + NT

    def f1_vec(e):
        for i in range(NT):
            bv = eV0 + NV1 * i
            e.wait_ge(ctx.sem["eT"], eT0 + i + 1)
            e.tensor_scalar_mul(rows[:, 0:TC], p_q[:], 1.0 / D).then_inc(ctx.sem["eV"], 1)
            e.wait_ge(ctx.sem["eV"], bv + 1)
            e.tensor_scalar_add(rows[:, 0:TC], rows[:, 0:TC], eps).then_inc(ctx.sem["eV"], 1)
            e.wait_ge(ctx.sem["eS"], eS0 + (NKD + 1) * (i + 1))
            e.reciprocal(rstdA[:, i * TC:(i + 1) * TC],
                         rows[:, TC:2 * TC]).then_inc(ctx.sem["eV"], 1)
    ctx.body("vector", f1_vec)
    ctx.cnt["eV"] = eV0 + NV1 * NT
    ph1 = ctx.cnt["eV"]

    if rowd is not None:
        def f_gpr(e):
            e.wait_ge(ctx.sem["eV"], ph1)
            e.dma_start(out=rowd[:, :], in_=rstdA[:]).then_inc(ctx.sem["qG"], 16)
        ctx.body("gpsimd", f_gpr)
        ctx.cnt["qG"] += 16

    if dstb is None:
        return

    b2 = dict(ctx.cnt)
    eT20, eV20 = b2["eT"], b2["eV"]
    st0 = b2["qG"]
    ld2 = [b2["qS"] + 16 * NKD * (i + 1) for i in range(NT)]
    ctx.cnt["qS"] = b2["qS"] + 16 * NKD * NT

    def f2_sync(e):
        e.wait_ge(ctx.sem["eV"], ph1)
        e.wait_ge(ctx.sem["qS"], b2["qS"])
        for i in range(NT):
            if i >= 1:
                e.wait_ge(ctx.sem["qS"], ld2[i - 1])
            if i >= 2:
                e.wait_ge(ctx.sem["eV"], eV20 + NKD * (i - 1))
            for kc in range(NKD):
                e.dma_start(out=xt[i % 2][kc][:],
                            in_=xfm[kc * 128:(kc + 1) * 128, i * TC:(i + 1) * TC]).then_inc(ctx.sem["qS"], 16)
    ctx.body("sync", f2_sync)

    def f2_tensor(e):
        e.wait_ge(ctx.sem["eV"], ph1)
        for i in range(NT):
            if i >= 1:
                e.wait_ge(ctx.sem["eV"], eV20 + NKD * i)
            e.matmul(p_r[:], onesr[:], rstdA[:, i * TC:(i + 1) * TC], start=True,
                     stop=True, skip_group_check=True).then_inc(ctx.sem["eT"], 1)
    ctx.body("tensor", f2_tensor)
    ctx.cnt["eT"] = eT20 + NT

    def f2_vec(e):
        for i in range(NT):
            e.wait_ge(ctx.sem["qS"], ld2[i])
            e.wait_ge(ctx.sem["eT"], eT20 + i + 1)
            for kc in range(NKD):
                kidx = i * NKD + kc
                if kidx >= 3:
                    e.wait_ge(ctx.sem["qG"], st0 + 16 * (kidx - 2))
                e.tensor_mul(xob[kidx % 3][:], xt[i % 2][kc][:], p_r[:]).then_inc(ctx.sem["eV"], 1)
    ctx.body("vector", f2_vec)
    ctx.cnt["eV"] = eV20 + NKD * NT

    def f2_gp(e):
        for i in range(NT):
            for kc in range(NKD):
                kidx = i * NKD + kc
                e.wait_ge(ctx.sem["eV"], eV20 + kidx + 1)
                if kidx >= 1:
                    e.wait_ge(ctx.sem["qG"], st0 + 16 * kidx)
                e.dma_start(out=dstb[kc * 128:(kc + 1) * 128, i * TC:(i + 1) * TC],
                            in_=xob[kidx % 3][:]).then_inc(ctx.sem["qG"], 16)
    ctx.body("gpsimd", f2_gp)
    ctx.cnt["qG"] = st0 + 16 * NKD * NT


def stage_wtr2(ctx):
    specs = [("F_wq", "wqB", "wqT", 1024), ("F_wk", "wkB", "wkT", 256),
             ("F_wv", "wvB", "wvT", 256), ("F_wo", "woB", "woT", 1024)]
    tmpf = ctx.sb((128, 1024), F32, "wf")
    tmpb = ctx.sb((128, 1024), BF16, "wb")
    ttile = [ctx.sb((128, 512), BF16, "tt") for _ in range(2)]
    b = dict(ctx.cnt)
    qS0, qG0, eV0 = b["qS"], b["qG"], b["eV"]

    n_ch = sum(s[3] // 128 for s in specs)
    ld = [qS0 + 16 * (j + 1) for j in range(n_ch)]
    st = [qG0 + 16 * (j + 1) for j in range(n_ch)]
    ctx.cnt["qS"] = qS0 + 16 * n_ch
    ctx.cnt["eV"] = eV0 + n_ch
    ctx.cnt["qG"] = qG0 + 16 * n_ch

    def f_sync(e):
        e.wait_ge(ctx.sem["qS"], qS0)
        j = 0
        for (fn, bn, tn, nr) in specs:
            for c in range(nr // 128):
                if j >= 1:
                    e.wait_ge(ctx.sem["qS"], ld[j - 1])
                if j >= 1:
                    e.wait_ge(ctx.sem["eV"], eV0 + j)
                e.dma_start(out=tmpf[:], in_=ctx.dram[fn][c * 128:(c + 1) * 128, :]).then_inc(ctx.sem["qS"], 16)
                j += 1
    ctx.body("sync", f_sync)

    def f_vec(e):
        j = 0
        for (fn, bn, tn, nr) in specs:
            for c in range(nr // 128):
                e.wait_ge(ctx.sem["qS"], ld[j])
                if j >= 1:
                    e.wait_ge(ctx.sem["qG"], st[j - 1])
                e.tensor_copy(tmpb[:], tmpf[:]).then_inc(ctx.sem["eV"], 1)
                j += 1
    ctx.body("vector", f_vec)

    def f_gp(e):
        j = 0
        for (fn, bn, tn, nr) in specs:
            for c in range(nr // 128):
                e.wait_ge(ctx.sem["eV"], eV0 + j + 1)
                if j >= 1:
                    e.wait_ge(ctx.sem["qG"], st[j - 1])
                e.dma_start(out=ctx.dram[bn][c * 128:(c + 1) * 128, :], in_=tmpb[:]).then_inc(ctx.sem["qG"], 16)
                j += 1
    ctx.body("gpsimd", f_gp)

    b2 = dict(ctx.cnt)
    qS20, qG20 = b2["qS"], b2["qG"]
    plan = []
    for (fn, bn, tn, nr) in specs:
        n_i = 1024 // 128
        n_o = nr // 512 if nr >= 512 else 0
        if nr >= 512:
            for i0 in range(n_i):
                for o0 in range(nr // 512):
                    plan.append((bn, tn, i0, o0, 512))
        else:
            for i0 in range(n_i):
                plan.append((bn, tn, i0, 0, nr))
    nP = len(plan)
    ld2 = [qS20 + 16 * (j + 1) for j in range(nP)]
    st2 = [qS20 + 16 * nP + 16 * (j + 1) for j in range(nP)]
    ctx.cnt["qS"] = qS20 + 16 * nP

    def f_sync2(e):
        e.wait_ge(ctx.sem["qG"], qG0 + 16 * n_ch)
        e.wait_ge(ctx.sem["qS"], qS20)
        for j, (bn, tn, i0, o0, ow) in enumerate(plan):
            if j >= 1:
                e.wait_ge(ctx.sem["qS"], ld2[j - 1])
            if j >= 2:
                e.wait_ge(ctx.sem["qG"], qG20 + 16 * (j - 1))
            e.dma_start(out=ttile[j % 2][:, 0:ow],
                        in_=ctx.dram[bn][o0 * 512:o0 * 512 + ow, i0 * 128:(i0 + 1) * 128],
                        transpose=True).then_inc(ctx.sem["qS"], 16)
    ctx.body("sync", f_sync2)

    def f_gp2(e):
        for j, (bn, tn, i0, o0, ow) in enumerate(plan):
            e.wait_ge(ctx.sem["qS"], ld2[j])
            if j >= 1:
                e.wait_ge(ctx.sem["qG"], qG20 + 16 * j)
            e.dma_start(out=ctx.dram[tn][i0 * 128:(i0 + 1) * 128, o0 * 512:o0 * 512 + ow],
                        in_=ttile[j % 2][:, 0:ow]).then_inc(ctx.sem["qG"], 16)
    ctx.body("gpsimd", f_gp2)
    ctx.cnt["qG"] = qG20 + 16 * nP


def stage_proj_fm(ctx, w_name, M, out_name, rhs_name):
    MC = M // 128
    wsb, wrdy = load_weights_resident(ctx, ctx.dram[w_name], D, M, "wp")
    od = ctx.dram[out_name]

    def stores(i, mc):
        return [od[mc * 128:(mc + 1) * 128, i * TC:(i + 1) * TC]]

    def epi(info):
        epi_act_store(ctx, info, stores, out_dtype=BF16)
    gemm_fm(ctx, wsb=wsb, w_ready=wrdy, NKC=NKD, MC=MC,
            rhs_dram=ctx.dram[rhs_name], epi_emit=epi)


def stage_rope2(ctx):
    qin, kin = ctx.dram["q2r"], ctx.dram["k2r"]
    qout, kout = ctx.dram["q2b"], ctx.dram["k2b"]
    NKK = 2

    tbl = {}
    ttmp = ctx.sb((128, T), F32, "ttmp")
    for nm in ("cq", "sq", "ck", "sk"):
        tbl[nm] = ctx.sb((128, T), BF16, "tb" + nm)
    iptP = ctx.sb((16, NKD * 128), BF16, "iptP")
    ipq = ctx.sb((128, NKD * H), BF16, "ipq")
    iptK = ctx.sb((4, NKK * 128), BF16, "iptK")
    ipk = ctx.sb((128, NKK * 4), BF16, "ipk")
    swp = ctx.sb((128, 128), BF16, "swp")
    gq = ctx.sb((H, 1), F32, "gq")
    gk = ctx.sb((4, 1), F32, "gk")
    qt = [ctx.sb((128, TC), BF16, "qt") for _ in range(NKD)]
    kt = [ctx.sb((128, TC), BF16, "kt") for _ in range(NKK)]
    sqb = [ctx.sb((128, TC), BF16, "sqb") for _ in range(NKD)]
    rows = ctx.sb((H, 2 * TC), F32, "rws")
    rowk = ctx.sb((4, 2 * TC), F32, "rwk")
    rqb = ctx.sb((H, TC), BF16, "rqb")
    rkb = ctx.sb((4, TC), BF16, "rkb")
    t1b = [ctx.sb((128, TC), BF16, "t1") for _ in range(2)]
    u1b = [ctx.sb((128, TC), BF16, "u1") for _ in range(2)]
    u2b = [ctx.sb((128, TC), BF16, "u2") for _ in range(2)]
    ob = [ctx.sb((128, TC), BF16, "rob") for _ in range(3)]
    p_qn = ctx.psum((H, TC), F32, "pqn")
    p_kn = ctx.psum((4, TC), F32, "pkn")
    p_rb = [ctx.psum((128, TC), F32, "prb") for _ in range(2)]
    p_sw = [ctx.psum((128, TC), F32, "psw") for _ in range(2)]

    b = dict(ctx.cnt)
    eT0, eS0, eV0, eG0 = b["eT"], b["eS"], b["eV"], b["eG"]
    qS0, qG0, qA0 = b["qS"], b["qG"], b["qA"]

    n_pre = 2 + NKD + NKD + NKK + NKK + 1

    def f_sc_pro(e):
        e.wait_ge(ctx.sem["qA"], qA0)
        e.dma_start(out=gq[:, 0], in_=ctx.dram["F_qg"][:]).then_inc(ctx.sem["qA"], 16)
        e.dma_start(out=gk[:, 0], in_=ctx.dram["F_kg"][:]).then_inc(ctx.sem["qA"], 16)
        for c in range(NKD):
            e.dma_start(out=iptP[:, c * 128:(c + 1) * 128], in_=ctx.dram["indPT"][c, :, :]).then_inc(ctx.sem["qA"], 16)
        for c in range(NKD):
            e.dma_start(out=ipq[:, c * H:(c + 1) * H], in_=ctx.dram["indP"][c, :, :]).then_inc(ctx.sem["qA"], 16)
        for c in range(NKK):
            e.dma_start(out=iptK[:, c * 128:(c + 1) * 128], in_=ctx.dram["indKT"][c, :, :]).then_inc(ctx.sem["qA"], 16)
        for c in range(NKK):
            e.dma_start(out=ipk[:, c * 4:(c + 1) * 4], in_=ctx.dram["indK"][c, :, :]).then_inc(ctx.sem["qA"], 16)
        e.dma_start(out=swp[:], in_=ctx.dram["swap64"][:, :]).then_inc(ctx.sem["qA"], 16)
        names = {"cq": "rope1_cq", "sq": "rope1_sq", "ck": "rope1_ck", "sk": "rope1_sk"}
        for j, nm in enumerate(("cq", "sq", "ck", "sk")):
            if j >= 1:
                e.wait_ge(ctx.sem["eV"], eV0 + j)
                e.wait_ge(ctx.sem["qA"], qA0 + 16 * (n_pre + j))
            e.dma_start(out=ttmp[:], in_=ctx.dram[names[nm]][:, :]).then_inc(ctx.sem["qA"], 16)
    ctx.body("scalar", f_sc_pro)
    ctx.cnt["qA"] = qA0 + 16 * (n_pre + 4)
    pro_tgt = ctx.cnt["qA"]

    def f_v_pro(e):
        for j, nm in enumerate(("cq", "sq", "ck", "sk")):
            e.wait_ge(ctx.sem["qA"], qA0 + 16 * (n_pre + j + 1))
            e.tensor_copy(tbl[nm][:], ttmp[:]).then_inc(ctx.sem["eV"], 1)
    ctx.body("vector", f_v_pro)
    ctx.cnt["eV"] = eV0 + 4
    eVp = ctx.cnt["eV"]

    ld1 = []
    cur = ctx.cnt["qS"]
    for ti in range(NT):
        cur += 16 * (NKD + NKK)
        ld1.append(cur)
    ctx.cnt["qS"] = cur

    NVsq = NKD + NKK
    NVrow = 10
    NVrow = 10
    NVtile = 4 * (NKD + NKK)
    NV = NVsq + NVrow + NVtile
    NSt = 2
    NTt = 2 + 2 * (NKD + NKK)
    eVb = lambda ti: eVp + NV * ti
    eTb = lambda ti: eT0 + NTt * ti
    eSb = lambda ti: eS0 + NSt * ti
    st0 = qG0
    ctx.cnt["eV"] = eVp + NV * NT
    ctx.cnt["eS"] = eS0 + NSt * NT
    ctx.cnt["eT"] = eT0 + NTt * NT
    ctx.cnt["qG"] = st0 + 16 * (NKD + NKK) * NT

    def f_sync(e):
        e.wait_ge(ctx.sem["qS"], qS0)
        for ti in range(NT):
            if ti >= 1:
                e.wait_ge(ctx.sem["qS"], ld1[ti - 1])
                e.wait_ge(ctx.sem["eV"], eVb(ti))
            for c in range(NKD):
                e.dma_start(out=qt[c][:], in_=qin[c * 128:(c + 1) * 128, ti * TC:(ti + 1) * TC]).then_inc(ctx.sem["qS"], 16)
            for c in range(NKK):
                e.dma_start(out=kt[c][:], in_=kin[c * 128:(c + 1) * 128, ti * TC:(ti + 1) * TC]).then_inc(ctx.sem["qS"], 16)
    ctx.body("sync", f_sync)

    def f_vec(e):
        w = WT(ctx, e)
        w("qA", pro_tgt)
        for ti in range(NT):
            bv = eVb(ti)
            w("qS", ld1[ti])
            if ti >= 1:
                w("eT", eTb(ti))
            for c in range(NKD):
                e.tensor_mul(sqb[c][:], qt[c][:], qt[c][:]).then_inc(ctx.sem["eV"], 1)
            w("eT", eTb(ti) + 1)
            for c in range(NKK):
                e.tensor_mul(sqb[c][:], kt[c][:], kt[c][:]).then_inc(ctx.sem["eV"], 1)
            w("eT", eTb(ti) + 1)
            e.tensor_scalar_mul(rows[:, 0:TC], p_qn[:], 1.0 / 64).then_inc(ctx.sem["eV"], 1)
            w("eV", bv + NVsq + 1)
            e.tensor_scalar_add(rows[:, 0:TC], rows[:, 0:TC], 1e-6).then_inc(ctx.sem["eV"], 1)
            w("eS", eSb(ti) + 1)
            e.reciprocal(rows[:, TC:2 * TC], rows[:, TC:2 * TC]).then_inc(ctx.sem["eV"], 1)
            w("eV", bv + NVsq + 3)
            e.tensor_scalar_mul(rows[:, TC:2 * TC], rows[:, TC:2 * TC],
                                gq[:, 0:1]).then_inc(ctx.sem["eV"], 1)
            w("eV", bv + NVsq + 4)
            e.tensor_copy(rqb[:], rows[:, TC:2 * TC]).then_inc(ctx.sem["eV"], 1)
            w("eT", eTb(ti) + 2)
            e.tensor_scalar_mul(rowk[:, 0:TC], p_kn[:], 1.0 / 64).then_inc(ctx.sem["eV"], 1)
            w("eV", bv + NVsq + 6)
            e.tensor_scalar_add(rowk[:, 0:TC], rowk[:, 0:TC], 1e-6).then_inc(ctx.sem["eV"], 1)
            w("eS", eSb(ti) + 2)
            e.reciprocal(rowk[:, TC:2 * TC], rowk[:, TC:2 * TC]).then_inc(ctx.sem["eV"], 1)
            w("eV", bv + NVsq + 8)
            e.tensor_scalar_mul(rowk[:, TC:2 * TC], rowk[:, TC:2 * TC],
                                gk[:, 0:1]).then_inc(ctx.sem["eV"], 1)
            w("eV", bv + NVsq + 9)
            e.tensor_copy(rkb[:], rowk[:, TC:2 * TC]).then_inc(ctx.sem["eV"], 1)
            for j in range(NKD + NKK):
                is_q = j < NKD
                c = j if is_q else j - NKD
                raw = qt[c] if is_q else kt[c]
                cs = tbl["cq"] if is_q else tbl["ck"]
                ss = tbl["sq"] if is_q else tbl["sk"]
                bt = bv + NVsq + NVrow + 4 * j
                tt = eTb(ti) + 2 + 2 * j
                w("eT", tt + 1)
                e.tensor_mul(t1b[j % 2][:], raw[:], p_rb[j % 2][:]).then_inc(ctx.sem["eV"], 1)
                w("eV", bt + 1)
                e.tensor_mul(u1b[j % 2][:], t1b[j % 2][:],
                             cs[:, ti * TC:(ti + 1) * TC]).then_inc(ctx.sem["eV"], 1)
                w("eT", tt + 2)
                e.tensor_mul(u2b[j % 2][:], p_sw[j % 2][:],
                             ss[:, ti * TC:(ti + 1) * TC]).then_inc(ctx.sem["eV"], 1)
                w("eV", bt + 3)
                jj = (NKD + NKK) * ti + j
                if jj >= 3:
                    w("qG", st0 + 16 * (jj - 2))
                e.tensor_add(ob[jj % 3][:], u1b[j % 2][:], u2b[j % 2][:]).then_inc(ctx.sem["eV"], 1)
    ctx.body("vector", f_vec)

    def f_tensor(e):
        w = WT(ctx, e)
        w("qA", pro_tgt)
        for ti in range(NT):
            bv = eVb(ti)
            w("qS", ld1[ti])
            w("eV", bv + NKD)
            if ti >= 1:
                w("eV", eVb(ti - 1) + NVsq + 6)
            ins = None
            for c in range(NKD):
                ins = e.matmul(p_qn[:], ipq[:, c * H:(c + 1) * H], sqb[c][:], start=(c == 0),
                               stop=(c == NKD - 1), skip_group_check=True)
            ins.then_inc(ctx.sem["eT"], 1)
            w("eV", bv + NKD + NKK)
            for c in range(NKK):
                ins = e.matmul(p_kn[:], ipk[:, c * 4:(c + 1) * 4], sqb[c][:], start=(c == 0),
                               stop=(c == NKK - 1), skip_group_check=True)
            ins.then_inc(ctx.sem["eT"], 1)
            for j in range(NKD + NKK):
                is_q = j < NKD
                c = j if is_q else j - NKD
                ipt_t = iptP[:, c * 128:(c + 1) * 128] if is_q else iptK[:, c * 128:(c + 1) * 128]
                rrow_b = rqb if is_q else rkb
                bt = bv + NVsq + NVrow + 4 * j
                w("eV", bv + NVsq + (5 if is_q else NVrow))
                if j >= 2:
                    w("eV", bv + NVsq + NVrow + 4 * (j - 2) + 2)
                e.matmul(p_rb[j % 2][:], ipt_t, rrow_b[:],
                         start=True, stop=True, skip_group_check=True).then_inc(ctx.sem["eT"], 1)
                w("eV", bt + 1)
                if j >= 2:
                    w("eV", bv + NVsq + NVrow + 4 * (j - 2) + 4)
                e.matmul(p_sw[j % 2][:], swp[:], t1b[j % 2][:],
                         start=True, stop=True, skip_group_check=True).then_inc(ctx.sem["eT"], 1)
    ctx.body("tensor", f_tensor)

    def f_scalar(e):
        w = WT(ctx, e)
        for ti in range(NT):
            bv = eVb(ti)
            w("eT", eTb(ti) + 1)
            w("eV", bv + NVsq + 2)
            e.activation(out=rows[:, TC:2 * TC], in_=rows[:, 0:TC],
                         func=AF.Sqrt).then_inc(ctx.sem["eS"], 1)
            w("eT", eTb(ti) + 2)
            w("eV", bv + NVsq + 7)
            e.activation(out=rowk[:, TC:2 * TC], in_=rowk[:, 0:TC],
                         func=AF.Sqrt).then_inc(ctx.sem["eS"], 1)
    ctx.body("scalar", f_scalar)

    def f_gp(e):
        for ti in range(NT):
            for j in range(NKD + NKK):
                is_q = j < NKD
                c = j if is_q else j - NKD
                dst = qout if is_q else kout
                e.wait_ge(ctx.sem["eV"], eVb(ti) + NVsq + NVrow + 4 * (j + 1))
                jj = (NKD + NKK) * ti + j
                if jj >= 1:
                    e.wait_ge(ctx.sem["qG"], st0 + 16 * jj)
                e.dma_start(out=dst[c * 128:(c + 1) * 128, ti * TC:(ti + 1) * TC],
                            in_=ob[jj % 3][:]).then_inc(ctx.sem["qG"], 16)
    ctx.body("gpsimd", f_gp)


def stage_gate(ctx):
    lrow = ctx.sb((1, 8), F32, "lg")
    iot = ctx.sb((1, 8), F32, "io")
    tmp = ctx.sb((1, 32), F32, "tm")
    outt = ctx.sb((1, 4), F32, "go")
    b = dict(ctx.cnt)
    qA0, eV0, eS0, eG0, qG0 = b["qA"], b["eV"], b["eS"], b["eG"], b["qG"]
    n_v = 14
    n_s = 1
    for bi, lname in enumerate(("F_flog", "F_plog")):
        qA_t = qA0 + 16 * (bi + 1)
        bv = eV0 + n_v * bi
        bs = eS0 + n_s * bi
        bg = eG0 + 1 + 2 * bi

        def f_sc(e, lname=lname, qA_t=qA_t, bs=bs, bv=bv):
            e.wait_ge(ctx.sem["qA"], qA_t - 16)
            e.dma_start(out=lrow[:], in_=ctx.dram[lname][:]).then_inc(ctx.sem["qA"], 16)
            e.wait_ge(ctx.sem["eV"], bv + 10)
            e.activation(out=tmp[:, 10:11], in_=tmp[:, 9:10], func=AF.Exp).then_inc(ctx.sem["eS"], 1)
        ctx.body("scalar", f_sc)

        def f_gp0(e, bg=bg, bv=bv, qG_t=qG0 + 32 * bi):
            if bi == 0:
                e.iota(iot[:], pattern=[[1, 8]], base=0, channel_multiplier=0,
                       allow_small_or_imprecise_dtypes=True).then_inc(ctx.sem["eG"], 1)
            e.wait_ge(ctx.sem["eV"], bv + n_v)
            if bi == 1:
                e.wait_ge(ctx.sem["qG"], qG0 + 32)
            e.dma_start(out=ctx.dram["gatev"][bi, :], in_=outt[:, 0:2]).then_inc(ctx.sem["qG"], 16)
            e.dma_start(out=ctx.dram["gidx"][bi, :], in_=outt[:, 2:4]).then_inc(ctx.sem["qG"], 16)
        ctx.body("gpsimd", f_gp0)

        def f_v(e, bv=bv, bs=bs, qA_t=qA_t, bi=bi):
            w = WT(ctx, e)
            w("qA", qA_t)
            w("eG", eG0 + 1)
            if bi == 1:
                w("qG", qG0 + 32)
            e.reduce_max(tmp[:, 0:1], lrow[:], axis=AX.X).then_inc(ctx.sem["eV"], 1)
            w("eV", bv + 1)
            e.tensor_scalar(tmp[:, 1:9], lrow[:], tmp[:, 0:1], None,
                            ALU.is_ge).then_inc(ctx.sem["eV"], 1)
            w("eV", bv + 2)
            e.tensor_mul(tmp[:, 11:19], tmp[:, 1:9], iot[:]).then_inc(ctx.sem["eV"], 1)
            w("eV", bv + 3)
            e.reduce_sum(outt[:, 2:3], tmp[:, 11:19], axis=AX.X).then_inc(ctx.sem["eV"], 1)
            w("eV", bv + 4)
            e.scalar_tensor_tensor(tmp[:, 19:27], tmp[:, 1:9], -1e9, lrow[:],
                                   ALU.mult, ALU.add).then_inc(ctx.sem["eV"], 1)
            w("eV", bv + 5)
            e.reduce_max(tmp[:, 27:28], tmp[:, 19:27], axis=AX.X).then_inc(ctx.sem["eV"], 1)
            w("eV", bv + 6)
            e.tensor_scalar(tmp[:, 1:9], tmp[:, 19:27], tmp[:, 27:28], None,
                            ALU.is_ge).then_inc(ctx.sem["eV"], 1)
            w("eV", bv + 7)
            e.tensor_mul(tmp[:, 11:19], tmp[:, 1:9], iot[:]).then_inc(ctx.sem["eV"], 1)
            w("eV", bv + 8)
            e.reduce_sum(outt[:, 3:4], tmp[:, 11:19], axis=AX.X).then_inc(ctx.sem["eV"], 1)
            w("eV", bv + 9)
            e.tensor_sub(tmp[:, 9:10], tmp[:, 27:28], tmp[:, 0:1]).then_inc(ctx.sem["eV"], 1)
            w("eS", bs + 1)
            e.tensor_scalar_add(tmp[:, 11:12], tmp[:, 10:11], 1.0).then_inc(ctx.sem["eV"], 1)
            w("eV", bv + 11)
            e.reciprocal(outt[:, 0:1], tmp[:, 11:12]).then_inc(ctx.sem["eV"], 1)
            w("eV", bv + 12)
            e.tensor_mul(outt[:, 1:2], tmp[:, 10:11], outt[:, 0:1]).then_inc(ctx.sem["eV"], 1)
            e.nop().then_inc(ctx.sem["eV"], 1)
        ctx.body("vector", f_v)
    ctx.cnt["qA"] = qA0 + 32
    ctx.cnt["eV"] = eV0 + 2 * n_v
    ctx.cnt["eS"] = eS0 + 2 * n_s
    ctx.cnt["eG"] = eG0 + 1 + 0
    ctx.cnt["qG"] = qG0 + 64
    ctx.cnt["eG"] = eG0 + 1


def _bcast_col(ctx, e_t, e_s, src_row_ap, dst_col, psum, onesr, wait_ev, eT_slot, eS_slot):
    raise NotImplementedError


def stage_mixprep(ctx):
    onesr = ctx.sb((1, 128), F32, "onr")
    gval = ctx.sb((1, 4), F32, "gv")
    scol = ctx.sb((128, 4), F32, "sc")
    iotc = ctx.sb((128, 1), F32, "ioc")
    idxi = ctx.sb((128, 4), I32, "idi")
    idxf = ctx.sb((128, 1), F32, "idf")
    gtile = ctx.sb((128, 4096), F32, "gt")
    btile = ctx.sb((128, 4096), BF16, "btl")
    ttile = [ctx.sb((128, 512), BF16, "tt") for _ in range(2)]
    p_b = ctx.psum((128, 4), F32, "pbc")
    b = dict(ctx.cnt)
    qA0, qS0, qG0, eT0, eS0, eV0, eG0 = b["qA"], b["qS"], b["qG"], b["eT"], b["eS"], b["eV"], b["eG"]

    def f_sc0(e):
        e.wait_ge(ctx.sem["qA"], qA0)
        e.dma_start(out=gval[:], in_=ctx.dram["gidx"][:, :].rearrange("a b -> () (a b)")).then_inc(ctx.sem["qA"], 16)
    ctx.body("scalar", f_sc0)
    ctx.cnt["qA"] = qA0 + 16

    def f_gp0(e):
        e.memset(onesr[:], 1.0).then_inc(ctx.sem["eG"], 1)
        e.iota(iotc[:], pattern=[[1, 1]], base=0, channel_multiplier=1,
               allow_small_or_imprecise_dtypes=True).then_inc(ctx.sem["eG"], 1)
    ctx.body("gpsimd", f_gp0)
    ctx.cnt["eG"] = eG0 + 2

    def f_t0(e):
        e.wait_ge(ctx.sem["qA"], qA0 + 16)
        e.wait_ge(ctx.sem["eG"], eG0 + 2)
        e.matmul(p_b[:], onesr[:], gval[:], start=True, stop=True,
                 skip_group_check=True).then_inc(ctx.sem["eT"], 1)
    ctx.body("tensor", f_t0)
    ctx.cnt["eT"] = eT0 + 1

    def f_s0(e):
        e.wait_ge(ctx.sem["eT"], eT0 + 1)
        e.copy(scol[:], p_b[:]).then_inc(ctx.sem["eS"], 1)
    ctx.body("scalar", f_s0)
    ctx.cnt["eS"] = eS0 + 1

    specs = []
    for bi, (mixname, Bname, Tname, NR, RL) in enumerate(
            (("F_fmix", "fmixB", "fmixT", 4096, 1024),
             ("F_pmix", "pmixB", "pmixT", 1024, 4096))):
        for k in range(2):
            specs.append((bi, k, mixname, Bname, Tname, NR, RL))

    eVc = eV0
    qGc = qG0
    qSc = ctx.cnt["qS"]
    for (bi, k, mixname, Bname, Tname, NR, RL) in specs:
        slot = 2 * bi + k
        nch = NR // 128
        eva = eVc

        def f_v1(e, slot=slot, eva=eva, NR=NR):
            e.wait_ge(ctx.sem["eS"], eS0 + 1)
            if eva > eV0:
                e.wait_ge(ctx.sem["eV"], eva)
            e.scalar_tensor_tensor(idxf[:], scol[:, slot:slot + 1], float(NR),
                                   iotc[:], ALU.mult, ALU.add).then_inc(ctx.sem["eV"], 1)
            e.wait_ge(ctx.sem["eV"], eva + 1)
            e.tensor_copy(idxi[:, slot:slot + 1], idxf[:]).then_inc(ctx.sem["eV"], 1)
        ctx.body("vector", f_v1)
        eVc += 2
        ctx.cnt["eV"] = eVc

        g0 = qGc
        mixd = ctx.dram[mixname].rearrange("b r c -> (b r) c")

        def f_gp1(e, slot=slot, nch=nch, g0=g0, RL=RL, Bname=Bname, eva=eVc - 2, evb=eVc):
            w = WT(ctx, e)
            w("eV", eva + 2)
            for c in range(nch):
                if c >= 1:
                    w("qG", g0 + 16 * (2 * c - 1))
                    w("eV", evb + c)
                e.indirect_dma_start(
                    gtile[:, 0:RL], None, mixd,
                    IndirectOffsetOnAxis(ap=idxi[:, slot:slot + 1], axis=0),
                    element_offset=c * 128 * RL,
                ).then_inc(ctx.sem["qG"], 16)
                w("qG", g0 + 16 * (2 * c + 1))
                w("eV", evb + c + 1)
                e.dma_start(out=ctx.dram[Bname][c * 128:(c + 1) * 128, :],
                            in_=btile[:, 0:RL]).then_inc(ctx.sem["qG"], 16)
        ctx.body("gpsimd", f_gp1)

        def f_v2(e, nch=nch, g0=g0, RL=RL, evb=eVc):
            w = WT(ctx, e)
            for c in range(nch):
                w("qG", g0 + 16 * (2 * c + 1))
                if c >= 1:
                    w("qG", g0 + 16 * 2 * c)
                e.tensor_copy(btile[:, 0:RL], gtile[:, 0:RL]).then_inc(ctx.sem["eV"], 1)
        ctx.body("vector", f_v2)
        qGc = g0 + 16 * 2 * nch
        eVc += nch
        ctx.cnt["eV"] = eVc
        ctx.cnt["qG"] = qGc

        n_i = RL // 128
        n_o = NR // 512
        nP = n_i * n_o
        ld2 = [qSc + 16 * (j + 1) for j in range(nP)]
        s0 = qGc

        def f_sy(e, ld2=ld2, s0=s0, n_i=n_i, n_o=n_o, Bname=Bname, qSc=qSc, g_all=qGc):
            e.wait_ge(ctx.sem["qS"], qSc)
            e.wait_ge(ctx.sem["qG"], g_all)
            j = 0
            for i0 in range(n_i):
                for o0 in range(n_o):
                    if j >= 1:
                        e.wait_ge(ctx.sem["qS"], ld2[j - 1])
                    if j >= 2:
                        e.wait_ge(ctx.sem["qG"], s0 + 16 * (j - 1))
                    e.dma_start(out=ttile[j % 2][:],
                                in_=ctx.dram[Bname][o0 * 512:(o0 + 1) * 512, i0 * 128:(i0 + 1) * 128],
                                transpose=True).then_inc(ctx.sem["qS"], 16)
                    j += 1
        ctx.body("sync", f_sy)

        def f_gp2(e, ld2=ld2, s0=s0, n_i=n_i, n_o=n_o, Tname=Tname, k=k):
            j = 0
            for i0 in range(n_i):
                for o0 in range(n_o):
                    e.wait_ge(ctx.sem["qS"], ld2[j])
                    if j >= 1:
                        e.wait_ge(ctx.sem["qG"], s0 + 16 * j)
                    e.dma_start(out=ctx.dram[Tname + str(k)][i0 * 128:(i0 + 1) * 128, o0 * 512:(o0 + 1) * 512],
                                in_=ttile[j % 2][:]).then_inc(ctx.sem["qG"], 16)
                    j += 1
        ctx.body("gpsimd", f_gp2)
        qSc += 16 * nP
        qGc = s0 + 16 * nP
        ctx.cnt["qS"] = qSc
        ctx.cnt["qG"] = qGc


def stage_member(ctx, branch, k):
    fc = branch == "fc"
    SIN = "F_fs1" if fc else "F_ps1"
    SOUT = "F_fs2" if fc else "F_ps2"
    MIXT = "fmixT" if fc else "pmixT"
    AN, BN = ("iotaA_fc", "iotaB_fc") if fc else ("iotaA_pj", "iotaB_pj")
    MIX_K = 1024 if fc else 4096
    MIX_M = 4096 if fc else 1024
    n_bi = 16 if fc else 64
    n_bo = 64 if fc else 16
    IN = "hb" if fc else "h2b"
    NKIN = MIX_K // 128
    NMO = MIX_M // 128
    slot = (0 if fc else 2) + k
    ind = ctx.dram[IN]

    onesr = ctx.sb((1, 128), F32, "onr")
    gv2 = ctx.sb((1, 8), F32, "gv2")
    ccol = ctx.sb((128, 8), F32, "cc")
    iotc = ctx.sb((128, 1), F32, "ioc")
    abt = ctx.sb((128, 2 * NKIN), F32, "abt")
    rxf = ctx.sb((128, NKIN), F32, "rxf")
    rxi = ctx.sb((128, NKIN + 2), I32, "rxi")
    sgf = ctx.sb((128, 64), F32, "sgf")
    sgb = ctx.sb((128, 64), BF16, "sgb")
    s1p = [ctx.sb((128, 128), BF16, "s1p") for _ in range(n_bi // 2)]
    s2p = [ctx.sb((128, 128), BF16, "s2p") for _ in range(n_bo // 2)]
    weff = ctx.sb((128, NKIN * MIX_M), BF16, "wef")
    int_ = [ctx.sb((128, TC), BF16, "int") for _ in range(NKIN)]
    h1 = [ctx.sb((128, TC), BF16, "h1m") for _ in range(NKIN)]
    mt = ctx.sb((128, TC), BF16, "mt")
    rt = [ctx.sb((128, TC), F32, "rt") for _ in range(2)]
    ob = [ctx.sb((128, TC), F32, "mo") for _ in range(3)]
    obb = [ctx.sb((128, TC), BF16, "mob") for _ in range(3)]
    p_1 = ctx.psum((128, TC), F32, "p1")
    p_m = [ctx.psum((128, TC), F32, "pm") for _ in range(2)]
    p_2 = [ctx.psum((128, TC), F32, "p2") for _ in range(2)]
    p_c = ctx.psum((128, 8), F32, "pc")

    b = dict(ctx.cnt)
    qA0, qS0, qG0 = b["qA"], b["qS"], b["qG"]
    eT0, eS0, eV0, eG0 = b["eT"], b["eS"], b["eV"], b["eG"]

    def f_sc0(e):
        e.wait_ge(ctx.sem["qA"], qA0)
        e.dma_start(out=gv2[:, 0:4], in_=ctx.dram["gatev"][:, :].rearrange("a b -> () (a b)")).then_inc(ctx.sem["qA"], 16)
        e.dma_start(out=gv2[:, 4:8], in_=ctx.dram["gidx"][:, :].rearrange("a b -> () (a b)")).then_inc(ctx.sem["qA"], 16)
        for c in range(NKIN):
            e.dma_start(out=abt[:, c], in_=ctx.dram[AN][c * 128:(c + 1) * 128]).then_inc(ctx.sem["qA"], 16)
        for c in range(NKIN):
            e.dma_start(out=abt[:, NKIN + c], in_=ctx.dram[BN][c * 128:(c + 1) * 128]).then_inc(ctx.sem["qA"], 16)
    ctx.body("scalar", f_sc0)
    ctx.cnt["qA"] = qA0 + 16 * (2 + 2 * NKIN)
    pro_qA = ctx.cnt["qA"]

    def f_gp0(e):
        e.memset(onesr[:], 1.0).then_inc(ctx.sem["eG"], 1)
        e.iota(iotc[:], pattern=[[1, 1]], base=0, channel_multiplier=1,
               allow_small_or_imprecise_dtypes=True).then_inc(ctx.sem["eG"], 1)
        for j in range(n_bi // 2):
            e.memset(s1p[j][:], 0.0).then_inc(ctx.sem["eG"], 1)
        for j in range(n_bo // 2):
            e.memset(s2p[j][:], 0.0).then_inc(ctx.sem["eG"], 1)
    ctx.body("gpsimd", f_gp0)
    n_ms = 2 + n_bi // 2 + n_bo // 2
    ctx.cnt["eG"] = eG0 + n_ms

    def f_t0(e):
        e.wait_ge(ctx.sem["qA"], pro_qA)
        e.wait_ge(ctx.sem["eG"], eG0 + 2)
        e.matmul(p_c[:], onesr[:], gv2[:], start=True, stop=True,
                 skip_group_check=True).then_inc(ctx.sem["eT"], 1)
    ctx.body("tensor", f_t0)
    ctx.cnt["eT"] = eT0 + 1

    def f_s1(e):
        e.wait_ge(ctx.sem["eT"], eT0 + 1)
        e.copy(ccol[:], p_c[:]).then_inc(ctx.sem["eS"], 1)
    ctx.body("scalar", f_s1)
    ctx.cnt["eS"] = eS0 + 1

    def f_v0(e):
        w = WT(ctx, e)
        w("eS", eS0 + 1)
        e.tensor_scalar_add(rxf[:], abt[:, NKIN:2 * NKIN],
                            ccol[:, 4 + slot:5 + slot]).then_inc(ctx.sem["eV"], 1)
        w("eV", eV0 + 1)
        e.tensor_scalar(abt[:, NKIN:2 * NKIN], rxf[:], float(n_bi), None,
                        ALU.is_ge).then_inc(ctx.sem["eV"], 1)
        w("eV", eV0 + 2)
        e.scalar_tensor_tensor(rxf[:], abt[:, NKIN:2 * NKIN], float(-n_bi), rxf[:],
                               ALU.mult, ALU.add).then_inc(ctx.sem["eV"], 1)
        w("eV", eV0 + 3)
        e.tensor_add(rxf[:], rxf[:], abt[:, 0:NKIN]).then_inc(ctx.sem["eV"], 1)
        w("eV", eV0 + 4)
        e.tensor_copy(rxi[:, 0:NKIN], rxf[:]).then_inc(ctx.sem["eV"], 1)
        w("eV", eV0 + 5)
        e.scalar_tensor_tensor(rxf[:, 0:1], ccol[:, 4 + slot:5 + slot],
                               float(n_bi * 64), iotc[:], ALU.mult,
                               ALU.add).then_inc(ctx.sem["eV"], 1)
        w("eV", eV0 + 6)
        e.tensor_copy(rxi[:, NKIN:NKIN + 1], rxf[:, 0:1]).then_inc(ctx.sem["eV"], 1)
        w("eV", eV0 + 7)
        e.scalar_tensor_tensor(rxf[:, 0:1], ccol[:, 4 + slot:5 + slot],
                               float(n_bo * 64), iotc[:], ALU.mult,
                               ALU.add).then_inc(ctx.sem["eV"], 1)
        w("eV", eV0 + 8)
        e.tensor_copy(rxi[:, NKIN + 1:NKIN + 2], rxf[:, 0:1]).then_inc(ctx.sem["eV"], 1)
    ctx.body("vector", f_v0)
    ctx.cnt["eV"] = eV0 + 9
    eVp = ctx.cnt["eV"]

    s1d = ctx.dram[SIN].rearrange("b r c -> (b r) c")
    s2d = ctx.dram[SOUT].rearrange("b r c -> (b r) c")
    g0 = ctx.cnt["qG"]
    n_sch = n_bi // 2 + n_bo // 2

    def f_gp1(e):
        w = WT(ctx, e)
        w("eV", eVp)
        j = 0
        for (src, n2, icol) in ((s1d, n_bi // 2, NKIN), (s2d, n_bo // 2, NKIN + 1)):
            for c in range(n2):
                if j >= 1:
                    w("eV", eVp + 3 * j)
                e.indirect_dma_start(
                    sgf[:, :], None, src,
                    IndirectOffsetOnAxis(ap=rxi[:, icol:icol + 1], axis=0),
                    element_offset=c * 128 * 64,
                ).then_inc(ctx.sem["qG"], 16)
                j += 1
    ctx.body("gpsimd", f_gp1)
    ctx.cnt["qG"] = g0 + 16 * n_sch

    def f_v1(e):
        w = WT(ctx, e)
        w("eG", eG0 + n_ms)
        j = 0
        for (plist, n2) in ((s1p, n_bi // 2), (s2p, n_bo // 2)):
            for c in range(n2):
                w("qG", g0 + 16 * (j + 1))
                e.tensor_copy(sgb[:], sgf[:]).then_inc(ctx.sem["eV"], 1)
                w("eV", eVp + 3 * j + 1)
                e.tensor_copy(plist[c][0:64, 0:64], sgb[0:64, :]).then_inc(ctx.sem["eV"], 1)
                w("eV", eVp + 3 * j + 2)
                e.tensor_copy(plist[c][64:128, 64:128], sgb[64:128, :]).then_inc(ctx.sem["eV"], 1)
                j += 1
    ctx.body("vector", f_v1)
    ctx.cnt["eV"] = eVp + 3 * n_sch
    eVs = ctx.cnt["eV"]

    g1 = ctx.cnt["qG"]
    mtd = ctx.dram[MIXT + str(k)]

    def f_gp2(e):
        w = WT(ctx, e)
        w("eV", eVp)
        w("qG", g1)
        for c in range(NKIN):
            if c >= 1:
                w("qG", g1 + 16 * c)
            e.indirect_dma_start(
                weff[:, c * MIX_M:(c + 1) * MIX_M], None, mtd,
                IndirectOffsetOnAxis(ap=rxi[:, c:c + 1], axis=0),
            ).then_inc(ctx.sem["qG"], 16)
    ctx.body("gpsimd", f_gp2)
    ctx.cnt["qG"] = g1 + 16 * NKIN
    weff_tgt = ctx.cnt["qG"]

    b2 = dict(ctx.cnt)
    eT20, eS20, eV20 = b2["eT"], b2["eS"], b2["eV"]
    st0 = b2["qG"]
    qA20 = b2["qA"]
    qS20 = b2["qS"]
    ld1 = [qS20 + 16 * NKIN * (ti + 1) for ti in range(NT)]
    ctx.cnt["qS"] = qS20 + 16 * NKIN * NT
    need_res = not (fc and k == 0)
    rld = [qA20 + 16 * NMO * (ti + 1) for ti in range(NT)] if need_res else None
    if need_res:
        ctx.cnt["qA"] = qA20 + 16 * NMO * NT
    res_name = {"fc1": "hmid", "pj0": "u", "pj1": "ffn1a"}.get(
        ("fc" if fc else "pj") + str(k))
    resd = ctx.dram[res_name] if need_res else None
    outd = ctx.dram[{"fc0": "hmid", "fc1": "h2b", "pj0": "ffn1a", "pj1": "ffn1"}[
        ("fc" if fc else "pj") + str(k)]]

    NTt = NKIN + NMO * (NKIN and 1) + NMO
    NTt = NKIN + 2 * NMO
    NSt = NKIN + NMO + (NMO if (fc and k == 1) else 0)
    NVt = NMO * (2 if (fc and k == 1) else (1 if need_res else 1))
    NVt = NMO * (2 if (fc and k == 1) else 1)
    ctx.cnt["eT"] = eT20 + NTt * NT
    ctx.cnt["eS"] = eS20 + NSt * NT
    ctx.cnt["eV"] = eV20 + NVt * NT
    ctx.cnt["qG"] = st0 + 16 * NMO * NT

    def f_sync(e):
        e.wait_ge(ctx.sem["qS"], qS20)
        for ti in range(NT):
            if ti >= 1:
                e.wait_ge(ctx.sem["qS"], ld1[ti - 1])
                e.wait_ge(ctx.sem["eT"], eT20 + NTt * (ti - 1) + NKIN)
            for c in range(NKIN):
                e.dma_start(out=int_[c][:], in_=ind[c * 128:(c + 1) * 128, ti * TC:(ti + 1) * TC]).then_inc(ctx.sem["qS"], 16)
    ctx.body("sync", f_sync)

    def f_tensor(e):
        w = WT(ctx, e)
        w("eV", eVs)
        w("qG", weff_tgt)
        for ti in range(NT):
            bt = eT20 + NTt * ti
            bs = eS20 + NSt * ti
            w("qS", ld1[ti])
            nvper = NVt // NMO
            nsper = 2 if (fc and k == 1) else 1
            bv = eV20 + NVt * ti
            for c in range(NKIN):
                if ti * NKIN + c >= 1:
                    w("eS", bs + c if c >= 1 else eS20 + NSt * (ti - 1) + NKIN)
                e.matmul(p_1[:], s1p[c][:], int_[c][:], start=True, stop=True,
                         skip_group_check=True).then_inc(ctx.sem["eT"], 1)
            for mo in range(NMO):
                it = ti * NMO + mo
                w("eS", bs + NKIN)
                if mo >= 2:
                    w("eS", bs + NKIN + nsper * (mo - 2) + 1)
                ins = None
                for kc in range(NKIN):
                    ins = e.matmul(p_m[mo % 2][:], weff[:, kc * MIX_M + mo * 128: kc * MIX_M + mo * 128 + 128],
                                   h1[kc][:], start=(kc == 0), stop=(kc == NKIN - 1),
                                   skip_group_check=True)
                ins.then_inc(ctx.sem["eT"], 1)
                w("eS", bs + NKIN + nsper * mo + 1)
                if mo >= 2:
                    w("eV", bv + nvper * (mo - 1))
                elif ti >= 1:
                    w("eV", eV20 + NVt * ti)
                e.matmul(p_2[mo % 2][:], s2p[mo][:], mt[:], start=True, stop=True,
                         skip_group_check=True).then_inc(ctx.sem["eT"], 1)
    ctx.body("tensor", f_tensor)


    def f_scalar(e):
        w = WT(ctx, e)
        if need_res:
            w("qA", qA20)
        nvper = NVt // NMO
        for ti in range(NT):
            bt = eT20 + NTt * ti
            bs = eS20 + NSt * ti
            bv = eV20 + NVt * ti
            for c in range(NKIN):
                w("eT", bt + c + 1)
                e.activation(out=h1[c][:], in_=p_1[:], func=AF.Copy).then_inc(ctx.sem["eS"], 1)
            for mo in range(NMO):
                it = ti * NMO + mo
                if need_res:
                    if it >= 1:
                        w("qA", qA20 + 16 * it)
                    if it >= 2:
                        w("eV", eV20 + nvper * (it - 2) + 1)
                    e.dma_start(out=rt[it % 2][:],
                                in_=resd[mo * 128:(mo + 1) * 128, ti * TC:(ti + 1) * TC]).then_inc(ctx.sem["qA"], 16)
                w("eT", bt + NKIN + 2 * mo + 1)
                e.activation(out=mt[:], in_=p_m[mo % 2][:], func=AF.Copy).then_inc(ctx.sem["eS"], 1)
                if fc and k == 1:
                    w("eV", bv + 2 * mo + 2)
                    e.activation(out=obb[it % 3][:], in_=ob[it % 3][:],
                                 func=AF.Square).then_inc(ctx.sem["eS"], 1)
    ctx.body("scalar", f_scalar)

    def f_vec(e):
        w = WT(ctx, e)
        for ti in range(NT):
            bt = eT20 + NTt * ti
            bv = eV20 + NVt * ti
            for mo in range(NMO):
                it = ti * NMO + mo
                w("eT", bt + NKIN + 2 * mo + 2)
                if need_res:
                    w("qA", qA20 + 16 * (it + 1))
                if it >= 3:
                    w("qG", st0 + 16 * (it - 2))
                if fc and k == 0:
                    e.tensor_scalar_mul(ob[it % 3][:], p_2[mo % 2][:],
                                        ccol[:, slot:slot + 1]).then_inc(ctx.sem["eV"], 1)
                elif fc and k == 1:
                    e.scalar_tensor_tensor(ob[it % 3][:], p_2[mo % 2][:],
                                           ccol[:, slot:slot + 1], rt[it % 2][:],
                                           ALU.mult, ALU.add).then_inc(ctx.sem["eV"], 1)
                    w("eV", bv + 2 * mo + 1)
                    e.tensor_scalar_max(ob[it % 3][:], ob[it % 3][:], 0.0).then_inc(ctx.sem["eV"], 1)
                else:
                    e.scalar_tensor_tensor(ob[it % 3][:], p_2[mo % 2][:],
                                           ccol[:, slot:slot + 1], rt[it % 2][:],
                                           ALU.mult, ALU.add).then_inc(ctx.sem["eV"], 1)
    ctx.body("vector", f_vec)

    def f_gp(e):
        w = WT(ctx, e)
        for ti in range(NT):
            bv = eV20 + NVt * ti
            bs = eS20 + NSt * ti
            for mo in range(NMO):
                it = ti * NMO + mo
                if fc and k == 1:
                    w("eS", bs + NKIN + 2 * mo + 2)
                    srcb = obb[it % 3]
                else:
                    w("eV", bv + (mo + 1) * (NVt // NMO))
                    srcb = ob[it % 3]
                if it >= 1:
                    w("qG", st0 + 16 * it)
                e.dma_start(out=outd[mo * 128:(mo + 1) * 128, ti * TC:(ti + 1) * TC],
                            in_=srcb[:]).then_inc(ctx.sem["qG"], 16)
    ctx.body("gpsimd", f_gp)


def stage_final(ctx):
    ident = ctx.dram["identity_f32"]
    src = ctx.dram["fsum"]
    dst = ctx.dram["out"]
    idt = ctx.sb((128, 128), F32, "id")
    tin = [ctx.sb((128, TC), F32, "ti") for _ in range(2)]
    tout = [ctx.sb((128, 128), F32, "to") for _ in range(4)]
    pss = [ctx.psum((128, 128), F32, "pt") for _ in range(2)]
    b = dict(ctx.cnt)
    qS0, qG0, eT0, eS0 = b["qS"], b["qG"], b["eT"], b["eS"]
    NTT = T // TC
    n_tiles = NKD * NTT * 4
    ld = []
    cur = qS0 + 16
    for j in range(NKD * NTT):
        cur += 16
        ld.append(cur)
    ctx.cnt["qS"] = cur
    ctx.cnt["eT"] = eT0 + n_tiles
    ctx.cnt["eS"] = eS0 + n_tiles
    ctx.cnt["qG"] = qG0 + 16 * n_tiles

    def f_sync(e):
        e.wait_ge(ctx.sem["qS"], qS0)
        e.dma_start(out=idt[:], in_=ident[:, :]).then_inc(ctx.sem["qS"], 16)
        j = 0
        for fch in range(NKD):
            for tc in range(NTT):
                if j >= 1:
                    e.wait_ge(ctx.sem["qS"], ld[j - 1])
                if j >= 2:
                    e.wait_ge(ctx.sem["eT"], eT0 + 4 * (j - 1))
                e.dma_start(out=tin[j % 2][:],
                            in_=src[fch * 128:(fch + 1) * 128, tc * TC:(tc + 1) * TC]).then_inc(ctx.sem["qS"], 16)
                j += 1
    ctx.body("sync", f_sync)

    def f_tensor(e):
        w = WT(ctx, e)
        j = 0
        for fch in range(NKD):
            for tc in range(NTT):
                w("qS", ld[j])
                for q in range(4):
                    kk = 4 * j + q
                    if kk >= 2:
                        w("eS", eS0 + kk - 1)
                    e.transpose(pss[kk % 2][:], tin[j % 2][:, q * 128:(q + 1) * 128],
                                idt[:]).then_inc(ctx.sem["eT"], 1)
                j += 1
    ctx.body("tensor", f_tensor)

    def f_scalar(e):
        w = WT(ctx, e)
        for kk in range(n_tiles):
            w("eT", eT0 + kk + 1)
            if kk >= 4:
                w("qG", qG0 + 16 * (kk - 3))
            e.copy(tout[kk % 4][:], pss[kk % 2][:]).then_inc(ctx.sem["eS"], 1)
    ctx.body("scalar", f_scalar)

    def f_gp(e):
        kk = 0
        for fch in range(NKD):
            for tc in range(NTT):
                for q in range(4):
                    e.wait_ge(ctx.sem["eS"], eS0 + kk + 1)
                    if kk >= 1:
                        e.wait_ge(ctx.sem["qG"], qG0 + 16 * kk)
                    t0 = tc * TC + q * 128
                    e.dma_start(out=dst[t0:t0 + 128, fch * 128:(fch + 1) * 128],
                                in_=tout[kk % 4][:]).then_inc(ctx.sem["qG"], 16)
                    kk += 1
    ctx.body("gpsimd", f_gp)


STAGES = {}


def _stage(name, fn, reads, writes, uses):
    STAGES[name] = dict(fn=fn, reads=list(reads), writes=list(writes), uses=list(uses))


_stage("ingest", stage_ingest, ["x_in"], ["x_fm"], ["identity_f32"])
_stage("ln0", lambda c: stage_ln(c, "x_fm", ["xn0", "xs0"], ["xn0b", "xs0b"]),
       ["x_fm"], ["xn0", "xs0", "xn0b", "xs0b"], ["zeros128"])
_stage("qkv0", stage_qkv0, ["xn0b"], ["q0b", "k0b"], ["L0qkw", "L0qkb", "zeros128b"])
_stage("vproj0", lambda c: stage_vproj(c, c.dram["L0wv"], "L0bv", "xn0b", "v0t"),
       ["xn0b"], ["v0t"], ["L0wv", "L0bv"])
_stage("cprep0", lambda c: stage_attnprep(c, "q0b", "k0b", "c0b", ind_q="indH",
                                          ind_k="indH", nh_k=H),
       ["q0b", "k0b"], ["c0b"], ["indH"])
_stage("attn0", lambda c: stage_attention(c, "q0b", "k0b", "v0t", "c0b", "ap0",
                                          mode="causal"),
       ["q0b", "k0b", "v0t", "c0b"], ["ap0"], ["cmask"])
_stage("wo0", lambda c: stage_wo(c, "L0wo", "L0bo", "ap0", "attn0"),
       ["ap0"], ["attn0"], ["L0wo", "L0bo", "zeros128b"])

_reg("svt", (T, D), BFNP)

for _i in range(6):
    _stage(f"s{_i}qkv", (lambda c, i=_i: stage_subqkv(c, i)),
           [f"xs{2 * _i}b"], ["sqr", "skr", "svf"], ["S_wqkv", "zeros128b"])
    _stage(f"s{_i}vtr", (lambda c, i=_i: stage_vtr(c, "svf", "svt")),
           ["svf"], ["svt"], [])
    _stage(f"s{_i}rope", (lambda c, i=_i: stage_rope(c, i, "xpos")),
           ["sqr", "skr"], ["sq", "sk"],
           ["S_qs", "S_ks", "xpos_cq", "xpos_sq", "xpos_ck", "xpos_sk",
            "indPT", "indP", "swap64"])
    _stage(f"s{_i}prep", (lambda c, i=_i: stage_attnprep(
               c, "sq", "sk", "scb", ind_q="indP", ind_k="indP", nh_k=H,
               a_const=("S_qs", _i, 8.0))),
           ["sq", "sk"], ["scb"], ["indP", "S_qs", "S_ks"])
    _stage(f"s{_i}att", (lambda c, i=_i: stage_attention(c, "sq", "sk", "svt", "scb",
                                                         "sap", mode="local",
                                                         perm_qk=True)),
           ["sq", "sk", "svt", "scb"], ["sap"], ["lmask"])
    _stage(f"s{_i}wo", (lambda c, i=_i: stage_subwo(c, i)),
           ["sap", f"xs{2 * _i}"], [f"xs{2 * _i + 1}", f"xs{2 * _i + 1}b"],
           ["S_wo", "zeros128b"])
    _stage(f"s{_i}f1", (lambda c, i=_i: stage_subf1(c, i)),
           [f"xs{2 * _i + 1}"], ["gact", "augrd"],
           ["S_w1", "S_g", "S_b", "S_b1", "zeros128"])
    _stage(f"s{_i}f2", (lambda c, i=_i: stage_subf2(c, i)),
           ["gact", f"xs{2 * _i + 1}"], [f"xs{2 * _i + 2}", f"xs{2 * _i + 2}b"],
           ["S_w2", "S_b2", "zeros128b"])

_stage("resid0", lambda c: stage_add3(c, "x_fm", "attn0", "xs12", "x1"),
       ["x_fm", "attn0", "xs12"], ["x1"], [])
_stage("ln1", lambda c: stage_ln(c, "x1", ["xn1"], ["xn1b"]),
       ["x1"], ["xn1", "xn1b"], ["zeros128"])
_stage("rms1", lambda c: stage_rms(c, "xn1", None, out_row="r1row"),
       ["xn1"], ["r1row"], [])
_stage("qkv1", lambda c: stage_qk(c, c.dram["L1qkv"][:, 0:2048], 2048, "q1b",
                                  "k1b", rhs_name="xn1b"),
       ["xn1b"], ["q1b", "k1b"], ["L1qkv", "zeros128b"])
_stage("vproj1", lambda c: stage_vproj(c, c.dram["L1qkv"][:, 2048:3072], None,
                                       "xn1b", "v1t"),
       ["xn1b"], ["v1t"], ["L1qkv"])
_stage("cprep1", lambda c: stage_attnprep(c, "q1b", "k1b", "c1m", ind_q="indH",
                                          ind_k="indH", nh_k=H,
                                          variants=[("c1n", 15), ("c1p", 31)]),
       ["q1b", "k1b"], ["c1m", "c1n", "c1p"], ["indH", "relT"])
_stage("tbias", stage_tbias, [], ["grev", "biasT"], ["relT", "rel32", "OHrev"])
_stage("attn1", lambda c: stage_attention(c, "q1b", "k1b", "v1t", "c1m", "ap1",
                                          mode="full",
                                          bias=("biasT", "c1n", "c1p")),
       ["q1b", "k1b", "v1t", "c1m", "c1n", "c1p", "biasT"], ["ap1"], [])
_stage("wo1", lambda c: stage_wo(c, "L1wo", None, "ap1", "attn1"),
       ["ap1"], ["attn1"], ["L1wo", "zeros128b"])


def stage_wo2(ctx):
    wsb, wrdy = load_weights_resident(ctx, ctx.dram["woT"], D, D, "wo2")

    def epi(info):
        epi_resid_store(ctx, info, "xn1", "u", None)
    gemm_fm(ctx, wsb=wsb, w_ready=wrdy, NKC=NKD, MC=8,
            rhs_dram=ctx.dram["ap2"], epi_emit=epi)


_stage("wtr2", stage_wtr2, [], ["wqB", "wqT", "wkB", "wkT", "wvB", "wvT", "woB", "woT"],
       ["F_wq", "F_wk", "F_wv", "F_wo"])
_stage("qproj2", lambda c: stage_proj_fm(c, "wqT", 1024, "q2r", "xn1b"),
       ["wqT", "xn1b"], ["q2r"], ["zeros128b"])
_stage("kproj2", lambda c: stage_proj_fm(c, "wkT", 256, "k2r", "xn1b"),
       ["wkT", "xn1b"], ["k2r"], ["zeros128b"])
_stage("vproj2", lambda c: stage_vproj(c, c.dram["wvT"], None, "xn1b", "v2t",
                                       scale_row="r1row"),
       ["wvT", "xn1b", "r1row"], ["v2t"], [])
_stage("rope2", stage_rope2, ["q2r", "k2r"], ["q2b", "k2b"],
       ["F_qg", "F_kg", "rope1_cq", "rope1_sq", "rope1_ck", "rope1_sk",
        "indPT", "indP", "indKT", "indK", "swap64"])
_stage("cprep2", lambda c: stage_attnprep(c, "q2b", "k2b", "c2b", ind_q="indP",
                                          ind_k="indK", nh_k=4, a_const=("gqa",)),
       ["q2b", "k2b"], ["c2b"], ["indP", "indK", "F_qg", "F_kg"])
_stage("attn2", lambda c: stage_attention(c, "q2b", "k2b", "v2t", "c2b", "ap2",
                                          mode="causal", perm_qk=True, nkv=4,
                                          heads_per_kv=4),
       ["q2b", "k2b", "v2t", "c2b"], ["ap2"], ["cmask"])
_stage("wo2", stage_wo2, ["ap2", "xn1", "woT"], ["u"], ["zeros128b"])
_stage("rmsu", lambda c: stage_rms(c, "u", "hb"), ["u"], ["hb"], [])
_stage("gate", stage_gate, [], ["gatev", "gidx"], ["F_flog", "F_plog"])
_stage("mixprep", stage_mixprep, ["gidx"],
       ["fmixB", "fmixT0", "fmixT1", "pmixB", "pmixT0", "pmixT1"],
       ["F_fmix", "F_pmix"])
_stage("fcm0", lambda c: stage_member(c, "fc", 0),
       ["hb", "gatev", "gidx", "fmixT0"], ["hmid"],
       ["F_fs1", "F_fs2", "iotaA_fc", "iotaB_fc"])
_stage("fcm1", lambda c: stage_member(c, "fc", 1),
       ["hb", "gatev", "gidx", "fmixT1", "hmid"], ["h2b"],
       ["F_fs1", "F_fs2", "iotaA_fc", "iotaB_fc"])
_stage("pjm0", lambda c: stage_member(c, "pj", 0),
       ["h2b", "gatev", "gidx", "pmixT0", "u"], ["ffn1a"],
       ["F_ps1", "F_ps2", "iotaA_pj", "iotaB_pj"])
_stage("pjm1", lambda c: stage_member(c, "pj", 1),
       ["h2b", "gatev", "gidx", "pmixT1", "ffn1a"], ["ffn1"],
       ["F_ps1", "F_ps2", "iotaA_pj", "iotaB_pj"])
_stage("fadd", lambda c: stage_add3(c, "x1", "attn1", "ffn1", "fsum"),
       ["x1", "attn1", "ffn1"], ["fsum"], [])
_stage("final", stage_final, ["fsum"], ["out"], ["identity_f32"])


def _subgroup(i):
    return [f"s{i}qkv", f"s{i}vtr", f"s{i}rope", f"s{i}prep", f"s{i}att",
            f"s{i}wo", f"s{i}f1", f"s{i}f2"]


GROUPS = [
    ["ingest", "ln0"],
    ["qkv0", "vproj0", "cprep0"],
    ["attn0", "wo0"],
] + [_subgroup(i) for i in range(6)] + [
    ["resid0", "ln1", "rms1"],
    ["qkv1", "vproj1", "cprep1", "tbias"],
    ["attn1", "wo1"],
    ["wtr2", "qproj2", "kproj2", "vproj2"],
    ["rope2", "cprep2"],
    ["attn2", "wo2"],
    ["rmsu", "gate", "mixprep"],
    ["fcm0", "fcm1"],
    ["pjm0", "pjm1"],
    ["fadd", "final"],
]


def _mybir_dt(np_dtype):
    if np_dtype == np.float32:
        return F32
    if np_dtype == BFNP:
        return BF16
    if np_dtype == np.int32:
        return I32
    raise ValueError(np_dtype)


def build_group(stage_names, feed, mode, export=None):
    reads, writes, uses = set(), set(), set()
    for sn in stage_names:
        st = STAGES[sn]
        for r in st["reads"]:
            if r not in writes and r != "x_in":
                reads.add(r)
        writes.update(st["writes"])
        uses.update(st["uses"])
        if "x_in" in st["reads"]:
            uses.add("x_in")
    nc = bass.Bass("TRN2", target_bir_lowering=False)
    aps = {}
    for name in sorted(uses):
        if name == "x_in":
            aps[name] = nc.dram_tensor("x_in", [T, D], F32, kind="ExternalInput").ap()
        else:
            arr = feed[name]
            aps[name] = nc.dram_tensor(name, list(arr.shape), _mybir_dt(arr.dtype),
                                       kind="ExternalInput").ap()
    for name in sorted(reads):
        shape, dt = SCRATCH[name]
        aps[name] = nc.dram_tensor(name, list(shape), _mybir_dt(dt), kind="ExternalInput").ap()
    if export is None:
        export = set(writes) if mode == "multi" else {"out"}
    for name in sorted(writes):
        shape, dt = SCRATCH[name]
        kind = "ExternalOutput" if name in export else "Internal"
        aps[name] = nc.dram_tensor(name, list(shape), _mybir_dt(dt), kind=kind).ap()

    with ExitStack() as stack:
        with nc.Block() as block:
            ctx = Ctx(nc, block, stack)
            ctx.dram = aps
            for sn in stage_names:
                ctx.sstack = ExitStack()
                with ctx.sstack:
                    STAGES[sn]["fn"](ctx)
                    emit_barrier(ctx)
    in_names = sorted(uses) + sorted(reads)
    out_names = sorted(writes & export)
    return nc, in_names, out_names


def run_groups(groups, inputs, n_cores=8, scratch=None, keep=("out",),
               mode="chain"):
    import gc
    consts = make_consts()
    w = make_weights(inputs)
    feed = {**consts, **w}
    x = np.ascontiguousarray(np.asarray(inputs["L0_attn_x"], np.float32))
    free_dead = scratch is None
    if scratch is None:
        scratch = [dict() for _ in range(n_cores)]
    keep = set(keep)
    last_use = {}
    for gi, grp in enumerate(groups):
        for sn in grp:
            for r in STAGES[sn]["reads"]:
                last_use[r] = gi
    exports = [None] * len(groups)
    if mode == "chain":
        need_after = {"out"}
        for gi in range(len(groups) - 1, -1, -1):
            wr = set()
            rd = set()
            for sn in groups[gi]:
                wr.update(STAGES[sn]["writes"])
                rd.update(STAGES[sn]["reads"])
            exports[gi] = wr & need_after
            need_after |= rd
    memlog = os.environ.get("KMEMLOG")
    for gi, grp in enumerate(groups):
        nc, in_names, out_names = build_group(grp, feed, mode,
                                              export=exports[gi])
        in_maps = []
        for c in range(n_cores):
            m = {}
            for nm in in_names:
                if nm == "x_in":
                    m[nm] = x[c]
                elif nm in feed:
                    m[nm] = feed[nm]
                else:
                    m[nm] = scratch[c][nm]
            in_maps.append(m)
        res = run_bass_kernel_spmd(nc, in_maps, core_ids=list(range(n_cores)))
        for c in range(n_cores):
            for nm in out_names:
                shape, dt = SCRATCH[nm]
                a = np.asarray(res.results[c][nm])
                if dt == BFNP and a.dtype != BFNP:
                    a = a.view(BFNP)
                scratch[c][nm] = np.array(a.reshape(shape))
        del res, in_maps, nc
        if free_dead:
            for c in range(n_cores):
                for nm in list(scratch[c]):
                    if nm not in keep and last_use.get(nm, -1) <= gi:
                        del scratch[c][nm]
        gc.collect()
        if memlog:
            import resource
            print(f"[mem] group {gi} {grp[0]}.. rss="
                  f"{resource.getrusage(resource.RUSAGE_SELF).ru_maxrss / 1e6:.1f}GB",
                  flush=True)
    return scratch


MGROUPS = [
    [s for g in GROUPS[0:3] for s in g],
    [s for g in GROUPS[3:9] for s in g],
    [s for g in GROUPS[9:12] for s in g],
    [s for g in GROUPS[12:19] for s in g],
]

ALLSTAGES = [s for g in GROUPS for s in g]


def kernel(**inputs):
    n_cores = 8
    groups = [ALLSTAGES] if MODE == "fused" else MGROUPS
    scratch = run_groups(groups, inputs, n_cores=n_cores)
    out = np.stack([scratch[c]["out"].astype(np.float32) for c in range(n_cores)])
    return out
